# version 01
# baseline (speedup 1.0000x reference)
import contextlib
import numpy as np
import ml_dtypes
import concourse.bass as bass
import concourse.mybir as mybir
from concourse.bass import IndirectOffsetOnAxis
from concourse.bass_utils import run_bass_kernel_spmd
from concourse.library_overlay import lower_extended_insts

F32 = mybir.dt.float32
BF16 = mybir.dt.bfloat16
U32 = mybir.dt.uint32
ALU = mybir.AluOpType
ACT = mybir.ActivationFunctionType
AX = mybir.AxisListType

T = 2048
D = 1024
H = 16
C = 64
FD = 256
NI = 2816
HM = 4096
NE = 8
NT = T // 128
NHALF = NI // 2


class Sem:

    def __init__(self, ctx, nc, name):
        self.h = ctx.enter_context(nc.semaphore(name))
        self.v = 0

    def bump(self, n):
        self.v += n
        return self.v


class Plan:

    ENGS = ("sync", "tensor", "vector", "scalar", "gpsimd")

    def __init__(self, nc, junk=None):
        self.nc = nc
        self._junk = junk
        self.ops = {e: [] for e in self.ENGS}

    def add(self, eng, fn):
        self.ops[eng].append(fn)

    def wait(self, eng, sem, val):
        if val > 0:
            self.ops[eng].append(lambda e, s=sem.h, v=val: e.wait_ge(s, v))

    def sig(self, eng, sem):
        v = sem.bump(1)
        last = self.ops[eng].pop()

        def wrapped(e, f=last, s=sem.h):
            inst = f(e)
            inst.then_inc(s, 1)
            return inst

        self.ops[eng].append(wrapped)
        return v

    def dma(self, eng, sem, out, in_, transpose=False, slow=False):
        if transpose:
            self.ops[eng].append(
                lambda e, o=out, i=in_, s=sem.h: e.dma_start_transpose(o, i).then_inc(s, 16))
        else:
            self.ops[eng].append(
                lambda e, o=out, i=in_, s=sem.h, sl=slow: e.dma_start(
                    out=o, in_=i, allow_slow_non_contiguous=sl).then_inc(s, 16))
        return sem.bump(16)

    def vsep(self, n=2):
        for _ in range(n):
            self.ops["vector"].append(
                lambda e, j=self._junk: e.tensor_scalar(j[:, 0:1], j[:, 0:1], 1.0, None, ALU.mult))

    def emit(self):
        with self.nc.Block() as B:
            for eng in self.ENGS:
                fns = self.ops[eng]
                if not fns:
                    continue

                def body(e, fns=fns):
                    for f in fns:
                        f(e)

                getattr(B, eng)(body)


def _quant_emit(P, qs, Z, OUT, width, vV, sS, gG, use_g=True, zmark=None):
    nb = width // 16
    bm, s_, sinv, Tt, a, b, b2, sg, v1 = (qs[k] for k in
                                          ("bm", "s", "sinv", "T", "a", "b", "b2", "sg", "v1"))
    Z3 = Z.rearrange("p (a b) -> p a b", b=16)
    P.add("vector", lambda e: e.tensor_reduce(
        bm[:, 0:nb], Z3, AX.X, ALU.max, apply_absolute_value=True))
    P.add("vector", lambda e: e.tensor_scalar(
        s_[:, 0:nb], bm[:, 0:nb], 0.16666667, 1e-12, ALU.mult, ALU.max))
    P.add("vector", lambda e: e.reciprocal(sinv[:, 0:nb], s_[:, 0:nb]))
    if zmark is not None:
        P.wait("scalar", vV, zmark)
    P.add("scalar", lambda e: e.sign(sg[:, 0:width], Z))
    sv = P.sig("scalar", sS)
    sb = sinv[:, 0:nb].unsqueeze(2).broadcast_to([128, nb, 16])
    T3 = Tt[:, 0:width].rearrange("p (a b) -> p a b", b=16)
    V3s = v1[:, 0:width].rearrange("p (a b) -> p a b", b=16)
    P.wait("vector", sS, sv)
    P.add("vector", lambda e: e.tensor_tensor(v1[:, 0:width], Z, sg[:, 0:width], ALU.mult))
    P.add("vector", lambda e: e.tensor_tensor(T3, V3s, sb, ALU.mult))
    tv = P.sig("vector", vV)
    P.add("vector", lambda e: e.tensor_scalar(a[:, 0:width], Tt[:, 0:width], 0.25, None, ALU.is_gt))
    for thr in (0.75, 1.25, 1.75):
        P.add("vector", lambda e, t=thr: e.scalar_tensor_tensor(
            a[:, 0:width], Tt[:, 0:width], t, a[:, 0:width], ALU.is_gt, ALU.add))
    beng = "gpsimd" if use_g else "vector"
    if use_g:
        P.wait("gpsimd", vV, tv)
    P.add(beng, lambda e: e.tensor_scalar(b[:, 0:width], Tt[:, 0:width], 2.5, None, ALU.is_gt))
    P.add(beng, lambda e: e.tensor_scalar(b2[:, 0:width], Tt[:, 0:width], 3.5, None, ALU.is_gt))
    P.add(beng, lambda e: e.tensor_tensor(b[:, 0:width], b[:, 0:width], b2[:, 0:width], ALU.add))
    P.add(beng, lambda e: e.tensor_scalar(b2[:, 0:width], Tt[:, 0:width], 5.0, 2.0, ALU.is_gt, ALU.mult))
    P.add(beng, lambda e: e.tensor_tensor(b[:, 0:width], b[:, 0:width], b2[:, 0:width], ALU.add))
    if use_g:
        gv = P.sig("gpsimd", gG)
        P.wait("vector", gG, gv)
    P.add("vector", lambda e: e.scalar_tensor_tensor(
        a[:, 0:width], a[:, 0:width], 0.5, b[:, 0:width], ALU.mult, ALU.add))
    P.wait("vector", sS, sv)
    P.add("vector", lambda e: e.tensor_tensor(v1[:, 0:width], a[:, 0:width], sg[:, 0:width], ALU.mult))
    ssb = s_[:, 0:nb].unsqueeze(2).broadcast_to([128, nb, 16])
    O3 = OUT.rearrange("p (a b) -> p a b", b=16)
    V3 = v1[:, 0:width].rearrange("p (a b) -> p a b", b=16)
    P.add("vector", lambda e: e.tensor_tensor(O3, V3, ssb, ALU.mult))


def build(dbg=(), upto=99):
    nc = bass.Bass("TRN2", target_bir_lowering=False)

    def dk(name):
        return "ExternalOutput" if name in dbg else "Internal"

    xin = nc.dram_tensor("x", [T, D], F32, kind="ExternalInput").ap()
    W = {}
    for nm, shp in [
        ("L0_attn_Wq", [D, D]), ("L0_attn_Wk", [D, D]), ("L0_attn_Wv", [D, D]),
        ("L0_attn_Wo", [D, D]), ("L0_attn_Wf", [FD, C]),
        ("L0_ffn_w_gate", [NI, D]), ("L0_ffn_w_up", [NI, D]), ("L0_ffn_w_down", [D, NI]),
        ("L1_attn_Wq", [D, D]), ("L1_attn_bq", [1, D]), ("L1_attn_Wk", [D, D]),
        ("L1_attn_bk", [1, D]), ("L1_attn_Wv", [D, D]), ("L1_attn_bv", [1, D]),
        ("L1_attn_Wo", [D, D]), ("L1_attn_bo", [1, D]),
        ("L1_ffn_alpha", [1, 1]), ("L1_ffn_w_gate", [D, NE]), ("L1_ffn_b_gate", [NE, 1]),
        ("L1_ffn_W1m", [D, HM]), ("L1_ffn_b1m", [1, HM]), ("L1_ffn_W2m", [HM, D]),
        ("L1_ffn_b2m", [1, D]), ("L1_ffn_W1e", [NE * D, HM]), ("L1_ffn_b1e", [NE, HM]),
        ("L1_ffn_W2e", [NE * HM, D]), ("L1_ffn_b2e", [NE, D]),
    ]:
        W[nm] = nc.dram_tensor(nm, shp, F32, kind="ExternalInput").ap()
    y = nc.dram_tensor("y", [T, D], F32, kind="ExternalOutput").ap()
    ident = nc.dram_tensor("ident", [128, 128], BF16, kind="ExternalInput").ap()

    def scr(name, shp, dt=BF16):
        return nc.dram_tensor(name, shp, dt, kind=dk(name)).ap()

    xn0_h = scr("xn0", [T, D])
    xq_h = scr("xq", [T, D])
    wq0_h = scr("wq0", [D, D])
    wk0_h = scr("wk0", [D, D])
    wv0_h = scr("wv0", [D, D])
    wo0_h = scr("wo0", [D, D])
    wf_h = scr("wfb", [FD, C])
    attn0_h = scr("attn0", [T, D])
    wqg_h = scr("wqg", [NI, D])
    wqu_h = scr("wqu", [NI, D])
    wqd_h = scr("wqd", [D, NI])
    hq_h = scr("hq", [T, NI])
    x1_h = scr("x1", [T, D], F32)
    xn1_h = scr("xn1", [T, D])
    wq1_h = scr("wq1", [D, D])
    wk1_h = scr("wk1", [D, D])
    wv1_h = scr("wv1", [D, D])
    wo1_h = scr("wo1", [D, D])
    w1m_h = scr("w1m", [D, HM])
    w2m_h = scr("w2m", [HM, D])
    wgate_h = scr("wgate", [D, NE])
    attn1_h = scr("attn1", [T, D])
    mean_h = scr("meanh", [1, D], F32)
    scal_h = scr("scalh", [1, 16], F32)
    w1g_h = [scr(f"w1g{k}", [D, HM]) for k in range(2)]
    w2g_h = [scr(f"w2g{k}", [HM, D]) for k in range(2)]
    b1g_h = [scr(f"b1g{k}", [1, HM], F32) for k in range(2)]
    b2g_h = [scr(f"b2g{k}", [1, D], F32) for k in range(2)]
    ffn1_h = scr("ffn1", [D, T])
    lnst_h = scr("lnst", [128, 64], F32)
    _dall = set(dbg)
    if "dqfm" in dbg:
        dbg = set(dbg) | {"dkfm", "dvtok", "dkv", "dwfT"}
    if "dQ" in dbg:
        dbg = set(dbg) | {"dK", "dV"}
    if "dSA" in dbg:
        dbg = set(dbg) | {"dSB"}
    dbg_qfm = scr("dqfm", [128, 16384])
    dbg_kfm = scr("dkfm", [128, 16384])
    dbg_vtok = scr("dvtok", [128, 16384])
    dbg_kv = scr("dkv", [128, 2080])
    dbg_wfT = scr("dwfT", [128, 256])
    dbg_h = scr("dh", [T, NI], F32)
    lnst4_h = scr("lnst4", [128, 64], F32)
    dbg_Q = scr("dQ", [T, D], F32)
    dbg_K = scr("dK", [T, D], F32)
    dbg_V = scr("dV", [T, D], F32)
    dbg_SA = scr("dSA", [128, 64, 32], F32)
    dbg_SB = scr("dSB", [128, 32, 32], F32)
    dbg_hfm = scr("dhfm", [128, 16384])
    dbg_exp = scr("dexp", [128, 4096], F32)
    dbg_ps = scr("dps", [128, 4096], F32)
    dbg_g = scr("dg", [T, NI], F32)

    ctx = contextlib.ExitStack()
    with ctx:
        qS = Sem(ctx, nc, "qS")
        qG = Sem(ctx, nc, "qG")
        tM = Sem(ctx, nc, "tM")
        vV = Sem(ctx, nc, "vV")
        sS = Sem(ctx, nc, "sS")
        gG = Sem(ctx, nc, "gG")

        def sb(n, shp, dt=F32, **kw):
            return ctx.enter_context(nc.sbuf_tensor(n, shp, dt, **kw))

        m0 = sb("m0", [128, NT])
        r0 = sb("r0", [128, NT])
        m1 = sb("m1", [128, NT])
        r1 = sb("r1", [128, NT])
        sum_a = sb("sum_a", [128, NT])
        sum_b = sb("sum_b", [128, NT])
        junk = sb("junk", [128, 1024])
        onesbf = sb("onesbf", [128, 1], BF16)
        screp = sb("screp", [128, 16])
        psA = ctx.enter_context(nc.psum_tensor("psA", [128, 4096], F32))
        xbuf = None

        def bank(i, w=512):
            return psA[:, i * 512: i * 512 + w]

        def bslice(i, off, w):
            return psA[:, i * 512 + off: i * 512 + off + w]

        P = Plan(nc, junk)
        P.add("vector", lambda e: e.memset(onesbf[:], 1.0))
        for src, dst in [(W["L0_attn_Wq"], wq0_h), (W["L0_attn_Wk"], wk0_h),
                         (W["L0_attn_Wv"], wv0_h), (W["L0_attn_Wo"], wo0_h),
                         (W["L0_attn_Wf"], wf_h),
                         (W["L1_attn_Wq"], wq1_h), (W["L1_attn_Wk"], wk1_h),
                         (W["L1_attn_Wv"], wv1_h), (W["L1_attn_Wo"], wo1_h),
                         (W["L1_ffn_W1m"], w1m_h), (W["L1_ffn_W2m"], w2m_h),
                         (W["L1_ffn_w_gate"], wgate_h)]:
            P.add("gpsimd", lambda e, s=src, d=dst: e.dma_start(out=d, in_=s).then_inc(qG.h, 16))
            qG.bump(16)
        castdone = qG.v
        with nc.sbuf_tensor("xt0", [128, 2, 1024], F32) as xt0, \
             nc.sbuf_tensor("jsq", [128, 2, 1024], F32) as jsq:
            smarks = []
            vmarks0 = []
            for i in range(NT):
                pp = i % 2
                if i >= 2:
                    P.wait("sync", vV, vmarks0[i - 2])
                ldv = P.dma("sync", qS, xt0[:, pp], xin[i * 128:(i + 1) * 128, :])
                P.wait("scalar", qS, ldv)
                if i >= 2:
                    P.wait("scalar", vV, vmarks0[i - 2])
                P.add("scalar", lambda e, pp=pp: e.activation(jsq[:, pp], xt0[:, pp], ACT.Square))
                sm = P.sig("scalar", sS)
                smarks.append(sm)
                P.wait("vector", sS, sm)
                P.add("vector", lambda e, pp=pp, i=i: e.tensor_reduce(
                    sum_a[:, i:i + 1], xt0[:, pp], AX.X, ALU.add))
                P.add("vector", lambda e, pp=pp, i=i: e.tensor_reduce(
                    sum_b[:, i:i + 1], jsq[:, pp], AX.X, ALU.add))
                vmarks0.append(P.sig("vector", vV))
            P.add("vector", lambda e: e.tensor_scalar(m0[:], sum_a[:], 1.0 / D, None, ALU.mult))
            P.add("vector", lambda e: e.tensor_scalar(junk[:, 0:NT], sum_b[:], 1.0 / D, None, ALU.mult))
            P.add("vector", lambda e: e.tensor_tensor(r0[:], m0[:], m0[:], ALU.mult))
            P.add("vector", lambda e: e.tensor_tensor(junk[:, 0:NT], junk[:, 0:NT], r0[:], ALU.subtract))
            P.add("vector", lambda e: e.tensor_scalar(junk[:, 0:NT], junk[:, 0:NT], 1e-5, None, ALU.add))
            vv = P.sig("vector", vV)
            P.wait("scalar", vV, vv)
            P.add("scalar", lambda e: e.activation(junk[:, NT:2 * NT], junk[:, 0:NT], ACT.Sqrt))
            sv2 = P.sig("scalar", sS)
            P.wait("vector", sS, sv2)
            P.add("vector", lambda e: e.reciprocal(r0[:], junk[:, NT:2 * NT]))
            P.emit()

        _es = contextlib.ExitStack()
        with _es:
            xt1 = _es.enter_context(nc.sbuf_tensor("xt1", [128, 2, 1024], F32))
            xnf = _es.enter_context(nc.sbuf_tensor("xnf", [128, 1024], F32))
            xnbf = _es.enter_context(nc.sbuf_tensor("xnbf", [128, 2, 1024], BF16))
            xqbf = _es.enter_context(nc.sbuf_tensor("xqbf", [128, 2, 1024], BF16))
            q_bm = _es.enter_context(nc.sbuf_tensor("q_bm", [128, 64], F32))
            q_s = _es.enter_context(nc.sbuf_tensor("q_s", [128, 64], F32))
            q_sinv = _es.enter_context(nc.sbuf_tensor("q_sinv", [128, 64], F32))
            q_T = _es.enter_context(nc.sbuf_tensor("q_T", [128, 1024], F32))
            q_a = _es.enter_context(nc.sbuf_tensor("q_a", [128, 1024], F32))
            q_b = _es.enter_context(nc.sbuf_tensor("q_b", [128, 1024], F32))
            q_b2 = _es.enter_context(nc.sbuf_tensor("q_b2", [128, 1024], F32))
            q_sg = _es.enter_context(nc.sbuf_tensor("q_sg", [128, 1024], F32))
            q_v1 = _es.enter_context(nc.sbuf_tensor("q_v1", [128, 1024], F32))
            qs = dict(bm=q_bm, s=q_s, sinv=q_sinv, T=q_T, a=q_a, b=q_b, b2=q_b2,
                      sg=q_sg, v1=q_v1)
            P = Plan(nc, junk)
            vmarks = []
            csm1 = []
            for i in range(NT):
                pp = i % 2
                if i >= 2:
                    P.wait("sync", vV, vmarks[i - 2])
                ldv = P.dma("sync", qS, xt1[:, pp], xin[i * 128:(i + 1) * 128, :])
                P.wait("vector", qS, ldv)
                if i >= 1:
                    P.wait("vector", sS, csm1[i - 1])
                rb = r0[:, i:i + 1].broadcast_to([128, 1024])
                P.add("vector", lambda e, pp=pp, i=i, rb=rb: e.scalar_tensor_tensor(
                    xnf[:], xt1[:, pp], m0[:, i:i + 1], rb, ALU.subtract, ALU.mult))
                xv = P.sig("vector", vV)
                P.wait("scalar", vV, xv)
                P.add("scalar", lambda e, pp=pp: e.copy(xnbf[:, pp], xnf[:]))
                cs = P.sig("scalar", sS)
                csm1.append(cs)
                P.wait("sync", sS, cs)
                P.dma("sync", qS, xn0_h[i * 128:(i + 1) * 128, :], xnbf[:, pp])
                _quant_emit(P, qs, xnf[:], xqbf[:, pp], 1024, vV, sS, gG, use_g=True)
                qv = P.sig("vector", vV)
                vmarks.append(qv)
                P.wait("sync", vV, qv)
                P.dma("sync", qS, xq_h[i * 128:(i + 1) * 128, :], xqbf[:, pp])
            P.dma("sync", qS, lnst_h[:, 0:NT], m0[:])
            P.dma("sync", qS, lnst_h[:, 16:16 + NT], r0[:])
            P.dma("sync", qS, lnst_h[:, 32:48], sum_a[:, 0:16])
            P.emit()

        if upto <= 1:
            return nc

        _es = contextlib.ExitStack()
        with _es:
            xn0T = _es.enter_context(nc.sbuf_tensor("xn0T", [128, 16384], BF16, is_transpose_dest=True))
            qfm = _es.enter_context(nc.sbuf_tensor("qfm", [128, 16384], BF16))
            kfm = _es.enter_context(nc.sbuf_tensor("kfm", [128, 16384], BF16))
            vtok = _es.enter_context(nc.sbuf_tensor("vtok", [128, 16384], BF16))
            a0sb = _es.enter_context(nc.sbuf_tensor("a0sb", [128, 16384], BF16))
            wT = _es.enter_context(nc.sbuf_tensor("wT", [128, 8192], BF16, is_transpose_dest=True))
            wfT = _es.enter_context(nc.sbuf_tensor("wfT", [128, 256], BF16, is_transpose_dest=True))
            kvsb = _es.enter_context(nc.sbuf_tensor("kvsb", [128, 2080], BF16))
            kfb = _es.enter_context(nc.sbuf_tensor("kfb", [128, 2, 256], BF16))
            etmp = _es.enter_context(nc.sbuf_tensor("etmp", [128, 512], F32))
            etmp2 = _es.enter_context(nc.sbuf_tensor("etmp2", [128, 512], F32))
            qff = _es.enter_context(nc.sbuf_tensor("qff", [128, 2, 4096], BF16))
            nrm = _es.enter_context(nc.sbuf_tensor("nrm", [128, 2], F32))
            P = Plan(nc, junk)
            P.wait("gpsimd", qG, castdone)
            P.wait("sync", qG, castdone)
            for j in range(8):
                P.dma("sync", qS, xn0T[:, j * 2048:(j + 1) * 2048],
                      xn0_h[:, j * 128:(j + 1) * 128], transpose=True)
            idl = P.dma("sync", qS, junk[:, 0:64].bitcast(BF16)[:, 0:128], ident)
            wfl = P.dma("sync", qS, kfb[:, 0, 0:128].rearrange("p (c f) -> p c f", f=64),
                        wf_h.rearrange("(c p) f -> p c f", p=128))
            P.wait("tensor", qS, max(idl, wfl))
            for cc in range(2):
                for bb in range(2):
                    P.add("tensor", lambda e, cc=cc, bb=bb: e.transpose(
                        psA[bb * 64:bb * 64 + 64,
                            3584 + cc * 64:3584 + cc * 64 + 64].bitcast(BF16),
                        kfb[:, 0, cc * 64:(cc + 1) * 64],
                        junk[:, 0:64].bitcast(BF16)[:, 0:128]))
            trm = P.sig("tensor", tM)
            P.wait("scalar", tM, trm)
            for cc in range(2):
                P.add("scalar", lambda e, cc=cc: e.copy(
                    wfT[:, cc * 128:(cc + 1) * 128],
                    psA[:, 3584 + cc * 64:3584 + cc * 64 + 64].bitcast(BF16)))
            wfev = P.sig("scalar", sS)
            P.wait("tensor", sS, wfev)

            def proj_fm(wsrc_h, dest):
                tuse0 = tM.v
                P.wait("sync", tM, tuse0)
                for j in range(8):
                    P.dma("sync", qS, wT[:, j * 1024:(j + 1) * 1024],
                          wsrc_h[:, j * 128:(j + 1) * 128], transpose=True)
                wv_ = qS.v
                P.wait("tensor", qS, wv_)
                for ci in range(8):
                    for j in range(8):
                        st = wT[:, j * 1024 + ci * 128: j * 1024 + ci * 128 + 128]
                        for tc in range(4):
                            rhs = xn0T[:, j * 2048 + tc * 512: j * 2048 + tc * 512 + 512]
                            P.add("tensor", lambda e, st=st, rhs=rhs, tc=tc, j=j: e.matmul(
                                bank(tc), st, rhs, start=(j == 0), stop=(j == 7)))
                    mv = P.sig("tensor", tM)
                    P.wait("scalar", tM, mv)
                    for tc in range(4):
                        dst = dest[:, ci * 2048 + tc * 512: ci * 2048 + tc * 512 + 512]
                        P.add("scalar", lambda e, dst=dst, tc=tc: e.copy(dst, bank(tc)))
                    ev = P.sig("scalar", sS)
                    P.wait("tensor", sS, ev)

            proj_fm(wq0_h, qfm)
            proj_fm(wk0_h, kfm)
            P.wait("sync", tM, tM.v)
            for j in range(8):
                P.dma("sync", qS, wT[:, j * 1024:(j + 1) * 1024],
                      wv0_h[:, j * 128:(j + 1) * 128], transpose=True)
            P.wait("tensor", qS, qS.v)
            ev_marks = []
            for tc in range(NT):
                b0 = (tc % 2) * 2
                if tc >= 2:
                    P.wait("tensor", sS, ev_marks[tc - 2])
                for j in range(8):
                    st = xn0T[:, j * 2048 + tc * 128: j * 2048 + tc * 128 + 128]
                    for hf in range(2):
                        rhs = wT[:, j * 1024 + hf * 512: j * 1024 + hf * 512 + 512]
                        P.add("tensor", lambda e, st=st, rhs=rhs, b=b0 + hf, j=j: e.matmul(
                            bank(b), st, rhs, start=(j == 0), stop=(j == 7)))
                mv = P.sig("tensor", tM)
                P.wait("scalar", tM, mv)
                for hf in range(2):
                    dst = vtok[:, tc * 1024 + hf * 512: tc * 1024 + hf * 512 + 512]
                    P.add("scalar", lambda e, dst=dst, b=b0 + hf: e.copy(dst, bank(b)))
                ev_marks.append(P.sig("scalar", sS))
            P.wait("tensor", sS, ev_marks[-1])

            def kvslot(h, half):
                idx = h * 2 + half
                return idx // 7, (idx % 7) * 65

            kf_ev = {}
            for tc in range(NT):
                for h in range(H):
                    u = tc * H + h
                    pb = 5 + (h % 2)
                    if u >= 2:
                        P.wait("tensor", vV, kf_ev[u - 2])
                    stat = kfm[(h % 2) * 64:(h % 2) * 64 + 64,
                               (h // 2) * 2048 + tc * 128:(h // 2) * 2048 + tc * 128 + 128]
                    hb0 = (h % 2) * 64
                    P.add("tensor", lambda e, st=stat, pb=pb, hb0=hb0: e.matmul(
                        bank(pb, 256), st, wfT[hb0:hb0 + 64, :], start=True, stop=True))
                    mv = P.sig("tensor", tM)
                    P.wait("vector", tM, mv)
                    P.add("vector", lambda e, pb=pb: e.tensor_scalar(
                        etmp[:, 0:256], bank(pb, 256), 0.0, None, ALU.min))
                    ev1 = P.sig("vector", vV)
                    P.wait("scalar", vV, ev1)
                    P.add("scalar", lambda e: e.activation(etmp2[:, 0:256], etmp[:, 0:256], ACT.Exp))
                    sv = P.sig("scalar", sS)
                    P.wait("vector", sS, sv)
                    P.add("vector", lambda e, pb=pb, h=h: e.scalar_tensor_tensor(
                        kfb[:, h % 2], bank(pb, 256), 0.0, etmp2[:, 0:256], ALU.max, ALU.add))
                    kf_ev[u] = P.sig("vector", vV)
                    P.wait("tensor", vV, kf_ev[u])
                    for half in range(2):
                        bk, off = kvslot(h, half)
                        idx = h * 2 + half
                        st0 = (tc == 0) and (idx % 7 == 0)
                        stat2 = kfb[:, h % 2][:, half * 128: half * 128 + 128]
                        rhsv = vtok[:, tc * 1024 + h * 64: tc * 1024 + h * 64 + 64]
                        P.add("tensor", lambda e, st=stat2, r=rhsv, bk=bk, off=off, tc=tc, st0=st0: e.matmul(
                            bslice(bk, off, 64), st, r, start=st0, stop=(tc == NT - 1)))
                        P.add("tensor", lambda e, st=stat2, bk=bk, off=off, tc=tc: e.matmul(
                            bslice(bk, off + 64, 1), st, onesbf[:], start=False, stop=(tc == NT - 1)))
            kv_mm_done = P.sig("tensor", tM)
            P.wait("scalar", tM, kv_mm_done)
            for h in range(H):
                for half in range(2):
                    bk, off = kvslot(h, half)
                    idx = h * 2 + half
                    P.add("scalar", lambda e, bk=bk, off=off, idx=idx: e.copy(
                        kvsb[:, idx * 65: idx * 65 + 65], bslice(bk, off, 65)))
            kv_ev_done = P.sig("scalar", sS)

            prev_a8_last = {}
            for h in range(H):
                hb = h % 2
                if h >= 2:
                    P.wait("tensor", vV, prev_a8_last[h - 2])
                for half in range(2):
                    pb = 5 + half
                    hb0 = (h % 2) * 64
                    stat = wfT[hb0:hb0 + 64, half * 128: half * 128 + 128]
                    for tc4 in range(4):
                        rhs = qfm[(h % 2) * 64:(h % 2) * 64 + 64,
                                  (h // 2) * 2048 + tc4 * 512:(h // 2) * 2048 + tc4 * 512 + 512]
                        P.add("tensor", lambda e, st=stat, r=rhs, pb=pb: e.matmul(
                            bank(pb), st, r, start=True, stop=True))
                        mv = P.sig("tensor", tM)
                        P.wait("vector", tM, mv)
                        P.add("vector", lambda e, pb=pb: e.tensor_scalar(
                            etmp[:], bank(pb), 0.0, None, ALU.min))
                        ev1 = P.sig("vector", vV)
                        P.wait("scalar", vV, ev1)
                        P.add("scalar", lambda e: e.activation(etmp2[:], etmp[:], ACT.Exp))
                        sv = P.sig("scalar", sS)
                        P.wait("vector", sS, sv)
                        dst = qff[:, hb][:, half * 2048 + tc4 * 512: half * 2048 + tc4 * 512 + 512]
                        P.add("vector", lambda e, pb=pb, dst=dst: e.scalar_tensor_tensor(
                            dst, bank(pb), 0.0, etmp2[:], ALU.max, ALU.add))
                        ev2 = P.sig("vector", vV)
                        P.wait("tensor", vV, ev2)
                if h == 0:
                    P.wait("tensor", sS, kv_ev_done)
                a8m = []
                for tc in range(NT):
                    pb = 2 + tc % 2
                    if tc >= 2:
                        P.wait("tensor", vV, a8m[tc - 2])
                    for half in range(2):
                        stat = qff[:, hb][:, half * 2048 + tc * 128: half * 2048 + tc * 128 + 128]
                        rhs = kvsb[:, (h * 2 + half) * 65: (h * 2 + half) * 65 + 65]
                        P.add("tensor", lambda e, st=stat, r=rhs, pb=pb, half=half: e.matmul(
                            bslice(pb, 0, 65), st, r, start=(half == 0), stop=(half == 1)))
                    mv = P.sig("tensor", tM)
                    P.wait("vector", tM, mv)
                    P.add("vector", lambda e, pb=pb: e.tensor_scalar(
                        nrm[:, 0:1], bslice(pb, 64, 1), 1e-6, None, ALU.add))
                    P.vsep(2)
                    P.add("vector", lambda e: e.reciprocal(nrm[:, 1:2], nrm[:, 0:1]))
                    P.vsep(2)
                    dst = a0sb[:, tc * 1024 + h * 64: tc * 1024 + h * 64 + 64]
                    P.add("vector", lambda e, pb=pb, dst=dst: e.tensor_scalar(
                        dst, bslice(pb, 0, 64), nrm[:, 1:2], None, ALU.mult))
                    a8m.append(P.sig("vector", vV))
                prev_a8_last[h] = a8m[-1]
            P.wait("sync", vV, vV.v)
            P.dma("sync", qS, attn0_h.rearrange("(tc p) d -> p tc d", p=128),
                  a0sb[:].rearrange("p (tc d) -> p tc d", d=1024))
            if "dqfm" in dbg or "dwfT" in dbg:
                P.dma("sync", qS, lnst_h[:, 0:64], junk[:, 100:164])
                P.dma("sync", qS, dbg_qfm, qfm[:])
                P.dma("sync", qS, dbg_kfm, kfm[:])
                P.dma("sync", qS, dbg_vtok, vtok[:])
                P.dma("sync", qS, dbg_kv, kvsb[:])
                P.dma("sync", qS, dbg_wfT, wfT[:])
            P.emit()

        if upto <= 2:
            return nc

        _es = contextlib.ExitStack()
        with _es:
            wqf32 = _es.enter_context(nc.sbuf_tensor("wqf32", [128, 2, NI], F32))
            wqbf = _es.enter_context(nc.sbuf_tensor("wqbf", [128, 2, NI], BF16))
            f_bm = _es.enter_context(nc.sbuf_tensor("f_bm", [128, 176], F32))
            f_s = _es.enter_context(nc.sbuf_tensor("f_s", [128, 176], F32))
            f_sinv = _es.enter_context(nc.sbuf_tensor("f_sinv", [128, 176], F32))
            f_T = _es.enter_context(nc.sbuf_tensor("f_T", [128, NI], F32))
            f_a = _es.enter_context(nc.sbuf_tensor("f_a", [128, NI], F32))
            f_b = _es.enter_context(nc.sbuf_tensor("f_b", [128, NI], F32))
            f_b2 = _es.enter_context(nc.sbuf_tensor("f_b2", [128, NI], F32))
            f_sg = _es.enter_context(nc.sbuf_tensor("f_sg", [128, NI], F32))
            f_v1 = _es.enter_context(nc.sbuf_tensor("f_v1", [128, NI], F32))
            fqs = dict(bm=f_bm, s=f_s, sinv=f_sinv, T=f_T, a=f_a, b=f_b, b2=f_b2,
                       sg=f_sg, v1=f_v1)
            P = Plan(nc, junk)
            vm, sm = [], []
            it = 0
            for (wsrc, wdst, nrows, width) in [
                    (W["L0_ffn_w_gate"], wqg_h, NI, D),
                    (W["L0_ffn_w_up"], wqu_h, NI, D),
                    (W["L0_ffn_w_down"], wqd_h, D, NI)]:
                for cchunk in range(nrows // 128):
                    pp = it % 2
                    if it >= 2:
                        P.wait("sync", vV, vm[it - 2])
                        P.wait("sync", sS, sm[it - 2])
                    ldv = P.dma("sync", qS, wqf32[:, pp][:, 0:width],
                                wsrc[cchunk * 128:(cchunk + 1) * 128, :])
                    P.wait("vector", qS, ldv)
                    P.wait("scalar", qS, ldv)
                    _quant_emit(P, fqs, wqf32[:, pp][:, 0:width], wqbf[:, pp][:, 0:width],
                                width, vV, sS, gG, use_g=True)
                    qv = P.sig("vector", vV)
                    vm.append(qv)
                    sm.append(sS.v)
                    P.wait("sync", vV, qv)
                    P.dma("sync", qS, wdst[cchunk * 128:(cchunk + 1) * 128, :],
                          wqbf[:, pp][:, 0:width])
                    it += 1
            P.emit()

        _es = contextlib.ExitStack()
        with _es:
            xqT = _es.enter_context(nc.sbuf_tensor("xqT", [128, 16384], BF16, is_transpose_dest=True))
            wgT = _es.enter_context(nc.sbuf_tensor("wgT", [128, 8, NHALF], BF16, is_transpose_dest=True))
            wuT = _es.enter_context(nc.sbuf_tensor("wuT", [128, 8, NHALF], BF16, is_transpose_dest=True))
            gsil = _es.enter_context(nc.sbuf_tensor("gsil", [128, NHALF], F32))
            hfull = _es.enter_context(nc.sbuf_tensor("hfull", [128, NHALF], F32))
            hqbf = _es.enter_context(nc.sbuf_tensor("hqbf", [128, NHALF], BF16))
            h_bm = _es.enter_context(nc.sbuf_tensor("h_bm", [128, 88], F32))
            h_s = _es.enter_context(nc.sbuf_tensor("h_s", [128, 88], F32))
            h_sinv = _es.enter_context(nc.sbuf_tensor("h_sinv", [128, 88], F32))
            h_T = _es.enter_context(nc.sbuf_tensor("h_T", [128, NHALF], F32))
            h_a = _es.enter_context(nc.sbuf_tensor("h_a", [128, NHALF], F32))
            h_b = _es.enter_context(nc.sbuf_tensor("h_b", [128, NHALF], F32))
            h_b2 = _es.enter_context(nc.sbuf_tensor("h_b2", [128, NHALF], F32))
            h_sg = _es.enter_context(nc.sbuf_tensor("h_sg", [128, NHALF], F32))
            h_v1 = _es.enter_context(nc.sbuf_tensor("h_v1", [128, NHALF], F32))
            hqs = dict(bm=h_bm, s=h_s, sinv=h_sinv, T=h_T, a=h_a, b=h_b, b2=h_b2,
                       sg=h_sg, v1=h_v1)
            P = Plan(nc, junk)
            for j in range(8):
                P.dma("sync", qS, xqT[:, j * 2048:(j + 1) * 2048],
                      xq_h[:, j * 128:(j + 1) * 128], transpose=True)
            ncs = [(0, 512), (512, 512), (1024, 384)]
            for nh in range(2):
                P.wait("sync", tM, tM.v)
                for k in range(8):
                    P.dma("sync", qS, wgT[:, k],
                          wqg_h[nh * NHALF:(nh + 1) * NHALF, k * 128:(k + 1) * 128], transpose=True)
                    P.dma("sync", qS, wuT[:, k],
                          wqu_h[nh * NHALF:(nh + 1) * NHALF, k * 128:(k + 1) * 128], transpose=True)
                wload = qS.v
                P.wait("tensor", qS, wload)
                for ts_ in range(16):
                    if nh > 0 or ts_ > 0:
                        P.wait("tensor", vV, vV.v)
                    for k in range(8):
                        st = xqT[:, k * 2048 + ts_ * 128: k * 2048 + ts_ * 128 + 128]
                        for ib, (off, wdt) in enumerate(ncs):
                            P.add("tensor", lambda e, st=st, k=k, ib=ib, off=off, wdt=wdt: e.matmul(
                                bank(ib, wdt), st, wgT[:, k][:, off:off + wdt],
                                start=(k == 0), stop=(k == 7)))
                            P.add("tensor", lambda e, st=st, k=k, ib=ib, off=off, wdt=wdt: e.matmul(
                                bank(3 + ib, wdt), st, wuT[:, k][:, off:off + wdt],
                                start=(k == 0), stop=(k == 7)))
                    mv = P.sig("tensor", tM)
                    P.wait("scalar", tM, mv)
                    P.add("scalar", lambda e: e.activation(gsil[:], psA[:, 0:NHALF], ACT.Silu))
                    sv = P.sig("scalar", sS)
                    P.wait("vector", sS, sv)
                    P.wait("vector", tM, mv)
                    P.add("vector", lambda e: e.tensor_tensor(
                        hfull[:], gsil[:], psA[:, 1536:1536 + NHALF], ALU.mult))
                    hm_ = P.sig("vector", vV)
                    P.vsep(2)
                    if "dh" in dbg:
                        P.wait("sync", vV, hm_)
                        P.dma("sync", qS,
                              dbg_h[ts_ * 128:(ts_ + 1) * 128, nh * NHALF:(nh + 1) * NHALF],
                              hfull[:])
                        P.dma("sync", qS,
                              dbg_g[ts_ * 128:(ts_ + 1) * 128, nh * NHALF:(nh + 1) * NHALF],
                              gsil[:])
                    _quant_emit(P, hqs, hfull[:], hqbf[:], NHALF, vV, sS, gG, use_g=True,
                                zmark=hm_)
                    qv = P.sig("vector", vV)
                    P.wait("sync", vV, qv)
                    P.dma("sync", qS,
                          hq_h[ts_ * 128:(ts_ + 1) * 128, nh * NHALF:(nh + 1) * NHALF], hqbf[:])
            P.emit()

        xbuf = sb("xbuf", [128, 16384])
        _es = contextlib.ExitStack()
        with _es:
            wdT = _es.enter_context(nc.sbuf_tensor("wdT", [128, 22, 1024], BF16, is_transpose_dest=True))
            hqT = _es.enter_context(nc.sbuf_tensor("hqT", [128, 22, 512], BF16, is_transpose_dest=True))
            xorig = _es.enter_context(nc.sbuf_tensor("xorig", [128, 2, 1024], F32))
            a0T = _es.enter_context(nc.sbuf_tensor("a0T", [128, 16384], BF16, is_transpose_dest=True))
            wo0T = _es.enter_context(nc.sbuf_tensor("wo0T", [128, 8, 1024], BF16, is_transpose_dest=True))
            P = Plan(nc, junk)
            for j in range(8):
                P.dma("sync", qS, a0T[:, j * 2048:(j + 1) * 2048],
                      attn0_h[:, j * 128:(j + 1) * 128], transpose=True)
                P.dma("sync", qS, wo0T[:, j], wo0_h[:, j * 128:(j + 1) * 128], transpose=True)
            for ncc in range(22):
                P.dma("sync", qS, wdT[:, ncc], wqd_h[:, ncc * 128:(ncc + 1) * 128], transpose=True)
            rmarks = []
            for tc in range(4):
                P.wait("sync", tM, tM.v)
                for ncc in range(22):
                    P.dma("sync", qS, hqT[:, ncc],
                          hq_h[tc * 512:(tc + 1) * 512, ncc * 128:(ncc + 1) * 128], transpose=True)
                hload = qS.v
                P.wait("tensor", qS, hload)
                for ts_ in range(4):
                    i_tile = tc * 4 + ts_
                    pp = i_tile % 2
                    pb = (i_tile % 2) * 2
                    if i_tile >= 2:
                        P.wait("tensor", vV, rmarks[i_tile - 2])
                        P.wait("sync", vV, rmarks[i_tile - 2])
                    for ncc in range(22):
                        st = hqT[:, ncc][:, ts_ * 128: ts_ * 128 + 128]
                        for hf in range(2):
                            P.add("tensor", lambda e, st=st, ncc=ncc, hf=hf, pb=pb: e.matmul(
                                bank(pb + hf), st, wdT[:, ncc][:, hf * 512:(hf + 1) * 512],
                                start=(ncc == 0), stop=(ncc == 21)))
                    ob = 4 + (i_tile % 2) * 2
                    for j in range(8):
                        sto = a0T[:, j * 2048 + i_tile * 128: j * 2048 + i_tile * 128 + 128]
                        for hf in range(2):
                            P.add("tensor", lambda e, sto=sto, j=j, hf=hf, ob=ob: e.matmul(
                                bank(ob + hf), sto, wo0T[:, j][:, hf * 512:(hf + 1) * 512],
                                start=(j == 0), stop=(j == 7)))
                    mv = P.sig("tensor", tM)
                    l1 = P.dma("sync", qS, xorig[:, pp], xin[i_tile * 128:(i_tile + 1) * 128, :])
                    P.wait("vector", qS, l1)
                    P.wait("vector", tM, mv)
                    xb = xbuf[:, i_tile * 1024:(i_tile + 1) * 1024]
                    for hf in range(2):
                        P.add("vector", lambda e, xb=xb, hf=hf, pp=pp, ob=ob: e.tensor_tensor(
                            xb[:, hf * 512:(hf + 1) * 512],
                            xorig[:, pp][:, hf * 512:(hf + 1) * 512],
                            bank(ob + hf), ALU.add))
                        P.add("vector", lambda e, xb=xb, hf=hf, pb=pb: e.tensor_tensor(
                            xb[:, hf * 512:(hf + 1) * 512], xb[:, hf * 512:(hf + 1) * 512],
                            bank(pb + hf), ALU.add))
                    rv = P.sig("vector", vV)
                    rmarks.append(rv)
                    P.wait("sync", vV, rv)
                    P.dma("sync", qS, x1_h[i_tile * 128:(i_tile + 1) * 128, :], xb)
            P.emit()

        if upto <= 3:
            return nc

        _es = contextlib.ExitStack()
        with _es:
            xn1f = _es.enter_context(nc.sbuf_tensor("xn1f", [128, 1024], F32))
            xn1b = _es.enter_context(nc.sbuf_tensor("xn1b", [128, 2, 1024], BF16))
            mrow = _es.enter_context(nc.sbuf_tensor("mrow", [1, 1024], F32))
            meanT = _es.enter_context(nc.sbuf_tensor("meanT", [128, 8], BF16))
            meanTf = _es.enter_context(nc.sbuf_tensor("meanTf", [128, 8], F32))
            wg8 = _es.enter_context(nc.sbuf_tensor("wg8", [128, 8], BF16))
            bg8 = _es.enter_context(nc.sbuf_tensor("bg8", [8, 1], F32))
            lg = _es.enter_context(nc.sbuf_tensor("lg", [8, 1], F32))
            lrow = _es.enter_context(nc.sbuf_tensor("lrow", [1, 8], F32))
            srt = _es.enter_context(nc.sbuf_tensor("srt", [1, 8], F32))
            io8 = _es.enter_context(nc.sbuf_tensor("io8", [1, 8], F32))
            tt8 = _es.enter_context(nc.sbuf_tensor("tt8", [1, 8], F32))
            tt8b = _es.enter_context(nc.sbuf_tensor("tt8b", [1, 8], F32))
            sc16 = _es.enter_context(nc.sbuf_tensor("sc16", [1, 16], F32))
            alph = _es.enter_context(nc.sbuf_tensor("alph", [1, 1], F32))
            iotaJ = _es.enter_context(nc.sbuf_tensor("iotaJ", [128, 1], F32))
            idxf = _es.enter_context(nc.sbuf_tensor("idxf", [128, 1], F32))
            idxu = _es.enter_context(nc.sbuf_tensor("idxu", [128, 1], U32))
            gbuf = _es.enter_context(nc.sbuf_tensor("gbuf", [128, HM], BF16))
            bgath = _es.enter_context(nc.sbuf_tensor("bgath", [1, HM], F32))
            bgath2 = _es.enter_context(nc.sbuf_tensor("bgath2", [128, 32], F32))
            jsq = _es.enter_context(nc.sbuf_tensor("jsq4", [128, 2, 1024], F32))
            P = Plan(nc, junk)
            vst = []
            for i in range(NT):
                pp = i % 2
                xb = xbuf[:, i * 1024:(i + 1) * 1024]
                if i >= 2:
                    P.wait("scalar", vV, vst[i - 2])
                P.add("scalar", lambda e, xb=xb, pp=pp: e.activation(jsq[:, pp], xb, ACT.Square))
                sm = P.sig("scalar", sS)
                P.wait("vector", sS, sm)
                P.add("vector", lambda e, xb=xb, i=i: e.tensor_reduce(
                    sum_a[:, i:i + 1], xb, AX.X, ALU.add))
                P.add("vector", lambda e, pp=pp, i=i: e.tensor_reduce(
                    sum_b[:, i:i + 1], jsq[:, pp], AX.X, ALU.add))
                vst.append(P.sig("vector", vV))
            P.add("vector", lambda e: e.tensor_scalar(m1[:], sum_a[:], 1.0 / D, None, ALU.mult))
            P.add("vector", lambda e: e.tensor_scalar(junk[:, 0:NT], sum_b[:], 1.0 / D, None, ALU.mult))
            P.add("vector", lambda e: e.tensor_tensor(r1[:], m1[:], m1[:], ALU.mult))
            P.add("vector", lambda e: e.tensor_tensor(junk[:, 0:NT], junk[:, 0:NT], r1[:], ALU.subtract))
            P.add("vector", lambda e: e.tensor_scalar(junk[:, 0:NT], junk[:, 0:NT], 1e-5, None, ALU.add))
            vv = P.sig("vector", vV)
            P.wait("scalar", vV, vv)
            P.add("scalar", lambda e: e.activation(junk[:, NT:2 * NT], junk[:, 0:NT], ACT.Sqrt))
            sv2 = P.sig("scalar", sS)
            P.wait("vector", sS, sv2)
            P.add("vector", lambda e: e.reciprocal(r1[:], junk[:, NT:2 * NT]))
            rstat = P.sig("vector", vV)
            if "lnst4" in dbg:
                P.wait("sync", vV, rstat)
                P.dma("sync", qS, lnst4_h[:, 0:16], m1[:])
                P.dma("sync", qS, lnst4_h[:, 16:32], r1[:])
                P.dma("sync", qS, lnst4_h[:, 32:48], sum_a[:])
                P.dma("sync", qS, lnst4_h[:, 48:64], sum_b[:])
            P.wait("scalar", vV, rstat)
            copy_marks = []
            mm_marks = []
            store_marks = []
            for i in range(NT):
                pp = i % 2
                xb = xbuf[:, i * 1024:(i + 1) * 1024]
                rb = r1[:, i:i + 1].broadcast_to([128, 1024])
                if i >= 1:
                    P.wait("vector", sS, copy_marks[i - 1])
                P.add("vector", lambda e, xb=xb, i=i, rb=rb: e.scalar_tensor_tensor(
                    xn1f[:], xb, m1[:, i:i + 1], rb, ALU.subtract, ALU.mult))
                xv = P.sig("vector", vV)
                P.wait("scalar", vV, xv)
                if i >= 2:
                    P.wait("scalar", tM, mm_marks[i - 2])
                    P.wait("scalar", qS, store_marks[i - 2])
                P.add("scalar", lambda e, pp=pp: e.copy(xn1b[:, pp], xn1f[:]))
                cs = P.sig("scalar", sS)
                copy_marks.append(cs)
                P.wait("sync", sS, cs)
                store_marks.append(P.dma("sync", qS, xn1_h[i * 128:(i + 1) * 128, :], xn1b[:, pp]))
                P.wait("tensor", sS, cs)
                for hf in range(2):
                    P.add("tensor", lambda e, pp=pp, hf=hf, i=i: e.matmul(
                        psA[0:1, 3072 + hf * 512: 3072 + hf * 512 + 512], onesbf[:],
                        xn1b[:, pp][:, hf * 512:(hf + 1) * 512],
                        start=(i == 0 and hf == 0), stop=(i == NT - 1)))
                mm_marks.append(P.sig("tensor", tM))
            P.wait("vector", tM, mm_marks[-1])
            P.add("vector", lambda e: e.tensor_scalar(
                mrow[:], psA[0:1, 3072:3072 + 1024], 1.0 / T, None, ALU.mult))
            mv2 = P.sig("vector", vV)
            P.wait("sync", vV, mv2)
            P.dma("sync", qS, mean_h, mrow[:])
            mh2 = P.dma("sync", qS, meanTf[:], mean_h.rearrange("o (c p) -> (o p) c", p=128), slow=True)
            P.wait("scalar", qS, mh2)
            P.add("scalar", lambda e: e.copy(meanT[:], meanTf[:]))
            sv3 = P.sig("scalar", sS)
            P.dma("sync", qS, bg8[:], W["L1_ffn_b_gate"])
            bl = P.dma("sync", qS, alph[:], W["L1_ffn_alpha"])
            P.wait("tensor", sS, sv3)
            P.wait("tensor", vV, mv2)
            lgm = []
            for j in range(8):
                if j > 0:
                    P.wait("sync", tM, lgm[-1])
                wv_ = P.dma("sync", qS, wg8[:], wgate_h[j * 128:(j + 1) * 128, :])
                P.wait("tensor", qS, wv_)
                P.add("tensor", lambda e, j=j: e.matmul(
                    psA[0:8, 3072:3073], wg8[:], meanT[:, j:j + 1], start=(j == 0), stop=(j == 7)))
                lgm.append(P.sig("tensor", tM))
            P.wait("vector", tM, lgm[-1])
            P.wait("vector", qS, bl)
            P.add("vector", lambda e: e.tensor_tensor(lg[:], psA[0:8, 3072:3073], bg8[:], ALU.add))
            lgv = P.sig("vector", vV)
            P.wait("sync", vV, lgv)
            P.dma("sync", qS, scal_h[:, 0:8].rearrange("o e -> (o e)").unsqueeze(1), lg[:])
            lrl = P.dma("sync", qS, lrow[:], scal_h[:, 0:8])
            P.add("gpsimd", lambda e: e.iota(io8[:], [[1, 8]], base=0, channel_multiplier=0,
                                             allow_small_or_imprecise_dtypes=True))
            P.add("gpsimd", lambda e: e.iota(iotaJ[:], [[0, 1]], base=0, channel_multiplier=1,
                                             allow_small_or_imprecise_dtypes=True))
            gio = P.sig("gpsimd", gG)
            P.wait("vector", qS, lrl)
            P.wait("vector", gG, gio)
            P.vsep(3)
            P.add("vector", lambda e: e.max(srt[:], lrow[:]))
            P.vsep(3)
            P.add("vector", lambda e: e.tensor_scalar(tt8[:], lrow[:], srt[:, 0:1], None, ALU.is_equal))
            P.vsep(3)
            P.add("vector", lambda e: e.tensor_tensor(tt8b[:], tt8[:], io8[:], ALU.mult))
            P.vsep(3)
            P.add("vector", lambda e: e.tensor_reduce(sc16[:, 3:4], tt8b[:], AX.X, ALU.add))
            P.vsep(3)
            P.add("vector", lambda e: e.tensor_scalar(tt8[:], tt8[:], -1.0, 1.0, ALU.mult, ALU.add))
            P.vsep(3)
            P.add("vector", lambda e: e.tensor_scalar(tt8b[:], lrow[:], srt[:, 1:2], None, ALU.is_equal))
            P.vsep(3)
            P.add("vector", lambda e: e.tensor_tensor(tt8b[:], tt8b[:], tt8[:], ALU.mult))
            P.vsep(3)
            P.add("vector", lambda e: e.tensor_tensor(tt8b[:], tt8b[:], io8[:], ALU.mult))
            P.vsep(3)
            P.add("vector", lambda e: e.tensor_reduce(sc16[:, 4:5], tt8b[:], AX.X, ALU.add))
            P.vsep(3)
            P.add("vector", lambda e: e.tensor_tensor(sc16[:, 5:6], srt[:, 0:1], srt[:, 1:2], ALU.subtract))
            gv1 = P.sig("vector", vV)
            P.wait("scalar", vV, gv1)
            P.add("scalar", lambda e: e.activation(sc16[:, 6:7], sc16[:, 5:6], ACT.Sigmoid))
            sgv = P.sig("scalar", sS)
            P.wait("vector", sS, sgv)
            P.vsep(3)
            P.add("vector", lambda e: e.tensor_scalar(sc16[:, 0:1], alph[:], 0.1, 1.0, ALU.max, ALU.min))
            P.vsep(3)
            P.add("vector", lambda e: e.tensor_scalar(sc16[:, 7:8], sc16[:, 0:1], -1.0, 1.0, ALU.mult, ALU.add))
            P.vsep(3)
            P.add("vector", lambda e: e.tensor_tensor(sc16[:, 1:2], sc16[:, 6:7], sc16[:, 7:8], ALU.mult))
            P.vsep(3)
            P.add("vector", lambda e: e.tensor_tensor(sc16[:, 2:3], sc16[:, 7:8], sc16[:, 1:2], ALU.subtract))
            scv = P.sig("vector", vV)
            P.wait("sync", vV, scv)
            P.dma("sync", qS, scal_h[:, 0:16], sc16[:])
            screpl = P.dma("sync", qS, screp[:], scal_h.broadcast_to([128, 16]))
            P.wait("vector", qS, screpl)
            for k in range(2):
                scol = screp[:, 3 + k:4 + k]
                for (nch, wid, srcap, dsth, rowsz) in [
                        (8, HM, W["L1_ffn_W1e"], w1g_h[k], D),
                        (32, D, W["L1_ffn_W2e"], w2g_h[k], HM)]:
                    for jc in range(nch):
                        P.add("vector", lambda e, scol=scol, rs=rowsz, jc=jc: e.scalar_tensor_tensor(
                            idxf[:], scol, float(rs), iotaJ[:, 0:1], ALU.mult, ALU.add))
                        P.vsep(3)
                        P.add("vector", lambda e, jc=jc: e.tensor_scalar(
                            idxf[:], idxf[:], float(jc * 128), None, ALU.add))
                        P.vsep(3)
                        P.add("vector", lambda e: e.tensor_copy(idxu[:], idxf[:]))
                        iv = P.sig("vector", vV)
                        P.vsep(3)
                        P.wait("gpsimd", vV, iv)
                        P.add("gpsimd", lambda e, srcap=srcap, wid=wid: e.indirect_dma_start(
                            gbuf[:, 0:wid], None, srcap[0:128, :],
                            IndirectOffsetOnAxis(idxu[:], axis=0)).then_inc(qG.h, 16))
                        gv_ = qG.bump(16)
                        P.wait("sync", qG, gv_)
                        sv_ = P.dma("sync", qS, dsth[jc * 128:(jc + 1) * 128, :], gbuf[:, 0:wid])
                        P.wait("gpsimd", qS, sv_)
                        P.wait("vector", qS, sv_)
                for (bsrc, bw, bdst) in [(W["L1_ffn_b1e"], HM // 128, b1g_h[k]),
                                         (W["L1_ffn_b2e"], D // 128, b2g_h[k])]:
                    P.add("vector", lambda e, scol=scol: e.scalar_tensor_tensor(
                        idxf[:], scol, 128.0, iotaJ[:, 0:1], ALU.mult, ALU.add))
                    P.vsep(3)
                    P.add("vector", lambda e: e.tensor_copy(idxu[:], idxf[:]))
                    iv = P.sig("vector", vV)
                    P.vsep(3)
                    P.wait("gpsimd", vV, iv)
                    srcv = bsrc.rearrange("e (p c) -> (e p) c", p=128)
                    P.add("gpsimd", lambda e, srcv=srcv, bw=bw: e.indirect_dma_start(
                        bgath[0:128, 0:bw].__class__(bgath.tensor, bgath[0:128, 0:bw].offset, bgath[0:128, 0:bw].ap)
                        if False else bgath2[:, 0:bw], None, srcv[0:128, :],
                        IndirectOffsetOnAxis(idxu[:], axis=0)).then_inc(qG.h, 16))
                    gv_ = qG.bump(16)
                    P.wait("sync", qG, gv_)
                    sv_ = P.dma("sync", qS,
                                bdst.rearrange("o (p c) -> (o p) c", c=bw), bgath2[:, 0:bw])
                    P.wait("gpsimd", qS, sv_)
                    P.wait("vector", qS, sv_)
            P.emit()

        if upto <= 4:
            return nc

        _es = contextlib.ExitStack()
        with _es:
            xn1T = _es.enter_context(nc.sbuf_tensor("xn1T", [128, 16384], BF16, is_transpose_dest=True))
            wqkv = _es.enter_context(nc.sbuf_tensor("wqkv", [128, 3, 8, 1024], BF16))
            brep = _es.enter_context(nc.sbuf_tensor("brep", [128, 3, 1024], F32))
            Qt = _es.enter_context(nc.sbuf_tensor("Qt", [128, 1024], F32))
            Kt = _es.enter_context(nc.sbuf_tensor("Kt", [128, 1024], F32))
            Vt = _es.enter_context(nc.sbuf_tensor("Vt", [128, 1024], F32))
            SA = _es.enter_context(nc.sbuf_tensor("SA", [128, 64, 32], F32))
            SB = _es.enter_context(nc.sbuf_tensor("SB", [128, 32, 32], F32))
            dnA = _es.enter_context(nc.sbuf_tensor("dnA", [128, 64], F32))
            nmA = _es.enter_context(nc.sbuf_tensor("nmA", [128, 64], F32))
            dnB = _es.enter_context(nc.sbuf_tensor("dnB", [128, 32], F32))
            nmB = _es.enter_context(nc.sbuf_tensor("nmB", [128, 32], F32))
            recd = _es.enter_context(nc.sbuf_tensor("recd", [128, 64], F32))
            mm4 = _es.enter_context(nc.sbuf_tensor("mm4", [128, 4], F32))
            negM = _es.enter_context(nc.sbuf_tensor("negM", [128, 1], F32))
            abuf = _es.enter_context(nc.sbuf_tensor("abuf", [128, 1024], F32))
            abbf = _es.enter_context(nc.sbuf_tensor("abbf", [128, 2, 1024], BF16))
            mneg = _es.enter_context(nc.sbuf_tensor("mneg", [128, 32, 32], F32))
            P = Plan(nc, junk)
            P.add("gpsimd", lambda e: e.memset(mneg[:], 0.0))
            P.add("gpsimd", lambda e: e.affine_select(
                mneg[:], mneg[:], [[1, 32], [-1, 32]], ALU.is_ge, -1e30,
                channel_multiplier=0))
            mkd = P.sig("gpsimd", gG)
            for j in range(8):
                P.dma("sync", qS, xn1T[:, j * 2048:(j + 1) * 2048],
                      xn1_h[:, j * 128:(j + 1) * 128], transpose=True)
            for wi, wh in enumerate([wq1_h, wk1_h, wv1_h]):
                for j in range(8):
                    P.dma("sync", qS, wqkv[:, wi, j], wh[j * 128:(j + 1) * 128, :])
            for bi_, nm_ in enumerate(["L1_attn_bq", "L1_attn_bk", "L1_attn_bv"]):
                P.dma("sync", qS, brep[:, bi_], W[nm_].broadcast_to([128, 1024]))
            ldall = qS.v
            P.wait("vector", qS, ldall)
            P.add("vector", lambda e: e.tensor_scalar(brep[:, 0], brep[:, 0], 0.125, None, ALU.mult))
            bqs = P.sig("vector", vV)
            P.wait("tensor", qS, ldall)
            scop = []
            for tc in range(NT):
                if tc > 0:
                    P.wait("tensor", vV, vV.v)
                for j in range(8):
                    st = xn1T[:, j * 2048 + tc * 128: j * 2048 + tc * 128 + 128]
                    for wi in range(3):
                        for hf in range(2):
                            P.add("tensor", lambda e, st=st, wi=wi, hf=hf, j=j: e.matmul(
                                bank(wi * 2 + hf), st, wqkv[:, wi, j][:, hf * 512:(hf + 1) * 512],
                                start=(j == 0), stop=(j == 7)))
                mv = P.sig("tensor", tM)
                P.wait("vector", tM, mv)
                if tc == 0:
                    P.wait("vector", vV, bqs)
                if tc >= 2:
                    P.wait("vector", sS, scop[tc - 2])
                for hf in range(2):
                    P.add("vector", lambda e, hf=hf: e.scalar_tensor_tensor(
                        Qt[:, hf * 512:(hf + 1) * 512], bank(hf), 0.125,
                        brep[:, 0][:, hf * 512:(hf + 1) * 512], ALU.mult, ALU.add))
                    P.add("vector", lambda e, hf=hf: e.tensor_tensor(
                        Kt[:, hf * 512:(hf + 1) * 512], bank(2 + hf),
                        brep[:, 1][:, hf * 512:(hf + 1) * 512], ALU.add))
                    P.add("vector", lambda e, hf=hf: e.tensor_tensor(
                        Vt[:, hf * 512:(hf + 1) * 512], bank(4 + hf),
                        brep[:, 2][:, hf * 512:(hf + 1) * 512], ALU.add))
                P.add("vector", lambda e: e.tensor_reduce(mm4[:, 0:1], Qt[:], AX.X, ALU.max,
                                                          apply_absolute_value=True))
                P.add("vector", lambda e: e.tensor_reduce(mm4[:, 1:2], Kt[:], AX.X, ALU.max,
                                                          apply_absolute_value=True))
                P.add("vector", lambda e: e.tensor_tensor(negM[:], mm4[:, 0:1], mm4[:, 1:2], ALU.mult))
                P.add("vector", lambda e: e.tensor_scalar(negM[:], negM[:], -1.0, None, ALU.mult))
                qkv = P.sig("vector", vV)
                if "dQ" in dbg:
                    P.wait("sync", vV, qkv)
                    P.dma("sync", qS, dbg_Q[tc * 128:(tc + 1) * 128, :], Qt[:])
                    P.dma("sync", qS, dbg_K[tc * 128:(tc + 1) * 128, :], Kt[:])
                    P.dma("sync", qS, dbg_V[tc * 128:(tc + 1) * 128, :], Vt[:])
                P.wait("scalar", vV, qkv)
                P.wait("gpsimd", vV, qkv)
                for h in range(H):
                    q0 = Qt[:, h * 64: h * 64 + 64]
                    k0 = Kt[:, h * 64: h * 64 + 32]
                    k1 = Kt[:, h * 64 + 32: h * 64 + 64]
                    v0 = Vt[:, h * 64: h * 64 + 32]
                    v1_ = Vt[:, h * 64 + 32: h * 64 + 64]
                    P.add("vector", lambda e, q0=q0, k0=k0: e.tensor_tensor(
                        SA[:], q0.unsqueeze(2).broadcast_to([128, 64, 32]),
                        k0.unsqueeze(1).broadcast_to([128, 64, 32]), ALU.mult))
                    P.add("vector", lambda e, q0=q0, k1=k1: e.tensor_tensor(
                        SB[:], q0[:, 32:64].unsqueeze(2).broadcast_to([128, 32, 32]),
                        k1.unsqueeze(1).broadcast_to([128, 32, 32]), ALU.mult))
                    sv_ = P.sig("vector", vV)
                    P.wait("gpsimd", vV, sv_)
                    P.add("gpsimd", lambda e: e.tensor_tensor(
                        SA[:, 0:32, :], SA[:, 0:32, :], mneg[:], ALU.add))
                    P.add("gpsimd", lambda e: e.tensor_tensor(
                        SB[:], SB[:], mneg[:], ALU.add))
                    gv_ = P.sig("gpsimd", gG)
                    P.wait("scalar", gG, gv_)
                    P.add("scalar", lambda e: e.activation(
                        SA[:].rearrange("p a b -> p (a b)"), SA[:].rearrange("p a b -> p (a b)"),
                        ACT.Exp, bias=negM[:]))
                    P.add("scalar", lambda e: e.activation(
                        SB[:].rearrange("p a b -> p (a b)"), SB[:].rearrange("p a b -> p (a b)"),
                        ACT.Exp, bias=negM[:]))
                    ev_ = P.sig("scalar", sS)
                    if "dSA" in dbg and tc == 0 and h == 0:
                        P.wait("sync", sS, ev_)
                        P.dma("sync", qS, dbg_SA, SA[:])
                        P.dma("sync", qS, dbg_SB, SB[:])
                        dwait = qS.v
                        P.wait("vector", qS, dwait)
                    P.wait("vector", sS, ev_)
                    P.add("vector", lambda e: e.tensor_reduce(dnA[:], SA[:], AX.X, ALU.add))
                    P.add("vector", lambda e: e.tensor_reduce(dnB[:], SB[:], AX.X, ALU.add))
                    P.add("vector", lambda e: e.tensor_tensor(
                        dnA[:, 32:64], dnA[:, 32:64], dnB[:], ALU.add))
                    P.add("vector", lambda e, v0=v0: e.tensor_tensor(
                        SA[:], SA[:], v0.unsqueeze(1).broadcast_to([128, 64, 32]), ALU.mult))
                    P.add("vector", lambda e, v1_=v1_: e.tensor_tensor(
                        SB[:], SB[:], v1_.unsqueeze(1).broadcast_to([128, 32, 32]), ALU.mult))
                    P.add("vector", lambda e: e.reciprocal(recd[:], dnA[:]))
                    P.add("vector", lambda e: e.tensor_reduce(nmA[:], SA[:], AX.X, ALU.add))
                    P.add("vector", lambda e: e.tensor_reduce(nmB[:], SB[:], AX.X, ALU.add))
                    P.add("vector", lambda e: e.tensor_tensor(
                        nmA[:, 32:64], nmA[:, 32:64], nmB[:], ALU.add))
                    P.add("vector", lambda e, h=h: e.tensor_tensor(
                        abuf[:, h * 64:(h + 1) * 64], nmA[:], recd[:], ALU.mult))
                    hv = P.sig("vector", vV)
                    if h < H - 1:
                        P.wait("scalar", vV, hv)
                        P.wait("gpsimd", vV, hv)
                tiledone = vV.v
                P.wait("scalar", vV, tiledone)
                pp = tc % 2
                P.add("scalar", lambda e, pp=pp: e.copy(abbf[:, pp], abuf[:]))
                av = P.sig("scalar", sS)
                scop.append(av)
                P.wait("sync", sS, av)
                P.dma("sync", qS, attn1_h[tc * 128:(tc + 1) * 128, :], abbf[:, pp])
            P.emit()

        _es = contextlib.ExitStack()
        with _es:
            a1T = _es.enter_context(nc.sbuf_tensor("a1T", [128, 16384], BF16, is_transpose_dest=True))
            woS = _es.enter_context(nc.sbuf_tensor("woS", [128, 8, 1024], BF16))
            borep = _es.enter_context(nc.sbuf_tensor("borep", [128, 1024], F32))
            P = Plan(nc, junk)
            for j in range(8):
                P.dma("sync", qS, a1T[:, j * 2048:(j + 1) * 2048],
                      attn1_h[:, j * 128:(j + 1) * 128], transpose=True)
                P.dma("sync", qS, woS[:, j], wo1_h[j * 128:(j + 1) * 128, :])
            lda = P.dma("sync", qS, borep[:], W["L1_attn_bo"].broadcast_to([128, 1024]))
            P.wait("tensor", qS, lda)
            P.wait("vector", qS, lda)
            om = []
            for tc in range(NT):
                pb = (tc % 2) * 2
                if tc >= 2:
                    P.wait("tensor", vV, om[tc - 2])
                for j in range(8):
                    st = a1T[:, j * 2048 + tc * 128: j * 2048 + tc * 128 + 128]
                    for hf in range(2):
                        P.add("tensor", lambda e, st=st, j=j, hf=hf, pb=pb: e.matmul(
                            bank(pb + hf), st, woS[:, j][:, hf * 512:(hf + 1) * 512],
                            start=(j == 0), stop=(j == 7)))
                mv = P.sig("tensor", tM)
                P.wait("vector", tM, mv)
                xb = xbuf[:, tc * 1024:(tc + 1) * 1024]
                P.add("vector", lambda e, xb=xb: e.tensor_tensor(xb, xb, borep[:], ALU.add))
                for hf in range(2):
                    P.add("vector", lambda e, xb=xb, hf=hf, pb=pb: e.tensor_tensor(
                        xb[:, hf * 512:(hf + 1) * 512], xb[:, hf * 512:(hf + 1) * 512],
                        bank(pb + hf), ALU.add))
                om.append(P.sig("vector", vV))
            P.emit()

        if upto <= 5:
            return nc

        _es = contextlib.ExitStack()
        with _es:
            xn1T2 = _es.enter_context(nc.sbuf_tensor("xn1T2", [128, 16384], BF16, is_transpose_dest=True))
            w1h = _es.enter_context(nc.sbuf_tensor("w1h", [128, 8, 2048], BF16))
            w2h = _es.enter_context(nc.sbuf_tensor("w2h", [128, 8, 1024], BF16))
            hfm = _es.enter_context(nc.sbuf_tensor("hfm", [128, 32, 512], BF16))
            expacc = _es.enter_context(nc.sbuf_tensor("expacc", [128, 8, 512], F32))
            b1col = _es.enter_context(nc.sbuf_tensor("b1col", [128, 3, 32], F32))
            bcmb = _es.enter_context(nc.sbuf_tensor("bcmb", [128, 8], F32))
            tmp8 = _es.enter_context(nc.sbuf_tensor("tmp8", [128, 8], F32))
            tmp8b = _es.enter_context(nc.sbuf_tensor("tmp8b", [128, 8], F32))
            ffv = _es.enter_context(nc.sbuf_tensor("ffv", [128, 512], F32))
            ffb = _es.enter_context(nc.sbuf_tensor("ffb", [128, 2, 512], BF16))
            ftok = _es.enter_context(nc.sbuf_tensor("ftok", [128, 2, 1024], BF16, is_transpose_dest=True))
            P = Plan(nc, junk)
            for j in range(8):
                P.dma("sync", qS, xn1T2[:, j * 2048:(j + 1) * 2048],
                      xn1_h[:, j * 128:(j + 1) * 128], transpose=True)
            P.dma("sync", qS, b1col[:, 0], W["L1_ffn_b1m"].rearrange("o (c p) -> (o p) c", p=128), slow=True)
            for k in range(2):
                P.dma("sync", qS, b1col[:, 1 + k], b1g_h[k].rearrange("o (c p) -> (o p) c", p=128), slow=True)
            P.dma("sync", qS, bcmb[:], W["L1_ffn_b2m"].rearrange("o (c p) -> (o p) c", p=128), slow=True)
            P.dma("sync", qS, tmp8[:], b2g_h[0].rearrange("o (c p) -> (o p) c", p=128), slow=True)
            bld = P.dma("sync", qS, tmp8b[:], b2g_h[1].rearrange("o (c p) -> (o p) c", p=128), slow=True)
            P.wait("vector", qS, bld)
            P.add("vector", lambda e: e.tensor_scalar(bcmb[:], bcmb[:], screp[:, 0:1], None, ALU.mult))
            P.vsep(3)
            P.add("vector", lambda e: e.tensor_scalar(tmp8[:], tmp8[:], screp[:, 1:2], None, ALU.mult))
            P.vsep(3)
            P.add("vector", lambda e: e.tensor_tensor(bcmb[:], bcmb[:], tmp8[:], ALU.add))
            P.vsep(3)
            P.add("vector", lambda e: e.tensor_scalar(tmp8b[:], tmp8b[:], screp[:, 2:3], None, ALU.mult))
            P.vsep(3)
            P.add("vector", lambda e: e.tensor_tensor(bcmb[:], bcmb[:], tmp8b[:], ALU.add))
            bcv = P.sig("vector", vV)
            if "dps" in dbg:
                P.wait("sync", vV, bcv)
                P.dma("sync", qS, dbg_ps[:, 24:32], bcmb[:])
                P.dma("sync", qS, dbg_ps[:, 32:40], tmp8[:])
                P.dma("sync", qS, dbg_ps[:, 40:48], tmp8b[:])

            parts = [(w1g_h[0], w2g_h[0], 1, "e1"),
                     (w1g_h[1], w2g_h[1], 2, "e2"),
                     (w1m_h, w2m_h, 0, "main")]
            fmarks = []
            for tc in range(4):
                for (w1src, w2src, bix, kind) in parts:
                    if fmarks:
                        P.wait("tensor", vV, fmarks[-1])
                    gelu_marks = []
                    for nh in range(2):
                        P.wait("sync", tM, tM.v)
                        for jj in range(8):
                            P.dma("sync", qS, w1h[:, jj],
                                  w1src[jj * 128:(jj + 1) * 128, nh * 2048:(nh + 1) * 2048])
                        wl = qS.v
                        P.wait("tensor", qS, wl)
                        for n_ in range(16):
                            n = nh * 16 + n_
                            pb = n_ % 2
                            if n_ >= 2:
                                P.wait("tensor", sS, gelu_marks[-2])
                            for j in range(8):
                                st = w1h[:, j][:, n_ * 128: n_ * 128 + 128]
                                rhs = xn1T2[:, j * 2048 + tc * 512: j * 2048 + tc * 512 + 512]
                                P.add("tensor", lambda e, st=st, rhs=rhs, pb=pb, j=j: e.matmul(
                                    bank(pb), st, rhs, start=(j == 0), stop=(j == 7)))
                            mv = P.sig("tensor", tM)
                            P.wait("scalar", tM, mv)
                            P.add("scalar", lambda e, n=n, pb=pb, bix=bix: e.activation(
                                hfm[:, n], bank(pb), ACT.Gelu_apprx_tanh,
                                bias=b1col[:, bix, n:n + 1]))
                            gelu_marks.append(P.sig("scalar", sS))
                    P.wait("tensor", sS, gelu_marks[-1])
                    if "dhfm" in dbg and tc == 0 and kind == "main":
                        P.wait("sync", sS, gelu_marks[-1])
                        P.dma("sync", qS, dbg_hfm, hfm[:].rearrange("p a b -> p (a b)"))
                    for nq in range(4):
                        P.wait("sync", tM, tM.v)
                        for n_ in range(8):
                            nn = nq * 8 + n_
                            P.dma("sync", qS, w2h[:, n_], w2src[nn * 128:(nn + 1) * 128, :])
                        w2l = qS.v
                        P.wait("tensor", qS, w2l)
                        for n_ in range(8):
                            nn = nq * 8 + n_
                            for i_ in range(8):
                                st = w2h[:, n_][:, i_ * 128: i_ * 128 + 128]
                                P.add("tensor", lambda e, st=st, n_=n_, i_=i_, nn=nn: e.matmul(
                                    bank(i_), st, hfm[:, nn], start=(nn == 0), stop=(nn == 31)))
                        if nq < 3:
                            P.sig("tensor", tM)
                    dmv = P.sig("tensor", tM)
                    if "dps" in dbg and tc == 0 and kind == "e1":
                        P.wait("scalar", tM, dmv)
                        for i_ in range(8):
                            P.add("scalar", lambda e, i_=i_: e.copy(junk[:, 0:512], bank(i_)))
                            dpv = P.sig("scalar", sS)
                            P.wait("sync", sS, dpv)
                            dw = P.dma("sync", qS, dbg_ps[:, i_ * 512:(i_ + 1) * 512], junk[:, 0:512])
                            P.wait("scalar", qS, dw)
                        P.wait("tensor", sS, sS.v)
                        P.wait("vector", sS, sS.v)
                    P.wait("vector", tM, dmv)
                    if kind == "e1":
                        P.wait("vector", vV, bcv)
                        for i_ in range(8):
                            P.add("vector", lambda e, i_=i_: e.tensor_scalar(
                                expacc[:, i_], bank(i_), screp[:, 1:2], bcmb[:, i_:i_ + 1],
                                ALU.mult, ALU.add))
                        fmarks.append(P.sig("vector", vV))
                        if "dexp" in dbg and tc == 0:
                            P.wait("sync", vV, fmarks[-1])
                            P.dma("sync", qS, dbg_exp, expacc[:].rearrange("p a b -> p (a b)"))
                            P.dma("sync", qS, dbg_ps[:, 0:16], screp[:])
                            P.dma("sync", qS, dbg_ps[:, 16:24], bcmb[:])
                            dxw = qS.v
                            P.wait("vector", qS, dxw)
                    elif kind == "e2":
                        for i_ in range(8):
                            P.add("vector", lambda e, i_=i_: e.scalar_tensor_tensor(
                                expacc[:, i_], bank(i_), screp[:, 2:3], expacc[:, i_],
                                ALU.mult, ALU.add))
                        fmarks.append(P.sig("vector", vV))
                        if "dexp" in dbg and tc == 0:
                            P.wait("sync", vV, fmarks[-1])
                            P.dma("sync", qS, dbg_h[0:128, 0:2048],
                                  expacc[:, 0:4].rearrange("p a b -> p (a b)"))
                            dx2 = P.dma("sync", qS, dbg_h[128:256, 0:2048],
                                        expacc[:, 4:8].rearrange("p a b -> p (a b)"))
                            P.wait("vector", qS, dx2)
                    else:
                        fv = None
                        cpm = []
                        dmam = []
                        for i_ in range(8):
                            if i_ >= 1:
                                P.wait("vector", sS, cpm[i_ - 1])
                            P.add("vector", lambda e, i_=i_: e.scalar_tensor_tensor(
                                ffv[:], bank(i_), screp[:, 0:1], expacc[:, i_],
                                ALU.mult, ALU.add))
                            fv = P.sig("vector", vV)
                            P.wait("scalar", vV, fv)
                            if i_ >= 2:
                                P.wait("scalar", qS, dmam[i_ - 2])
                            P.add("scalar", lambda e, i_=i_: e.copy(ffb[:, i_ % 2], ffv[:]))
                            sv_ = P.sig("scalar", sS)
                            cpm.append(sv_)
                            P.wait("sync", sS, sv_)
                            dmam.append(P.dma("sync", qS,
                                  ffn1_h[i_ * 128:(i_ + 1) * 128, tc * 512:(tc + 1) * 512],
                                  ffb[:, i_ % 2]))
                        fmarks.append(fv)
            fsync = qS.v
            P.wait("sync", vV, fmarks[-1])
            addm = []
            for i in range(NT):
                pp = i % 2
                if i >= 2:
                    P.wait("sync", vV, addm[i - 2])
                fl = P.dma("sync", qS, ftok[:, pp], ffn1_h[:, i * 128:(i + 1) * 128], transpose=True)
                P.wait("vector", qS, fl)
                xb = xbuf[:, i * 1024:(i + 1) * 1024]
                P.add("vector", lambda e, xb=xb, pp=pp: e.tensor_tensor(xb, xb, ftok[:, pp], ALU.add))
                addm.append(P.sig("vector", vV))
            P.emit()

        P = Plan(nc, junk)
        fin = P.dma("sync", qS, y.rearrange("(tc p) d -> p tc d", p=128),
                    xbuf[:].rearrange("p (tc d) -> p tc d", d=1024))
        P.wait("sync", qS, fin)
        P.emit()
        return nc



def kernel(**inputs):
    nc = build()
    lower_extended_insts(nc)
    x = np.ascontiguousarray(inputs["L0_attn_hidden_states"])
    base = {}
    for k, v in inputs.items():
        if k == "L0_attn_hidden_states":
            continue
        a = np.ascontiguousarray(v)
        if k == "L1_ffn_alpha":
            a = a.reshape(1, 1)
        elif k == "L1_ffn_W1e":
            a = a.reshape(NE * D, HM)
        elif k == "L1_ffn_W2e":
            a = a.reshape(NE * HM, D)
        elif k == "L1_ffn_b_gate":
            a = a.reshape(NE, 1)
        elif a.ndim == 1:
            a = a.reshape(1, -1)
        base[k] = a
    base["ident"] = np.eye(128, dtype=ml_dtypes.bfloat16)
    in_maps = [dict(base, x=np.ascontiguousarray(x[c])) for c in range(8)]
    res = run_bass_kernel_spmd(nc, in_maps, core_ids=list(range(8)))
    return np.stack([np.asarray(r["y"]) for r in res.results], axis=0)


# version 03
# speedup vs baseline: 1.0053x; 1.0053x over previous
import contextlib
import numpy as np
import ml_dtypes
import concourse.bass as bass
import concourse.mybir as mybir
from concourse.bass import IndirectOffsetOnAxis
from concourse.bass_utils import run_bass_kernel_spmd
from concourse.library_overlay import lower_extended_insts

F32 = mybir.dt.float32
BF16 = mybir.dt.bfloat16
U32 = mybir.dt.uint32
ALU = mybir.AluOpType
ACT = mybir.ActivationFunctionType
AX = mybir.AxisListType

T = 2048
D = 1024
H = 16
C = 64
FD = 256
NI = 2816
HM = 4096
NE = 8
NT = T // 128
NHALF = NI // 2


class Sem:

    def __init__(self, ctx, nc, name):
        self.h = ctx.enter_context(nc.semaphore(name))
        self.v = 0

    def bump(self, n):
        self.v += n
        return self.v


class Plan:

    ENGS = ("sync", "tensor", "vector", "scalar", "gpsimd")

    def __init__(self, nc, junk=None):
        self.nc = nc
        self._junk = junk
        self.ops = {e: [] for e in self.ENGS}

    def add(self, eng, fn):
        self.ops[eng].append(fn)

    def wait(self, eng, sem, val):
        if val > 0:
            self.ops[eng].append(lambda e, s=sem.h, v=val: e.wait_ge(s, v))

    def sig(self, eng, sem):
        v = sem.bump(1)
        last = self.ops[eng].pop()

        def wrapped(e, f=last, s=sem.h):
            inst = f(e)
            inst.then_inc(s, 1)
            return inst

        self.ops[eng].append(wrapped)
        return v

    def dma(self, eng, sem, out, in_, transpose=False, slow=False):
        if transpose:
            self.ops[eng].append(
                lambda e, o=out, i=in_, s=sem.h: e.dma_start_transpose(o, i).then_inc(s, 16))
        else:
            self.ops[eng].append(
                lambda e, o=out, i=in_, s=sem.h, sl=slow: e.dma_start(
                    out=o, in_=i, allow_slow_non_contiguous=sl).then_inc(s, 16))
        return sem.bump(16)

    def vsep(self, n=2):
        for _ in range(n):
            self.ops["vector"].append(
                lambda e, j=self._junk: e.memset(j[:, 0:1], 0.0))

    def emit(self):
        with self.nc.Block() as B:
            for eng in self.ENGS:
                fns = self.ops[eng]
                if not fns:
                    continue

                def body(e, fns=fns):
                    for f in fns:
                        f(e)

                getattr(B, eng)(body)


def _quant_emit(P, qs, Z, OUT, width, vV, sS, gG, use_g=True, zmark=None):
    nb = width // 16
    bm, s_, sinv, Tt, a, b, b2, sg, v1 = (qs[k] for k in
                                          ("bm", "s", "sinv", "T", "a", "b", "b2", "sg", "v1"))
    Z3 = Z.rearrange("p (a b) -> p a b", b=16)
    P.add("vector", lambda e: e.tensor_reduce(
        bm[:, 0:nb], Z3, AX.X, ALU.max, apply_absolute_value=True))
    P.add("vector", lambda e: e.tensor_scalar(
        s_[:, 0:nb], bm[:, 0:nb], 0.16666667, 1e-12, ALU.mult, ALU.max))
    P.vsep(2)
    P.add("vector", lambda e: e.reciprocal(sinv[:, 0:nb], s_[:, 0:nb]))
    if zmark is not None:
        P.wait("scalar", vV, zmark)
    P.add("scalar", lambda e: e.sign(sg[:, 0:width], Z))
    sv = P.sig("scalar", sS)
    sb = sinv[:, 0:nb].unsqueeze(2).broadcast_to([128, nb, 16])
    T3 = Tt[:, 0:width].rearrange("p (a b) -> p a b", b=16)
    V3s = v1[:, 0:width].rearrange("p (a b) -> p a b", b=16)
    P.wait("vector", sS, sv)
    P.add("vector", lambda e: e.tensor_tensor(v1[:, 0:width], Z, sg[:, 0:width], ALU.mult))
    P.add("vector", lambda e: e.tensor_tensor(T3, V3s, sb, ALU.mult))
    tv = P.sig("vector", vV)
    P.add("vector", lambda e: e.tensor_scalar(a[:, 0:width], Tt[:, 0:width], 0.25, None, ALU.is_gt))
    for thr in (0.75, 1.25, 1.75):
        P.add("vector", lambda e, t=thr: e.scalar_tensor_tensor(
            a[:, 0:width], Tt[:, 0:width], t, a[:, 0:width], ALU.is_gt, ALU.add))
    beng = "gpsimd" if use_g else "vector"
    if use_g:
        P.wait("gpsimd", vV, tv)
    P.add(beng, lambda e: e.tensor_scalar(b[:, 0:width], Tt[:, 0:width], 2.5, None, ALU.is_gt))
    P.add(beng, lambda e: e.tensor_scalar(b2[:, 0:width], Tt[:, 0:width], 3.5, None, ALU.is_gt))
    P.add(beng, lambda e: e.tensor_tensor(b[:, 0:width], b[:, 0:width], b2[:, 0:width], ALU.add))
    P.add(beng, lambda e: e.tensor_scalar(b2[:, 0:width], Tt[:, 0:width], 5.0, 2.0, ALU.is_gt, ALU.mult))
    P.add(beng, lambda e: e.tensor_tensor(b[:, 0:width], b[:, 0:width], b2[:, 0:width], ALU.add))
    if use_g:
        gv = P.sig("gpsimd", gG)
        P.wait("vector", gG, gv)
    P.add("vector", lambda e: e.scalar_tensor_tensor(
        a[:, 0:width], a[:, 0:width], 0.5, b[:, 0:width], ALU.mult, ALU.add))
    P.wait("vector", sS, sv)
    P.add("vector", lambda e: e.tensor_tensor(v1[:, 0:width], a[:, 0:width], sg[:, 0:width], ALU.mult))
    ssb = s_[:, 0:nb].unsqueeze(2).broadcast_to([128, nb, 16])
    O3 = OUT.rearrange("p (a b) -> p a b", b=16)
    V3 = v1[:, 0:width].rearrange("p (a b) -> p a b", b=16)
    P.add("vector", lambda e: e.tensor_tensor(O3, V3, ssb, ALU.mult))


def build(dbg=(), upto=99):
    nc = bass.Bass("TRN2", target_bir_lowering=False)

    def dk(name):
        return "ExternalOutput" if name in dbg else "Internal"

    xin = nc.dram_tensor("x", [T, D], F32, kind="ExternalInput").ap()
    W = {}
    for nm, shp in [
        ("L0_attn_Wq", [D, D]), ("L0_attn_Wk", [D, D]), ("L0_attn_Wv", [D, D]),
        ("L0_attn_Wo", [D, D]), ("L0_attn_Wf", [FD, C]),
        ("L0_ffn_w_gate", [NI, D]), ("L0_ffn_w_up", [NI, D]), ("L0_ffn_w_down", [D, NI]),
        ("L1_attn_Wq", [D, D]), ("L1_attn_bq", [1, D]), ("L1_attn_Wk", [D, D]),
        ("L1_attn_bk", [1, D]), ("L1_attn_Wv", [D, D]), ("L1_attn_bv", [1, D]),
        ("L1_attn_Wo", [D, D]), ("L1_attn_bo", [1, D]),
        ("L1_ffn_alpha", [1, 1]), ("L1_ffn_w_gate", [D, NE]), ("L1_ffn_b_gate", [NE, 1]),
        ("L1_ffn_W1m", [D, HM]), ("L1_ffn_b1m", [1, HM]), ("L1_ffn_W2m", [HM, D]),
        ("L1_ffn_b2m", [1, D]), ("L1_ffn_W1e", [NE * D, HM]), ("L1_ffn_b1e", [NE, HM]),
        ("L1_ffn_W2e", [NE * HM, D]), ("L1_ffn_b2e", [NE, D]),
    ]:
        W[nm] = nc.dram_tensor(nm, shp, F32, kind="ExternalInput").ap()
    y = nc.dram_tensor("y", [T, D], F32, kind="ExternalOutput").ap()
    ident = nc.dram_tensor("ident", [128, 128], BF16, kind="ExternalInput").ap()

    def scr(name, shp, dt=BF16):
        return nc.dram_tensor(name, shp, dt, kind=dk(name)).ap()

    xn0_h = scr("xn0", [T, D])
    xq_h = scr("xq", [T, D])
    wq0_h = scr("wq0", [D, D])
    wk0_h = scr("wk0", [D, D])
    wv0_h = scr("wv0", [D, D])
    wo0_h = scr("wo0", [D, D])
    wf_h = scr("wfb", [FD, C])
    attn0_h = scr("attn0", [T, D])
    wqg_h = scr("wqg", [NI, D])
    wqu_h = scr("wqu", [NI, D])
    wqd_h = scr("wqd", [D, NI])
    hq_h = scr("hq", [T, NI])
    x1_h = scr("x1", [T, D], F32)
    xn1_h = scr("xn1", [T, D])
    wq1_h = scr("wq1", [D, D])
    wk1_h = scr("wk1", [D, D])
    wv1_h = scr("wv1", [D, D])
    wo1_h = scr("wo1", [D, D])
    w1m_h = scr("w1m", [D, HM])
    w2m_h = scr("w2m", [HM, D])
    wgate_h = scr("wgate", [D, NE])
    attn1_h = scr("attn1", [T, D])
    mean_h = scr("meanh", [1, D], F32)
    scal_h = scr("scalh", [1, 16], F32)
    w1g_h = [scr(f"w1g{k}", [D, HM]) for k in range(2)]
    w2g_h = [scr(f"w2g{k}", [HM, D]) for k in range(2)]
    b1g_h = [scr(f"b1g{k}", [1, HM], F32) for k in range(2)]
    b2g_h = [scr(f"b2g{k}", [1, D], F32) for k in range(2)]
    ffn1_h = scr("ffn1", [D, T])
    lnst_h = scr("lnst", [128, 64], F32)
    _dall = set(dbg)
    if "dqfm" in dbg:
        dbg = set(dbg) | {"dkfm", "dvtok", "dkv", "dwfT"}
    if "dQ" in dbg:
        dbg = set(dbg) | {"dK", "dV"}
    if "dSA" in dbg:
        dbg = set(dbg) | {"dSB"}
    dbg_qfm = scr("dqfm", [128, 16384])
    dbg_kfm = scr("dkfm", [128, 16384])
    dbg_vtok = scr("dvtok", [128, 16384])
    dbg_kv = scr("dkv", [128, 2080])
    dbg_wfT = scr("dwfT", [128, 256])
    dbg_h = scr("dh", [T, NI], F32)
    lnst4_h = scr("lnst4", [128, 64], F32)
    dbg_Q = scr("dQ", [T, D], F32)
    dbg_K = scr("dK", [T, D], F32)
    dbg_V = scr("dV", [T, D], F32)
    dbg_SA = scr("dSA", [128, 64, 32], F32)
    dbg_SB = scr("dSB", [128, 32, 32], F32)
    dbg_hfm = scr("dhfm", [128, 16384])
    dbg_exp = scr("dexp", [128, 4096], F32)
    dbg_ps = scr("dps", [128, 4096], F32)
    dbg_g = scr("dg", [T, NI], F32)

    ctx = contextlib.ExitStack()
    with ctx:
        qS = Sem(ctx, nc, "qS")
        qG = Sem(ctx, nc, "qG")
        tM = Sem(ctx, nc, "tM")
        vV = Sem(ctx, nc, "vV")
        sS = Sem(ctx, nc, "sS")
        gG = Sem(ctx, nc, "gG")

        def sb(n, shp, dt=F32, **kw):
            return ctx.enter_context(nc.sbuf_tensor(n, shp, dt, **kw))

        m0 = sb("m0", [128, NT])
        r0 = sb("r0", [128, NT])
        m1 = sb("m1", [128, NT])
        r1 = sb("r1", [128, NT])
        sum_a = sb("sum_a", [128, NT])
        sum_b = sb("sum_b", [128, NT])
        junk = sb("junk", [128, 1024])
        vsept = sb("vsept", [128, 1])
        onesbf = sb("onesbf", [128, 1], BF16)
        screp = sb("screp", [128, 16])
        psA = ctx.enter_context(nc.psum_tensor("psA", [128, 4096], F32))
        xbuf = None

        def bank(i, w=512):
            return psA[:, i * 512: i * 512 + w]

        def bslice(i, off, w):
            return psA[:, i * 512 + off: i * 512 + off + w]

        P = Plan(nc, vsept)
        P.add("vector", lambda e: e.memset(onesbf[:], 1.0))
        for src, dst in [(W["L0_attn_Wq"], wq0_h), (W["L0_attn_Wk"], wk0_h),
                         (W["L0_attn_Wv"], wv0_h), (W["L0_attn_Wo"], wo0_h),
                         (W["L0_attn_Wf"], wf_h),
                         (W["L1_attn_Wq"], wq1_h), (W["L1_attn_Wk"], wk1_h),
                         (W["L1_attn_Wv"], wv1_h), (W["L1_attn_Wo"], wo1_h),
                         (W["L1_ffn_W1m"], w1m_h), (W["L1_ffn_W2m"], w2m_h),
                         (W["L1_ffn_w_gate"], wgate_h)]:
            P.add("gpsimd", lambda e, s=src, d=dst: e.dma_start(out=d, in_=s).then_inc(qG.h, 16))
            qG.bump(16)
        castdone = qG.v
        with nc.sbuf_tensor("xt0", [128, 2, 1024], F32) as xt0, \
             nc.sbuf_tensor("jsq", [128, 2, 1024], F32) as jsq:
            smarks = []
            vmarks0 = []
            for i in range(NT):
                pp = i % 2
                if i >= 2:
                    P.wait("sync", vV, vmarks0[i - 2])
                ldv = P.dma("sync", qS, xt0[:, pp], xin[i * 128:(i + 1) * 128, :])
                P.wait("scalar", qS, ldv)
                if i >= 2:
                    P.wait("scalar", vV, vmarks0[i - 2])
                P.add("scalar", lambda e, pp=pp: e.activation(jsq[:, pp], xt0[:, pp], ACT.Square))
                sm = P.sig("scalar", sS)
                smarks.append(sm)
                P.wait("vector", sS, sm)
                P.add("vector", lambda e, pp=pp, i=i: e.tensor_reduce(
                    sum_a[:, i:i + 1], xt0[:, pp], AX.X, ALU.add))
                P.add("vector", lambda e, pp=pp, i=i: e.tensor_reduce(
                    sum_b[:, i:i + 1], jsq[:, pp], AX.X, ALU.add))
                vmarks0.append(P.sig("vector", vV))
            P.add("vector", lambda e: e.tensor_scalar(m0[:], sum_a[:], 1.0 / D, None, ALU.mult))
            P.add("vector", lambda e: e.tensor_scalar(junk[:, 0:NT], sum_b[:], 1.0 / D, None, ALU.mult))
            P.vsep(3)
            P.add("vector", lambda e: e.tensor_tensor(r0[:], m0[:], m0[:], ALU.mult))
            P.vsep(3)
            P.add("vector", lambda e: e.tensor_tensor(junk[:, 0:NT], junk[:, 0:NT], r0[:], ALU.subtract))
            P.vsep(3)
            P.add("vector", lambda e: e.tensor_scalar(junk[:, 0:NT], junk[:, 0:NT], 1e-5, None, ALU.add))
            vv = P.sig("vector", vV)
            P.wait("scalar", vV, vv)
            P.add("scalar", lambda e: e.activation(junk[:, NT:2 * NT], junk[:, 0:NT], ACT.Sqrt))
            sv2 = P.sig("scalar", sS)
            P.wait("vector", sS, sv2)
            P.add("vector", lambda e: e.reciprocal(r0[:], junk[:, NT:2 * NT]))
            P.emit()

        _es = contextlib.ExitStack()
        with _es:
            xt1 = _es.enter_context(nc.sbuf_tensor("xt1", [128, 2, 1024], F32))
            xnf = _es.enter_context(nc.sbuf_tensor("xnf", [128, 1024], F32))
            xnbf = _es.enter_context(nc.sbuf_tensor("xnbf", [128, 2, 1024], BF16))
            xqbf = _es.enter_context(nc.sbuf_tensor("xqbf", [128, 2, 1024], BF16))
            q_bm = _es.enter_context(nc.sbuf_tensor("q_bm", [128, 64], F32))
            q_s = _es.enter_context(nc.sbuf_tensor("q_s", [128, 64], F32))
            q_sinv = _es.enter_context(nc.sbuf_tensor("q_sinv", [128, 64], F32))
            q_T = _es.enter_context(nc.sbuf_tensor("q_T", [128, 1024], F32))
            q_a = _es.enter_context(nc.sbuf_tensor("q_a", [128, 1024], F32))
            q_b = _es.enter_context(nc.sbuf_tensor("q_b", [128, 1024], F32))
            q_b2 = _es.enter_context(nc.sbuf_tensor("q_b2", [128, 1024], F32))
            q_sg = _es.enter_context(nc.sbuf_tensor("q_sg", [128, 1024], F32))
            q_v1 = _es.enter_context(nc.sbuf_tensor("q_v1", [128, 1024], F32))
            qs = dict(bm=q_bm, s=q_s, sinv=q_sinv, T=q_T, a=q_a, b=q_b, b2=q_b2,
                      sg=q_sg, v1=q_v1)
            P = Plan(nc, vsept)
            vmarks = []
            csm1 = []
            for i in range(NT):
                pp = i % 2
                if i >= 2:
                    P.wait("sync", vV, vmarks[i - 2])
                ldv = P.dma("sync", qS, xt1[:, pp], xin[i * 128:(i + 1) * 128, :])
                P.wait("vector", qS, ldv)
                if i >= 1:
                    P.wait("vector", sS, csm1[i - 1])
                rb = r0[:, i:i + 1].broadcast_to([128, 1024])
                P.add("vector", lambda e, pp=pp, i=i, rb=rb: e.scalar_tensor_tensor(
                    xnf[:], xt1[:, pp], m0[:, i:i + 1], rb, ALU.subtract, ALU.mult))
                xv = P.sig("vector", vV)
                P.wait("scalar", vV, xv)
                P.add("scalar", lambda e, pp=pp: e.copy(xnbf[:, pp], xnf[:]))
                cs = P.sig("scalar", sS)
                csm1.append(cs)
                P.wait("sync", sS, cs)
                P.dma("sync", qS, xn0_h[i * 128:(i + 1) * 128, :], xnbf[:, pp])
                _quant_emit(P, qs, xnf[:], xqbf[:, pp], 1024, vV, sS, gG, use_g=True)
                qv = P.sig("vector", vV)
                vmarks.append(qv)
                P.wait("sync", vV, qv)
                P.dma("sync", qS, xq_h[i * 128:(i + 1) * 128, :], xqbf[:, pp])
            P.dma("sync", qS, lnst_h[:, 0:NT], m0[:])
            P.dma("sync", qS, lnst_h[:, 16:16 + NT], r0[:])
            P.dma("sync", qS, lnst_h[:, 32:48], sum_a[:, 0:16])
            P.emit()

        if upto <= 1:
            return nc

        _es = contextlib.ExitStack()
        with _es:
            xn0T = _es.enter_context(nc.sbuf_tensor("xn0T", [128, 16384], BF16, is_transpose_dest=True))
            qfm = _es.enter_context(nc.sbuf_tensor("qfm", [128, 16384], BF16))
            kfm = _es.enter_context(nc.sbuf_tensor("kfm", [128, 16384], BF16))
            vtok = _es.enter_context(nc.sbuf_tensor("vtok", [128, 16384], BF16))
            a0sb = _es.enter_context(nc.sbuf_tensor("a0sb", [128, 16384], BF16))
            wT = _es.enter_context(nc.sbuf_tensor("wT", [128, 8192], BF16, is_transpose_dest=True))
            wfT = _es.enter_context(nc.sbuf_tensor("wfT", [128, 256], BF16, is_transpose_dest=True))
            kvsb = _es.enter_context(nc.sbuf_tensor("kvsb", [128, 2080], BF16))
            kfb = _es.enter_context(nc.sbuf_tensor("kfb", [128, 2, 256], BF16))
            etmp = _es.enter_context(nc.sbuf_tensor("etmp", [128, 512], F32))
            etmp2 = _es.enter_context(nc.sbuf_tensor("etmp2", [128, 512], F32))
            qff = _es.enter_context(nc.sbuf_tensor("qff", [128, 2, 4096], BF16))
            nrm = _es.enter_context(nc.sbuf_tensor("nrm", [128, 2], F32))
            P = Plan(nc, vsept)
            P.wait("gpsimd", qG, castdone)
            P.wait("sync", qG, castdone)
            for j in range(8):
                P.dma("sync", qS, xn0T[:, j * 2048:(j + 1) * 2048],
                      xn0_h[:, j * 128:(j + 1) * 128], transpose=True)
            idl = P.dma("sync", qS, junk[:, 0:64].bitcast(BF16)[:, 0:128], ident)
            wfl = P.dma("sync", qS, kfb[:, 0, 0:128].rearrange("p (c f) -> p c f", f=64),
                        wf_h.rearrange("(c p) f -> p c f", p=128))
            P.wait("tensor", qS, max(idl, wfl))
            for cc in range(2):
                for bb in range(2):
                    P.add("tensor", lambda e, cc=cc, bb=bb: e.transpose(
                        psA[bb * 64:bb * 64 + 64,
                            3584 + cc * 64:3584 + cc * 64 + 64].bitcast(BF16),
                        kfb[:, 0, cc * 64:(cc + 1) * 64],
                        junk[:, 0:64].bitcast(BF16)[:, 0:128]))
            trm = P.sig("tensor", tM)
            P.wait("scalar", tM, trm)
            for cc in range(2):
                P.add("scalar", lambda e, cc=cc: e.copy(
                    wfT[:, cc * 128:(cc + 1) * 128],
                    psA[:, 3584 + cc * 64:3584 + cc * 64 + 64].bitcast(BF16)))
            wfev = P.sig("scalar", sS)
            P.wait("tensor", sS, wfev)

            def proj_fm(wsrc_h, dest):
                tuse0 = tM.v
                P.wait("sync", tM, tuse0)
                for j in range(8):
                    P.dma("sync", qS, wT[:, j * 1024:(j + 1) * 1024],
                          wsrc_h[:, j * 128:(j + 1) * 128], transpose=True)
                wv_ = qS.v
                P.wait("tensor", qS, wv_)
                for ci in range(8):
                    for j in range(8):
                        st = wT[:, j * 1024 + ci * 128: j * 1024 + ci * 128 + 128]
                        for tc in range(4):
                            rhs = xn0T[:, j * 2048 + tc * 512: j * 2048 + tc * 512 + 512]
                            P.add("tensor", lambda e, st=st, rhs=rhs, tc=tc, j=j: e.matmul(
                                bank(tc), st, rhs, start=(j == 0), stop=(j == 7)))
                    mv = P.sig("tensor", tM)
                    P.wait("scalar", tM, mv)
                    for tc in range(4):
                        dst = dest[:, ci * 2048 + tc * 512: ci * 2048 + tc * 512 + 512]
                        P.add("scalar", lambda e, dst=dst, tc=tc: e.copy(dst, bank(tc)))
                    ev = P.sig("scalar", sS)
                    P.wait("tensor", sS, ev)

            proj_fm(wq0_h, qfm)
            proj_fm(wk0_h, kfm)
            P.wait("sync", tM, tM.v)
            for j in range(8):
                P.dma("sync", qS, wT[:, j * 1024:(j + 1) * 1024],
                      wv0_h[:, j * 128:(j + 1) * 128], transpose=True)
            P.wait("tensor", qS, qS.v)
            ev_marks = []
            for tc in range(NT):
                b0 = (tc % 2) * 2
                if tc >= 2:
                    P.wait("tensor", sS, ev_marks[tc - 2])
                for j in range(8):
                    st = xn0T[:, j * 2048 + tc * 128: j * 2048 + tc * 128 + 128]
                    for hf in range(2):
                        rhs = wT[:, j * 1024 + hf * 512: j * 1024 + hf * 512 + 512]
                        P.add("tensor", lambda e, st=st, rhs=rhs, b=b0 + hf, j=j: e.matmul(
                            bank(b), st, rhs, start=(j == 0), stop=(j == 7)))
                mv = P.sig("tensor", tM)
                P.wait("scalar", tM, mv)
                for hf in range(2):
                    dst = vtok[:, tc * 1024 + hf * 512: tc * 1024 + hf * 512 + 512]
                    P.add("scalar", lambda e, dst=dst, b=b0 + hf: e.copy(dst, bank(b)))
                ev_marks.append(P.sig("scalar", sS))
            P.wait("tensor", sS, ev_marks[-1])

            def kvslot(h, half):
                idx = h * 2 + half
                return idx // 7, (idx % 7) * 65

            kf_ev = {}
            for tc in range(NT):
                for h in range(H):
                    u = tc * H + h
                    pb = 5 + (h % 2)
                    if u >= 2:
                        P.wait("tensor", vV, kf_ev[u - 2])
                    stat = kfm[(h % 2) * 64:(h % 2) * 64 + 64,
                               (h // 2) * 2048 + tc * 128:(h // 2) * 2048 + tc * 128 + 128]
                    hb0 = (h % 2) * 64
                    P.add("tensor", lambda e, st=stat, pb=pb, hb0=hb0: e.matmul(
                        bank(pb, 256), st, wfT[hb0:hb0 + 64, :], start=True, stop=True))
                    mv = P.sig("tensor", tM)
                    P.wait("vector", tM, mv)
                    P.add("vector", lambda e, pb=pb: e.tensor_scalar(
                        etmp[:, 0:256], bank(pb, 256), 0.0, None, ALU.min))
                    ev1 = P.sig("vector", vV)
                    P.wait("scalar", vV, ev1)
                    P.add("scalar", lambda e: e.activation(etmp2[:, 0:256], etmp[:, 0:256], ACT.Exp))
                    sv = P.sig("scalar", sS)
                    P.wait("vector", sS, sv)
                    P.add("vector", lambda e, pb=pb, h=h: e.scalar_tensor_tensor(
                        kfb[:, h % 2], bank(pb, 256), 0.0, etmp2[:, 0:256], ALU.max, ALU.add))
                    kf_ev[u] = P.sig("vector", vV)
                    P.wait("tensor", vV, kf_ev[u])
                    for half in range(2):
                        bk, off = kvslot(h, half)
                        idx = h * 2 + half
                        st0 = (tc == 0) and (idx % 7 == 0)
                        stat2 = kfb[:, h % 2][:, half * 128: half * 128 + 128]
                        rhsv = vtok[:, tc * 1024 + h * 64: tc * 1024 + h * 64 + 64]
                        P.add("tensor", lambda e, st=stat2, r=rhsv, bk=bk, off=off, tc=tc, st0=st0: e.matmul(
                            bslice(bk, off, 64), st, r, start=st0, stop=(tc == NT - 1)))
                        P.add("tensor", lambda e, st=stat2, bk=bk, off=off, tc=tc: e.matmul(
                            bslice(bk, off + 64, 1), st, onesbf[:], start=False, stop=(tc == NT - 1)))
            kv_mm_done = P.sig("tensor", tM)
            P.wait("scalar", tM, kv_mm_done)
            for h in range(H):
                for half in range(2):
                    bk, off = kvslot(h, half)
                    idx = h * 2 + half
                    P.add("scalar", lambda e, bk=bk, off=off, idx=idx: e.copy(
                        kvsb[:, idx * 65: idx * 65 + 65], bslice(bk, off, 65)))
            kv_ev_done = P.sig("scalar", sS)

            prev_a8_last = {}
            for h in range(H):
                hb = h % 2
                if h >= 2:
                    P.wait("tensor", vV, prev_a8_last[h - 2])
                for half in range(2):
                    pb = 5 + half
                    hb0 = (h % 2) * 64
                    stat = wfT[hb0:hb0 + 64, half * 128: half * 128 + 128]
                    for tc4 in range(4):
                        rhs = qfm[(h % 2) * 64:(h % 2) * 64 + 64,
                                  (h // 2) * 2048 + tc4 * 512:(h // 2) * 2048 + tc4 * 512 + 512]
                        P.add("tensor", lambda e, st=stat, r=rhs, pb=pb: e.matmul(
                            bank(pb), st, r, start=True, stop=True))
                        mv = P.sig("tensor", tM)
                        P.wait("vector", tM, mv)
                        P.add("vector", lambda e, pb=pb: e.tensor_scalar(
                            etmp[:], bank(pb), 0.0, None, ALU.min))
                        ev1 = P.sig("vector", vV)
                        P.wait("scalar", vV, ev1)
                        P.add("scalar", lambda e: e.activation(etmp2[:], etmp[:], ACT.Exp))
                        sv = P.sig("scalar", sS)
                        P.wait("vector", sS, sv)
                        dst = qff[:, hb][:, half * 2048 + tc4 * 512: half * 2048 + tc4 * 512 + 512]
                        P.add("vector", lambda e, pb=pb, dst=dst: e.scalar_tensor_tensor(
                            dst, bank(pb), 0.0, etmp2[:], ALU.max, ALU.add))
                        ev2 = P.sig("vector", vV)
                        P.wait("tensor", vV, ev2)
                if h == 0:
                    P.wait("tensor", sS, kv_ev_done)
                a8m = []
                for tc in range(NT):
                    pb = 2 + tc % 2
                    if tc >= 2:
                        P.wait("tensor", vV, a8m[tc - 2])
                    for half in range(2):
                        stat = qff[:, hb][:, half * 2048 + tc * 128: half * 2048 + tc * 128 + 128]
                        rhs = kvsb[:, (h * 2 + half) * 65: (h * 2 + half) * 65 + 65]
                        P.add("tensor", lambda e, st=stat, r=rhs, pb=pb, half=half: e.matmul(
                            bslice(pb, 0, 65), st, r, start=(half == 0), stop=(half == 1)))
                    mv = P.sig("tensor", tM)
                    P.wait("vector", tM, mv)
                    P.add("vector", lambda e, pb=pb: e.tensor_scalar(
                        nrm[:, 0:1], bslice(pb, 64, 1), 1e-6, None, ALU.add))
                    P.vsep(2)
                    P.add("vector", lambda e: e.reciprocal(nrm[:, 1:2], nrm[:, 0:1]))
                    P.vsep(2)
                    dst = a0sb[:, tc * 1024 + h * 64: tc * 1024 + h * 64 + 64]
                    P.add("vector", lambda e, pb=pb, dst=dst: e.tensor_scalar(
                        dst, bslice(pb, 0, 64), nrm[:, 1:2], None, ALU.mult))
                    a8m.append(P.sig("vector", vV))
                prev_a8_last[h] = a8m[-1]
            P.wait("sync", vV, vV.v)
            P.dma("sync", qS, attn0_h.rearrange("(tc p) d -> p tc d", p=128),
                  a0sb[:].rearrange("p (tc d) -> p tc d", d=1024))
            if "dqfm" in dbg or "dwfT" in dbg:
                P.dma("sync", qS, lnst_h[:, 0:64], junk[:, 100:164])
                P.dma("sync", qS, dbg_qfm, qfm[:])
                P.dma("sync", qS, dbg_kfm, kfm[:])
                P.dma("sync", qS, dbg_vtok, vtok[:])
                P.dma("sync", qS, dbg_kv, kvsb[:])
                P.dma("sync", qS, dbg_wfT, wfT[:])
            P.emit()

        if upto <= 2:
            return nc

        _es = contextlib.ExitStack()
        with _es:
            wqf32 = _es.enter_context(nc.sbuf_tensor("wqf32", [128, 2, NI], F32))
            wqbf = _es.enter_context(nc.sbuf_tensor("wqbf", [128, 2, NI], BF16))
            f_bm = _es.enter_context(nc.sbuf_tensor("f_bm", [128, 176], F32))
            f_s = _es.enter_context(nc.sbuf_tensor("f_s", [128, 176], F32))
            f_sinv = _es.enter_context(nc.sbuf_tensor("f_sinv", [128, 176], F32))
            f_T = _es.enter_context(nc.sbuf_tensor("f_T", [128, NI], F32))
            f_a = _es.enter_context(nc.sbuf_tensor("f_a", [128, NI], F32))
            f_b = _es.enter_context(nc.sbuf_tensor("f_b", [128, NI], F32))
            f_b2 = _es.enter_context(nc.sbuf_tensor("f_b2", [128, NI], F32))
            f_sg = _es.enter_context(nc.sbuf_tensor("f_sg", [128, NI], F32))
            f_v1 = _es.enter_context(nc.sbuf_tensor("f_v1", [128, NI], F32))
            fqs = dict(bm=f_bm, s=f_s, sinv=f_sinv, T=f_T, a=f_a, b=f_b, b2=f_b2,
                       sg=f_sg, v1=f_v1)
            P = Plan(nc, vsept)
            vm, sm = [], []
            it = 0
            for (wsrc, wdst, nrows, width) in [
                    (W["L0_ffn_w_gate"], wqg_h, NI, D),
                    (W["L0_ffn_w_up"], wqu_h, NI, D),
                    (W["L0_ffn_w_down"], wqd_h, D, NI)]:
                for cchunk in range(nrows // 128):
                    pp = it % 2
                    if it >= 2:
                        P.wait("sync", vV, vm[it - 2])
                        P.wait("sync", sS, sm[it - 2])
                    ldv = P.dma("sync", qS, wqf32[:, pp][:, 0:width],
                                wsrc[cchunk * 128:(cchunk + 1) * 128, :])
                    P.wait("vector", qS, ldv)
                    P.wait("scalar", qS, ldv)
                    _quant_emit(P, fqs, wqf32[:, pp][:, 0:width], wqbf[:, pp][:, 0:width],
                                width, vV, sS, gG, use_g=True)
                    qv = P.sig("vector", vV)
                    vm.append(qv)
                    sm.append(sS.v)
                    P.wait("sync", vV, qv)
                    P.dma("sync", qS, wdst[cchunk * 128:(cchunk + 1) * 128, :],
                          wqbf[:, pp][:, 0:width])
                    it += 1
            P.emit()

        _es = contextlib.ExitStack()
        with _es:
            xqT = _es.enter_context(nc.sbuf_tensor("xqT", [128, 16384], BF16, is_transpose_dest=True))
            wgT = _es.enter_context(nc.sbuf_tensor("wgT", [128, 8, NHALF], BF16, is_transpose_dest=True))
            wuT = _es.enter_context(nc.sbuf_tensor("wuT", [128, 8, NHALF], BF16, is_transpose_dest=True))
            gsil = _es.enter_context(nc.sbuf_tensor("gsil", [128, NHALF], F32))
            hfull = _es.enter_context(nc.sbuf_tensor("hfull", [128, NHALF], F32))
            hqbf = _es.enter_context(nc.sbuf_tensor("hqbf", [128, NHALF], BF16))
            h_bm = _es.enter_context(nc.sbuf_tensor("h_bm", [128, 88], F32))
            h_s = _es.enter_context(nc.sbuf_tensor("h_s", [128, 88], F32))
            h_sinv = _es.enter_context(nc.sbuf_tensor("h_sinv", [128, 88], F32))
            h_T = _es.enter_context(nc.sbuf_tensor("h_T", [128, NHALF], F32))
            h_a = _es.enter_context(nc.sbuf_tensor("h_a", [128, NHALF], F32))
            h_b = _es.enter_context(nc.sbuf_tensor("h_b", [128, NHALF], F32))
            h_b2 = _es.enter_context(nc.sbuf_tensor("h_b2", [128, NHALF], F32))
            h_sg = _es.enter_context(nc.sbuf_tensor("h_sg", [128, NHALF], F32))
            h_v1 = _es.enter_context(nc.sbuf_tensor("h_v1", [128, NHALF], F32))
            hqs = dict(bm=h_bm, s=h_s, sinv=h_sinv, T=h_T, a=h_a, b=h_b, b2=h_b2,
                       sg=h_sg, v1=h_v1)
            P = Plan(nc, vsept)
            for j in range(8):
                P.dma("sync", qS, xqT[:, j * 2048:(j + 1) * 2048],
                      xq_h[:, j * 128:(j + 1) * 128], transpose=True)
            ncs = [(0, 512), (512, 512), (1024, 384)]
            for nh in range(2):
                P.wait("sync", tM, tM.v)
                for k in range(8):
                    P.dma("sync", qS, wgT[:, k],
                          wqg_h[nh * NHALF:(nh + 1) * NHALF, k * 128:(k + 1) * 128], transpose=True)
                    P.dma("sync", qS, wuT[:, k],
                          wqu_h[nh * NHALF:(nh + 1) * NHALF, k * 128:(k + 1) * 128], transpose=True)
                wload = qS.v
                P.wait("tensor", qS, wload)
                for ts_ in range(16):
                    if nh > 0 or ts_ > 0:
                        P.wait("tensor", vV, vV.v)
                    for k in range(8):
                        st = xqT[:, k * 2048 + ts_ * 128: k * 2048 + ts_ * 128 + 128]
                        for ib, (off, wdt) in enumerate(ncs):
                            P.add("tensor", lambda e, st=st, k=k, ib=ib, off=off, wdt=wdt: e.matmul(
                                bank(ib, wdt), st, wgT[:, k][:, off:off + wdt],
                                start=(k == 0), stop=(k == 7)))
                            P.add("tensor", lambda e, st=st, k=k, ib=ib, off=off, wdt=wdt: e.matmul(
                                bank(3 + ib, wdt), st, wuT[:, k][:, off:off + wdt],
                                start=(k == 0), stop=(k == 7)))
                    mv = P.sig("tensor", tM)
                    P.wait("scalar", tM, mv)
                    P.add("scalar", lambda e: e.activation(gsil[:], psA[:, 0:NHALF], ACT.Silu))
                    sv = P.sig("scalar", sS)
                    P.wait("vector", sS, sv)
                    P.wait("vector", tM, mv)
                    P.add("vector", lambda e: e.tensor_tensor(
                        hfull[:], gsil[:], psA[:, 1536:1536 + NHALF], ALU.mult))
                    hm_ = P.sig("vector", vV)
                    P.vsep(2)
                    if "dh" in dbg:
                        P.wait("sync", vV, hm_)
                        P.dma("sync", qS,
                              dbg_h[ts_ * 128:(ts_ + 1) * 128, nh * NHALF:(nh + 1) * NHALF],
                              hfull[:])
                        P.dma("sync", qS,
                              dbg_g[ts_ * 128:(ts_ + 1) * 128, nh * NHALF:(nh + 1) * NHALF],
                              gsil[:])
                    _quant_emit(P, hqs, hfull[:], hqbf[:], NHALF, vV, sS, gG, use_g=True,
                                zmark=hm_)
                    qv = P.sig("vector", vV)
                    P.wait("sync", vV, qv)
                    P.dma("sync", qS,
                          hq_h[ts_ * 128:(ts_ + 1) * 128, nh * NHALF:(nh + 1) * NHALF], hqbf[:])
            P.emit()

        xbuf = sb("xbuf", [128, 16384])
        _es = contextlib.ExitStack()
        with _es:
            wdT = _es.enter_context(nc.sbuf_tensor("wdT", [128, 22, 1024], BF16, is_transpose_dest=True))
            hqT = _es.enter_context(nc.sbuf_tensor("hqT", [128, 22, 512], BF16, is_transpose_dest=True))
            xorig = _es.enter_context(nc.sbuf_tensor("xorig", [128, 2, 1024], F32))
            a0T = _es.enter_context(nc.sbuf_tensor("a0T", [128, 16384], BF16, is_transpose_dest=True))
            wo0T = _es.enter_context(nc.sbuf_tensor("wo0T", [128, 8, 1024], BF16, is_transpose_dest=True))
            P = Plan(nc, vsept)
            for j in range(8):
                P.dma("sync", qS, a0T[:, j * 2048:(j + 1) * 2048],
                      attn0_h[:, j * 128:(j + 1) * 128], transpose=True)
                P.dma("sync", qS, wo0T[:, j], wo0_h[:, j * 128:(j + 1) * 128], transpose=True)
            for ncc in range(22):
                P.dma("sync", qS, wdT[:, ncc], wqd_h[:, ncc * 128:(ncc + 1) * 128], transpose=True)
            rmarks = []
            for tc in range(4):
                P.wait("sync", tM, tM.v)
                for ncc in range(22):
                    P.dma("sync", qS, hqT[:, ncc],
                          hq_h[tc * 512:(tc + 1) * 512, ncc * 128:(ncc + 1) * 128], transpose=True)
                hload = qS.v
                P.wait("tensor", qS, hload)
                for ts_ in range(4):
                    i_tile = tc * 4 + ts_
                    pp = i_tile % 2
                    pb = (i_tile % 2) * 2
                    if i_tile >= 2:
                        P.wait("tensor", vV, rmarks[i_tile - 2])
                        P.wait("sync", vV, rmarks[i_tile - 2])
                    for ncc in range(22):
                        st = hqT[:, ncc][:, ts_ * 128: ts_ * 128 + 128]
                        for hf in range(2):
                            P.add("tensor", lambda e, st=st, ncc=ncc, hf=hf, pb=pb: e.matmul(
                                bank(pb + hf), st, wdT[:, ncc][:, hf * 512:(hf + 1) * 512],
                                start=(ncc == 0), stop=(ncc == 21)))
                    ob = 4 + (i_tile % 2) * 2
                    for j in range(8):
                        sto = a0T[:, j * 2048 + i_tile * 128: j * 2048 + i_tile * 128 + 128]
                        for hf in range(2):
                            P.add("tensor", lambda e, sto=sto, j=j, hf=hf, ob=ob: e.matmul(
                                bank(ob + hf), sto, wo0T[:, j][:, hf * 512:(hf + 1) * 512],
                                start=(j == 0), stop=(j == 7)))
                    mv = P.sig("tensor", tM)
                    l1 = P.dma("sync", qS, xorig[:, pp], xin[i_tile * 128:(i_tile + 1) * 128, :])
                    P.wait("vector", qS, l1)
                    P.wait("vector", tM, mv)
                    xb = xbuf[:, i_tile * 1024:(i_tile + 1) * 1024]
                    for hf in range(2):
                        P.add("vector", lambda e, xb=xb, hf=hf, pp=pp, ob=ob: e.tensor_tensor(
                            xb[:, hf * 512:(hf + 1) * 512],
                            xorig[:, pp][:, hf * 512:(hf + 1) * 512],
                            bank(ob + hf), ALU.add))
                        P.add("vector", lambda e, xb=xb, hf=hf, pb=pb: e.tensor_tensor(
                            xb[:, hf * 512:(hf + 1) * 512], xb[:, hf * 512:(hf + 1) * 512],
                            bank(pb + hf), ALU.add))
                    rv = P.sig("vector", vV)
                    rmarks.append(rv)
                    P.wait("sync", vV, rv)
                    P.dma("sync", qS, x1_h[i_tile * 128:(i_tile + 1) * 128, :], xb)
            P.emit()

        if upto <= 3:
            return nc

        _es = contextlib.ExitStack()
        with _es:
            xn1f = _es.enter_context(nc.sbuf_tensor("xn1f", [128, 1024], F32))
            xn1b = _es.enter_context(nc.sbuf_tensor("xn1b", [128, 2, 1024], BF16))
            mrow = _es.enter_context(nc.sbuf_tensor("mrow", [1, 1024], F32))
            meanT = _es.enter_context(nc.sbuf_tensor("meanT", [128, 8], BF16))
            meanTf = _es.enter_context(nc.sbuf_tensor("meanTf", [128, 8], F32))
            wg8 = _es.enter_context(nc.sbuf_tensor("wg8", [128, 8], BF16))
            bg8 = _es.enter_context(nc.sbuf_tensor("bg8", [8, 1], F32))
            lg = _es.enter_context(nc.sbuf_tensor("lg", [8, 1], F32))
            lrow = _es.enter_context(nc.sbuf_tensor("lrow", [1, 8], F32))
            srt = _es.enter_context(nc.sbuf_tensor("srt", [1, 8], F32))
            io8 = _es.enter_context(nc.sbuf_tensor("io8", [1, 8], F32))
            tt8 = _es.enter_context(nc.sbuf_tensor("tt8", [1, 8], F32))
            tt8b = _es.enter_context(nc.sbuf_tensor("tt8b", [1, 8], F32))
            sc16 = _es.enter_context(nc.sbuf_tensor("sc16", [1, 16], F32))
            alph = _es.enter_context(nc.sbuf_tensor("alph", [1, 1], F32))
            iotaJ = _es.enter_context(nc.sbuf_tensor("iotaJ", [128, 1], F32))
            idxf = _es.enter_context(nc.sbuf_tensor("idxf", [128, 1], F32))
            idxu = _es.enter_context(nc.sbuf_tensor("idxu", [128, 1], U32))
            gbuf = _es.enter_context(nc.sbuf_tensor("gbuf", [128, HM], BF16))
            bgath = _es.enter_context(nc.sbuf_tensor("bgath", [1, HM], F32))
            bgath2 = _es.enter_context(nc.sbuf_tensor("bgath2", [128, 32], F32))
            jsq = _es.enter_context(nc.sbuf_tensor("jsq4", [128, 2, 1024], F32))
            P = Plan(nc, vsept)
            vst = []
            for i in range(NT):
                pp = i % 2
                xb = xbuf[:, i * 1024:(i + 1) * 1024]
                if i >= 2:
                    P.wait("scalar", vV, vst[i - 2])
                P.add("scalar", lambda e, xb=xb, pp=pp: e.activation(jsq[:, pp], xb, ACT.Square))
                sm = P.sig("scalar", sS)
                P.wait("vector", sS, sm)
                P.add("vector", lambda e, xb=xb, i=i: e.tensor_reduce(
                    sum_a[:, i:i + 1], xb, AX.X, ALU.add))
                P.add("vector", lambda e, pp=pp, i=i: e.tensor_reduce(
                    sum_b[:, i:i + 1], jsq[:, pp], AX.X, ALU.add))
                vst.append(P.sig("vector", vV))
            P.add("vector", lambda e: e.tensor_scalar(m1[:], sum_a[:], 1.0 / D, None, ALU.mult))
            P.add("vector", lambda e: e.tensor_scalar(junk[:, 0:NT], sum_b[:], 1.0 / D, None, ALU.mult))
            P.vsep(3)
            P.add("vector", lambda e: e.tensor_tensor(r1[:], m1[:], m1[:], ALU.mult))
            P.vsep(3)
            P.add("vector", lambda e: e.tensor_tensor(junk[:, 0:NT], junk[:, 0:NT], r1[:], ALU.subtract))
            P.vsep(3)
            P.add("vector", lambda e: e.tensor_scalar(junk[:, 0:NT], junk[:, 0:NT], 1e-5, None, ALU.add))
            vv = P.sig("vector", vV)
            P.wait("scalar", vV, vv)
            P.add("scalar", lambda e: e.activation(junk[:, NT:2 * NT], junk[:, 0:NT], ACT.Sqrt))
            sv2 = P.sig("scalar", sS)
            P.wait("vector", sS, sv2)
            P.add("vector", lambda e: e.reciprocal(r1[:], junk[:, NT:2 * NT]))
            rstat = P.sig("vector", vV)
            if "lnst4" in dbg:
                P.wait("sync", vV, rstat)
                P.dma("sync", qS, lnst4_h[:, 0:16], m1[:])
                P.dma("sync", qS, lnst4_h[:, 16:32], r1[:])
                P.dma("sync", qS, lnst4_h[:, 32:48], sum_a[:])
                P.dma("sync", qS, lnst4_h[:, 48:64], sum_b[:])
            P.wait("scalar", vV, rstat)
            copy_marks = []
            mm_marks = []
            store_marks = []
            for i in range(NT):
                pp = i % 2
                xb = xbuf[:, i * 1024:(i + 1) * 1024]
                rb = r1[:, i:i + 1].broadcast_to([128, 1024])
                if i >= 1:
                    P.wait("vector", sS, copy_marks[i - 1])
                P.add("vector", lambda e, xb=xb, i=i, rb=rb: e.scalar_tensor_tensor(
                    xn1f[:], xb, m1[:, i:i + 1], rb, ALU.subtract, ALU.mult))
                xv = P.sig("vector", vV)
                P.wait("scalar", vV, xv)
                if i >= 2:
                    P.wait("scalar", tM, mm_marks[i - 2])
                    P.wait("scalar", qS, store_marks[i - 2])
                P.add("scalar", lambda e, pp=pp: e.copy(xn1b[:, pp], xn1f[:]))
                cs = P.sig("scalar", sS)
                copy_marks.append(cs)
                P.wait("sync", sS, cs)
                store_marks.append(P.dma("sync", qS, xn1_h[i * 128:(i + 1) * 128, :], xn1b[:, pp]))
                P.wait("tensor", sS, cs)
                for hf in range(2):
                    P.add("tensor", lambda e, pp=pp, hf=hf, i=i: e.matmul(
                        psA[0:1, 3072 + hf * 512: 3072 + hf * 512 + 512], onesbf[:],
                        xn1b[:, pp][:, hf * 512:(hf + 1) * 512],
                        start=(i == 0 and hf == 0), stop=(i == NT - 1)))
                mm_marks.append(P.sig("tensor", tM))
            P.wait("vector", tM, mm_marks[-1])
            P.add("vector", lambda e: e.tensor_scalar(
                mrow[:], psA[0:1, 3072:3072 + 1024], 1.0 / T, None, ALU.mult))
            mv2 = P.sig("vector", vV)
            P.wait("sync", vV, mv2)
            P.dma("sync", qS, mean_h, mrow[:])
            mh2 = P.dma("sync", qS, meanTf[:], mean_h.rearrange("o (c p) -> (o p) c", p=128), slow=True)
            P.wait("scalar", qS, mh2)
            P.add("scalar", lambda e: e.copy(meanT[:], meanTf[:]))
            sv3 = P.sig("scalar", sS)
            P.dma("sync", qS, bg8[:], W["L1_ffn_b_gate"])
            bl = P.dma("sync", qS, alph[:], W["L1_ffn_alpha"])
            P.wait("tensor", sS, sv3)
            P.wait("tensor", vV, mv2)
            lgm = []
            for j in range(8):
                if j > 0:
                    P.wait("sync", tM, lgm[-1])
                wv_ = P.dma("sync", qS, wg8[:], wgate_h[j * 128:(j + 1) * 128, :])
                P.wait("tensor", qS, wv_)
                P.add("tensor", lambda e, j=j: e.matmul(
                    psA[0:8, 3072:3073], wg8[:], meanT[:, j:j + 1], start=(j == 0), stop=(j == 7)))
                lgm.append(P.sig("tensor", tM))
            P.wait("vector", tM, lgm[-1])
            P.wait("vector", qS, bl)
            P.add("vector", lambda e: e.tensor_tensor(lg[:], psA[0:8, 3072:3073], bg8[:], ALU.add))
            lgv = P.sig("vector", vV)
            P.wait("sync", vV, lgv)
            P.dma("sync", qS, scal_h[:, 0:8].rearrange("o e -> (o e)").unsqueeze(1), lg[:])
            lrl = P.dma("sync", qS, lrow[:], scal_h[:, 0:8])
            P.add("gpsimd", lambda e: e.iota(io8[:], [[1, 8]], base=0, channel_multiplier=0,
                                             allow_small_or_imprecise_dtypes=True))
            P.add("gpsimd", lambda e: e.iota(iotaJ[:], [[0, 1]], base=0, channel_multiplier=1,
                                             allow_small_or_imprecise_dtypes=True))
            gio = P.sig("gpsimd", gG)
            P.wait("vector", qS, lrl)
            P.wait("vector", gG, gio)
            P.vsep(3)
            P.add("vector", lambda e: e.max(srt[:], lrow[:]))
            P.vsep(3)
            P.add("vector", lambda e: e.tensor_scalar(tt8[:], lrow[:], srt[:, 0:1], None, ALU.is_equal))
            P.vsep(3)
            P.add("vector", lambda e: e.tensor_tensor(tt8b[:], tt8[:], io8[:], ALU.mult))
            P.vsep(3)
            P.add("vector", lambda e: e.tensor_reduce(sc16[:, 3:4], tt8b[:], AX.X, ALU.add))
            P.vsep(3)
            P.add("vector", lambda e: e.tensor_scalar(tt8[:], tt8[:], -1.0, 1.0, ALU.mult, ALU.add))
            P.vsep(3)
            P.add("vector", lambda e: e.tensor_scalar(tt8b[:], lrow[:], srt[:, 1:2], None, ALU.is_equal))
            P.vsep(3)
            P.add("vector", lambda e: e.tensor_tensor(tt8b[:], tt8b[:], tt8[:], ALU.mult))
            P.vsep(3)
            P.add("vector", lambda e: e.tensor_tensor(tt8b[:], tt8b[:], io8[:], ALU.mult))
            P.vsep(3)
            P.add("vector", lambda e: e.tensor_reduce(sc16[:, 4:5], tt8b[:], AX.X, ALU.add))
            P.vsep(3)
            P.add("vector", lambda e: e.tensor_tensor(sc16[:, 5:6], srt[:, 0:1], srt[:, 1:2], ALU.subtract))
            gv1 = P.sig("vector", vV)
            P.wait("scalar", vV, gv1)
            P.add("scalar", lambda e: e.activation(sc16[:, 6:7], sc16[:, 5:6], ACT.Sigmoid))
            sgv = P.sig("scalar", sS)
            P.wait("vector", sS, sgv)
            P.vsep(3)
            P.add("vector", lambda e: e.tensor_scalar(sc16[:, 0:1], alph[:], 0.1, 1.0, ALU.max, ALU.min))
            P.vsep(3)
            P.add("vector", lambda e: e.tensor_scalar(sc16[:, 7:8], sc16[:, 0:1], -1.0, 1.0, ALU.mult, ALU.add))
            P.vsep(3)
            P.add("vector", lambda e: e.tensor_tensor(sc16[:, 1:2], sc16[:, 6:7], sc16[:, 7:8], ALU.mult))
            P.vsep(3)
            P.add("vector", lambda e: e.tensor_tensor(sc16[:, 2:3], sc16[:, 7:8], sc16[:, 1:2], ALU.subtract))
            scv = P.sig("vector", vV)
            P.wait("sync", vV, scv)
            P.dma("sync", qS, scal_h[:, 0:16], sc16[:])
            screpl = P.dma("sync", qS, screp[:], scal_h.broadcast_to([128, 16]))
            P.wait("vector", qS, screpl)
            for k in range(2):
                scol = screp[:, 3 + k:4 + k]
                for (nch, wid, srcap, dsth, rowsz) in [
                        (8, HM, W["L1_ffn_W1e"], w1g_h[k], D),
                        (32, D, W["L1_ffn_W2e"], w2g_h[k], HM)]:
                    for jc in range(nch):
                        P.add("vector", lambda e, scol=scol, rs=rowsz, jc=jc: e.scalar_tensor_tensor(
                            idxf[:], scol, float(rs), iotaJ[:, 0:1], ALU.mult, ALU.add))
                        P.vsep(3)
                        P.add("vector", lambda e, jc=jc: e.tensor_scalar(
                            idxf[:], idxf[:], float(jc * 128), None, ALU.add))
                        P.vsep(3)
                        P.add("vector", lambda e: e.tensor_copy(idxu[:], idxf[:]))
                        iv = P.sig("vector", vV)
                        P.vsep(3)
                        P.wait("gpsimd", vV, iv)
                        P.add("gpsimd", lambda e, srcap=srcap, wid=wid: e.indirect_dma_start(
                            gbuf[:, 0:wid], None, srcap[0:128, :],
                            IndirectOffsetOnAxis(idxu[:], axis=0)).then_inc(qG.h, 16))
                        gv_ = qG.bump(16)
                        P.wait("sync", qG, gv_)
                        sv_ = P.dma("sync", qS, dsth[jc * 128:(jc + 1) * 128, :], gbuf[:, 0:wid])
                        P.wait("gpsimd", qS, sv_)
                        P.wait("vector", qS, sv_)
                for (bsrc, bw, bdst) in [(W["L1_ffn_b1e"], HM // 128, b1g_h[k]),
                                         (W["L1_ffn_b2e"], D // 128, b2g_h[k])]:
                    P.add("vector", lambda e, scol=scol: e.scalar_tensor_tensor(
                        idxf[:], scol, 128.0, iotaJ[:, 0:1], ALU.mult, ALU.add))
                    P.vsep(3)
                    P.add("vector", lambda e: e.tensor_copy(idxu[:], idxf[:]))
                    iv = P.sig("vector", vV)
                    P.vsep(3)
                    P.wait("gpsimd", vV, iv)
                    srcv = bsrc.rearrange("e (p c) -> (e p) c", p=128)
                    P.add("gpsimd", lambda e, srcv=srcv, bw=bw: e.indirect_dma_start(
                        bgath[0:128, 0:bw].__class__(bgath.tensor, bgath[0:128, 0:bw].offset, bgath[0:128, 0:bw].ap)
                        if False else bgath2[:, 0:bw], None, srcv[0:128, :],
                        IndirectOffsetOnAxis(idxu[:], axis=0)).then_inc(qG.h, 16))
                    gv_ = qG.bump(16)
                    P.wait("sync", qG, gv_)
                    sv_ = P.dma("sync", qS,
                                bdst.rearrange("o (p c) -> (o p) c", c=bw), bgath2[:, 0:bw])
                    P.wait("gpsimd", qS, sv_)
                    P.wait("vector", qS, sv_)
            P.emit()

        if upto <= 4:
            return nc

        _es = contextlib.ExitStack()
        with _es:
            xn1T = _es.enter_context(nc.sbuf_tensor("xn1T", [128, 16384], BF16, is_transpose_dest=True))
            wqkv = _es.enter_context(nc.sbuf_tensor("wqkv", [128, 3, 8, 1024], BF16))
            brep = _es.enter_context(nc.sbuf_tensor("brep", [128, 3, 1024], F32))
            Qt = _es.enter_context(nc.sbuf_tensor("Qt", [128, 1024], F32))
            Kt = _es.enter_context(nc.sbuf_tensor("Kt", [128, 1024], F32))
            Vt = _es.enter_context(nc.sbuf_tensor("Vt", [128, 1024], F32))
            SA = _es.enter_context(nc.sbuf_tensor("SA", [128, 64, 32], F32))
            SB = _es.enter_context(nc.sbuf_tensor("SB", [128, 32, 32], F32))
            dnA = _es.enter_context(nc.sbuf_tensor("dnA", [128, 64], F32))
            nmA = _es.enter_context(nc.sbuf_tensor("nmA", [128, 64], F32))
            dnB = _es.enter_context(nc.sbuf_tensor("dnB", [128, 32], F32))
            nmB = _es.enter_context(nc.sbuf_tensor("nmB", [128, 32], F32))
            recd = _es.enter_context(nc.sbuf_tensor("recd", [128, 64], F32))
            mm4 = _es.enter_context(nc.sbuf_tensor("mm4", [128, 4], F32))
            negM = _es.enter_context(nc.sbuf_tensor("negM", [128, 1], F32))
            abuf = _es.enter_context(nc.sbuf_tensor("abuf", [128, 1024], F32))
            abbf = _es.enter_context(nc.sbuf_tensor("abbf", [128, 2, 1024], BF16))
            mneg = _es.enter_context(nc.sbuf_tensor("mneg", [128, 32, 32], F32))
            P = Plan(nc, vsept)
            P.add("gpsimd", lambda e: e.memset(mneg[:], 0.0))
            P.add("gpsimd", lambda e: e.affine_select(
                mneg[:], mneg[:], [[1, 32], [-1, 32]], ALU.is_ge, -1e30,
                channel_multiplier=0))
            mkd = P.sig("gpsimd", gG)
            for j in range(8):
                P.dma("sync", qS, xn1T[:, j * 2048:(j + 1) * 2048],
                      xn1_h[:, j * 128:(j + 1) * 128], transpose=True)
            for wi, wh in enumerate([wq1_h, wk1_h, wv1_h]):
                for j in range(8):
                    P.dma("sync", qS, wqkv[:, wi, j], wh[j * 128:(j + 1) * 128, :])
            for bi_, nm_ in enumerate(["L1_attn_bq", "L1_attn_bk", "L1_attn_bv"]):
                P.dma("sync", qS, brep[:, bi_], W[nm_].broadcast_to([128, 1024]))
            ldall = qS.v
            P.wait("vector", qS, ldall)
            P.add("vector", lambda e: e.tensor_scalar(brep[:, 0], brep[:, 0], 0.125, None, ALU.mult))
            bqs = P.sig("vector", vV)
            P.wait("tensor", qS, ldall)
            scop = []
            for tc in range(NT):
                if tc > 0:
                    P.wait("tensor", vV, vV.v)
                for j in range(8):
                    st = xn1T[:, j * 2048 + tc * 128: j * 2048 + tc * 128 + 128]
                    for wi in range(3):
                        for hf in range(2):
                            P.add("tensor", lambda e, st=st, wi=wi, hf=hf, j=j: e.matmul(
                                bank(wi * 2 + hf), st, wqkv[:, wi, j][:, hf * 512:(hf + 1) * 512],
                                start=(j == 0), stop=(j == 7)))
                mv = P.sig("tensor", tM)
                P.wait("vector", tM, mv)
                if tc == 0:
                    P.wait("vector", vV, bqs)
                if tc >= 2:
                    P.wait("vector", sS, scop[tc - 2])
                for hf in range(2):
                    P.add("vector", lambda e, hf=hf: e.scalar_tensor_tensor(
                        Qt[:, hf * 512:(hf + 1) * 512], bank(hf), 0.125,
                        brep[:, 0][:, hf * 512:(hf + 1) * 512], ALU.mult, ALU.add))
                    P.add("vector", lambda e, hf=hf: e.tensor_tensor(
                        Kt[:, hf * 512:(hf + 1) * 512], bank(2 + hf),
                        brep[:, 1][:, hf * 512:(hf + 1) * 512], ALU.add))
                    P.add("vector", lambda e, hf=hf: e.tensor_tensor(
                        Vt[:, hf * 512:(hf + 1) * 512], bank(4 + hf),
                        brep[:, 2][:, hf * 512:(hf + 1) * 512], ALU.add))
                P.add("vector", lambda e: e.tensor_reduce(mm4[:, 0:1], Qt[:], AX.X, ALU.max,
                                                          apply_absolute_value=True))
                P.add("vector", lambda e: e.tensor_reduce(mm4[:, 1:2], Kt[:], AX.X, ALU.max,
                                                          apply_absolute_value=True))
                P.vsep(3)
                P.add("vector", lambda e: e.tensor_tensor(negM[:], mm4[:, 0:1], mm4[:, 1:2], ALU.mult))
                P.vsep(3)
                P.add("vector", lambda e: e.tensor_scalar(negM[:], negM[:], -1.0, None, ALU.mult))
                qkv = P.sig("vector", vV)
                if "dQ" in dbg:
                    P.wait("sync", vV, qkv)
                    P.dma("sync", qS, dbg_Q[tc * 128:(tc + 1) * 128, :], Qt[:])
                    P.dma("sync", qS, dbg_K[tc * 128:(tc + 1) * 128, :], Kt[:])
                    P.dma("sync", qS, dbg_V[tc * 128:(tc + 1) * 128, :], Vt[:])
                P.wait("scalar", vV, qkv)
                P.wait("gpsimd", vV, qkv)
                for h in range(H):
                    q0 = Qt[:, h * 64: h * 64 + 64]
                    k0 = Kt[:, h * 64: h * 64 + 32]
                    k1 = Kt[:, h * 64 + 32: h * 64 + 64]
                    v0 = Vt[:, h * 64: h * 64 + 32]
                    v1_ = Vt[:, h * 64 + 32: h * 64 + 64]
                    P.add("vector", lambda e, q0=q0, k0=k0: e.tensor_tensor(
                        SA[:], q0.unsqueeze(2).broadcast_to([128, 64, 32]),
                        k0.unsqueeze(1).broadcast_to([128, 64, 32]), ALU.mult))
                    P.add("vector", lambda e, q0=q0, k1=k1: e.tensor_tensor(
                        SB[:], q0[:, 32:64].unsqueeze(2).broadcast_to([128, 32, 32]),
                        k1.unsqueeze(1).broadcast_to([128, 32, 32]), ALU.mult))
                    sv_ = P.sig("vector", vV)
                    P.wait("gpsimd", vV, sv_)
                    P.add("gpsimd", lambda e: e.tensor_tensor(
                        SA[:, 0:32, :], SA[:, 0:32, :], mneg[:], ALU.add))
                    P.add("gpsimd", lambda e: e.tensor_tensor(
                        SB[:], SB[:], mneg[:], ALU.add))
                    gv_ = P.sig("gpsimd", gG)
                    P.wait("scalar", gG, gv_)
                    P.add("scalar", lambda e: e.activation(
                        SA[:].rearrange("p a b -> p (a b)"), SA[:].rearrange("p a b -> p (a b)"),
                        ACT.Exp, bias=negM[:]))
                    P.add("scalar", lambda e: e.activation(
                        SB[:].rearrange("p a b -> p (a b)"), SB[:].rearrange("p a b -> p (a b)"),
                        ACT.Exp, bias=negM[:]))
                    ev_ = P.sig("scalar", sS)
                    if "dSA" in dbg and tc == 0 and h == 0:
                        P.wait("sync", sS, ev_)
                        P.dma("sync", qS, dbg_SA, SA[:])
                        P.dma("sync", qS, dbg_SB, SB[:])
                        dwait = qS.v
                        P.wait("vector", qS, dwait)
                    P.wait("vector", sS, ev_)
                    P.add("vector", lambda e: e.tensor_reduce(dnA[:], SA[:], AX.X, ALU.add))
                    P.add("vector", lambda e: e.tensor_reduce(dnB[:], SB[:], AX.X, ALU.add))
                    P.vsep(2)
                    P.add("vector", lambda e: e.tensor_tensor(
                        dnA[:, 32:64], dnA[:, 32:64], dnB[:], ALU.add))
                    P.add("vector", lambda e, v0=v0: e.tensor_tensor(
                        SA[:], SA[:], v0.unsqueeze(1).broadcast_to([128, 64, 32]), ALU.mult))
                    P.add("vector", lambda e, v1_=v1_: e.tensor_tensor(
                        SB[:], SB[:], v1_.unsqueeze(1).broadcast_to([128, 32, 32]), ALU.mult))
                    P.add("vector", lambda e: e.reciprocal(recd[:], dnA[:]))
                    P.add("vector", lambda e: e.tensor_reduce(nmA[:], SA[:], AX.X, ALU.add))
                    P.add("vector", lambda e: e.tensor_reduce(nmB[:], SB[:], AX.X, ALU.add))
                    P.vsep(2)
                    P.add("vector", lambda e: e.tensor_tensor(
                        nmA[:, 32:64], nmA[:, 32:64], nmB[:], ALU.add))
                    P.vsep(2)
                    P.add("vector", lambda e, h=h: e.tensor_tensor(
                        abuf[:, h * 64:(h + 1) * 64], nmA[:], recd[:], ALU.mult))
                    hv = P.sig("vector", vV)
                    if h < H - 1:
                        P.wait("scalar", vV, hv)
                        P.wait("gpsimd", vV, hv)
                tiledone = vV.v
                P.wait("scalar", vV, tiledone)
                pp = tc % 2
                P.add("scalar", lambda e, pp=pp: e.copy(abbf[:, pp], abuf[:]))
                av = P.sig("scalar", sS)
                scop.append(av)
                P.wait("sync", sS, av)
                P.dma("sync", qS, attn1_h[tc * 128:(tc + 1) * 128, :], abbf[:, pp])
            P.emit()

        _es = contextlib.ExitStack()
        with _es:
            a1T = _es.enter_context(nc.sbuf_tensor("a1T", [128, 16384], BF16, is_transpose_dest=True))
            woS = _es.enter_context(nc.sbuf_tensor("woS", [128, 8, 1024], BF16))
            borep = _es.enter_context(nc.sbuf_tensor("borep", [128, 1024], F32))
            P = Plan(nc, vsept)
            for j in range(8):
                P.dma("sync", qS, a1T[:, j * 2048:(j + 1) * 2048],
                      attn1_h[:, j * 128:(j + 1) * 128], transpose=True)
                P.dma("sync", qS, woS[:, j], wo1_h[j * 128:(j + 1) * 128, :])
            lda = P.dma("sync", qS, borep[:], W["L1_attn_bo"].broadcast_to([128, 1024]))
            P.wait("tensor", qS, lda)
            P.wait("vector", qS, lda)
            om = []
            for tc in range(NT):
                pb = (tc % 2) * 2
                if tc >= 2:
                    P.wait("tensor", vV, om[tc - 2])
                for j in range(8):
                    st = a1T[:, j * 2048 + tc * 128: j * 2048 + tc * 128 + 128]
                    for hf in range(2):
                        P.add("tensor", lambda e, st=st, j=j, hf=hf, pb=pb: e.matmul(
                            bank(pb + hf), st, woS[:, j][:, hf * 512:(hf + 1) * 512],
                            start=(j == 0), stop=(j == 7)))
                mv = P.sig("tensor", tM)
                P.wait("vector", tM, mv)
                xb = xbuf[:, tc * 1024:(tc + 1) * 1024]
                P.add("vector", lambda e, xb=xb: e.tensor_tensor(xb, xb, borep[:], ALU.add))
                for hf in range(2):
                    P.add("vector", lambda e, xb=xb, hf=hf, pb=pb: e.tensor_tensor(
                        xb[:, hf * 512:(hf + 1) * 512], xb[:, hf * 512:(hf + 1) * 512],
                        bank(pb + hf), ALU.add))
                om.append(P.sig("vector", vV))
            P.emit()

        if upto <= 5:
            return nc

        _es = contextlib.ExitStack()
        with _es:
            xn1T2 = _es.enter_context(nc.sbuf_tensor("xn1T2", [128, 16384], BF16, is_transpose_dest=True))
            w1h = _es.enter_context(nc.sbuf_tensor("w1h", [128, 8, 2048], BF16))
            w2h = _es.enter_context(nc.sbuf_tensor("w2h", [128, 8, 1024], BF16))
            hfm = _es.enter_context(nc.sbuf_tensor("hfm", [128, 32, 512], BF16))
            expacc = _es.enter_context(nc.sbuf_tensor("expacc", [128, 8, 512], F32))
            b1col = _es.enter_context(nc.sbuf_tensor("b1col", [128, 3, 32], F32))
            bcmb = _es.enter_context(nc.sbuf_tensor("bcmb", [128, 8], F32))
            tmp8 = _es.enter_context(nc.sbuf_tensor("tmp8", [128, 8], F32))
            tmp8b = _es.enter_context(nc.sbuf_tensor("tmp8b", [128, 8], F32))
            ffv = _es.enter_context(nc.sbuf_tensor("ffv", [128, 512], F32))
            ffb = _es.enter_context(nc.sbuf_tensor("ffb", [128, 2, 512], BF16))
            ftok = _es.enter_context(nc.sbuf_tensor("ftok", [128, 2, 1024], BF16, is_transpose_dest=True))
            P = Plan(nc, vsept)
            for j in range(8):
                P.dma("sync", qS, xn1T2[:, j * 2048:(j + 1) * 2048],
                      xn1_h[:, j * 128:(j + 1) * 128], transpose=True)
            P.dma("sync", qS, b1col[:, 0], W["L1_ffn_b1m"].rearrange("o (c p) -> (o p) c", p=128), slow=True)
            for k in range(2):
                P.dma("sync", qS, b1col[:, 1 + k], b1g_h[k].rearrange("o (c p) -> (o p) c", p=128), slow=True)
            P.dma("sync", qS, bcmb[:], W["L1_ffn_b2m"].rearrange("o (c p) -> (o p) c", p=128), slow=True)
            P.dma("sync", qS, tmp8[:], b2g_h[0].rearrange("o (c p) -> (o p) c", p=128), slow=True)
            bld = P.dma("sync", qS, tmp8b[:], b2g_h[1].rearrange("o (c p) -> (o p) c", p=128), slow=True)
            P.wait("vector", qS, bld)
            P.add("vector", lambda e: e.tensor_scalar(bcmb[:], bcmb[:], screp[:, 0:1], None, ALU.mult))
            P.vsep(3)
            P.add("vector", lambda e: e.tensor_scalar(tmp8[:], tmp8[:], screp[:, 1:2], None, ALU.mult))
            P.vsep(3)
            P.add("vector", lambda e: e.tensor_tensor(bcmb[:], bcmb[:], tmp8[:], ALU.add))
            P.vsep(3)
            P.add("vector", lambda e: e.tensor_scalar(tmp8b[:], tmp8b[:], screp[:, 2:3], None, ALU.mult))
            P.vsep(3)
            P.add("vector", lambda e: e.tensor_tensor(bcmb[:], bcmb[:], tmp8b[:], ALU.add))
            bcv = P.sig("vector", vV)
            if "dps" in dbg:
                P.wait("sync", vV, bcv)
                P.dma("sync", qS, dbg_ps[:, 24:32], bcmb[:])
                P.dma("sync", qS, dbg_ps[:, 32:40], tmp8[:])
                P.dma("sync", qS, dbg_ps[:, 40:48], tmp8b[:])

            parts = [(w1g_h[0], w2g_h[0], 1, "e1"),
                     (w1g_h[1], w2g_h[1], 2, "e2"),
                     (w1m_h, w2m_h, 0, "main")]
            fmarks = []
            for tc in range(4):
                for (w1src, w2src, bix, kind) in parts:
                    if fmarks:
                        P.wait("tensor", vV, fmarks[-1])
                    gelu_marks = []
                    for nh in range(2):
                        P.wait("sync", tM, tM.v)
                        for jj in range(8):
                            P.dma("sync", qS, w1h[:, jj],
                                  w1src[jj * 128:(jj + 1) * 128, nh * 2048:(nh + 1) * 2048])
                        wl = qS.v
                        P.wait("tensor", qS, wl)
                        for n_ in range(16):
                            n = nh * 16 + n_
                            pb = n_ % 2
                            if n_ >= 2:
                                P.wait("tensor", sS, gelu_marks[-2])
                            for j in range(8):
                                st = w1h[:, j][:, n_ * 128: n_ * 128 + 128]
                                rhs = xn1T2[:, j * 2048 + tc * 512: j * 2048 + tc * 512 + 512]
                                P.add("tensor", lambda e, st=st, rhs=rhs, pb=pb, j=j: e.matmul(
                                    bank(pb), st, rhs, start=(j == 0), stop=(j == 7)))
                            mv = P.sig("tensor", tM)
                            P.wait("scalar", tM, mv)
                            P.add("scalar", lambda e, n=n, pb=pb, bix=bix: e.activation(
                                hfm[:, n], bank(pb), ACT.Gelu_apprx_tanh,
                                bias=b1col[:, bix, n:n + 1]))
                            gelu_marks.append(P.sig("scalar", sS))
                    P.wait("tensor", sS, gelu_marks[-1])
                    if "dhfm" in dbg and tc == 0 and kind == "main":
                        P.wait("sync", sS, gelu_marks[-1])
                        P.dma("sync", qS, dbg_hfm, hfm[:].rearrange("p a b -> p (a b)"))
                    for nq in range(4):
                        P.wait("sync", tM, tM.v)
                        for n_ in range(8):
                            nn = nq * 8 + n_
                            P.dma("sync", qS, w2h[:, n_], w2src[nn * 128:(nn + 1) * 128, :])
                        w2l = qS.v
                        P.wait("tensor", qS, w2l)
                        for n_ in range(8):
                            nn = nq * 8 + n_
                            for i_ in range(8):
                                st = w2h[:, n_][:, i_ * 128: i_ * 128 + 128]
                                P.add("tensor", lambda e, st=st, n_=n_, i_=i_, nn=nn: e.matmul(
                                    bank(i_), st, hfm[:, nn], start=(nn == 0), stop=(nn == 31)))
                        if nq < 3:
                            P.sig("tensor", tM)
                    dmv = P.sig("tensor", tM)
                    if "dps" in dbg and tc == 0 and kind == "e1":
                        P.wait("scalar", tM, dmv)
                        for i_ in range(8):
                            P.add("scalar", lambda e, i_=i_: e.copy(junk[:, 0:512], bank(i_)))
                            dpv = P.sig("scalar", sS)
                            P.wait("sync", sS, dpv)
                            dw = P.dma("sync", qS, dbg_ps[:, i_ * 512:(i_ + 1) * 512], junk[:, 0:512])
                            P.wait("scalar", qS, dw)
                        P.wait("tensor", sS, sS.v)
                        P.wait("vector", sS, sS.v)
                    P.wait("vector", tM, dmv)
                    if kind == "e1":
                        P.wait("vector", vV, bcv)
                        for i_ in range(8):
                            P.add("vector", lambda e, i_=i_: e.tensor_scalar(
                                expacc[:, i_], bank(i_), screp[:, 1:2], bcmb[:, i_:i_ + 1],
                                ALU.mult, ALU.add))
                        fmarks.append(P.sig("vector", vV))
                        if "dexp" in dbg and tc == 0:
                            P.wait("sync", vV, fmarks[-1])
                            P.dma("sync", qS, dbg_exp, expacc[:].rearrange("p a b -> p (a b)"))
                            P.dma("sync", qS, dbg_ps[:, 0:16], screp[:])
                            P.dma("sync", qS, dbg_ps[:, 16:24], bcmb[:])
                            dxw = qS.v
                            P.wait("vector", qS, dxw)
                    elif kind == "e2":
                        for i_ in range(8):
                            P.add("vector", lambda e, i_=i_: e.scalar_tensor_tensor(
                                expacc[:, i_], bank(i_), screp[:, 2:3], expacc[:, i_],
                                ALU.mult, ALU.add))
                        fmarks.append(P.sig("vector", vV))
                        if "dexp" in dbg and tc == 0:
                            P.wait("sync", vV, fmarks[-1])
                            P.dma("sync", qS, dbg_h[0:128, 0:2048],
                                  expacc[:, 0:4].rearrange("p a b -> p (a b)"))
                            dx2 = P.dma("sync", qS, dbg_h[128:256, 0:2048],
                                        expacc[:, 4:8].rearrange("p a b -> p (a b)"))
                            P.wait("vector", qS, dx2)
                    else:
                        fv = None
                        cpm = []
                        dmam = []
                        for i_ in range(8):
                            if i_ >= 1:
                                P.wait("vector", sS, cpm[i_ - 1])
                            P.add("vector", lambda e, i_=i_: e.scalar_tensor_tensor(
                                ffv[:], bank(i_), screp[:, 0:1], expacc[:, i_],
                                ALU.mult, ALU.add))
                            fv = P.sig("vector", vV)
                            P.wait("scalar", vV, fv)
                            if i_ >= 2:
                                P.wait("scalar", qS, dmam[i_ - 2])
                            P.add("scalar", lambda e, i_=i_: e.copy(ffb[:, i_ % 2], ffv[:]))
                            sv_ = P.sig("scalar", sS)
                            cpm.append(sv_)
                            P.wait("sync", sS, sv_)
                            dmam.append(P.dma("sync", qS,
                                  ffn1_h[i_ * 128:(i_ + 1) * 128, tc * 512:(tc + 1) * 512],
                                  ffb[:, i_ % 2]))
                        fmarks.append(fv)
            fsync = qS.v
            P.wait("sync", vV, fmarks[-1])
            addm = []
            for i in range(NT):
                pp = i % 2
                if i >= 2:
                    P.wait("sync", vV, addm[i - 2])
                fl = P.dma("sync", qS, ftok[:, pp], ffn1_h[:, i * 128:(i + 1) * 128], transpose=True)
                P.wait("vector", qS, fl)
                xb = xbuf[:, i * 1024:(i + 1) * 1024]
                P.add("vector", lambda e, xb=xb, pp=pp: e.tensor_tensor(xb, xb, ftok[:, pp], ALU.add))
                addm.append(P.sig("vector", vV))
            P.emit()

        P = Plan(nc, vsept)
        fin = P.dma("sync", qS, y.rearrange("(tc p) d -> p tc d", p=128),
                    xbuf[:].rearrange("p (tc d) -> p tc d", d=1024))
        P.wait("sync", qS, fin)
        P.emit()
        return nc



def kernel(**inputs):
    nc = build()
    lower_extended_insts(nc)
    x = np.ascontiguousarray(inputs["L0_attn_hidden_states"])
    base = {}
    for k, v in inputs.items():
        if k == "L0_attn_hidden_states":
            continue
        a = np.ascontiguousarray(v)
        if k == "L1_ffn_alpha":
            a = a.reshape(1, 1)
        elif k == "L1_ffn_W1e":
            a = a.reshape(NE * D, HM)
        elif k == "L1_ffn_W2e":
            a = a.reshape(NE * HM, D)
        elif k == "L1_ffn_b_gate":
            a = a.reshape(NE, 1)
        elif a.ndim == 1:
            a = a.reshape(1, -1)
        base[k] = a
    base["ident"] = np.eye(128, dtype=ml_dtypes.bfloat16)
    in_maps = [dict(base, x=np.ascontiguousarray(x[c])) for c in range(8)]
    res = run_bass_kernel_spmd(nc, in_maps, core_ids=list(range(8)))
    return np.stack([np.asarray(r["y"]) for r in res.results], axis=0)


# version 05
# speedup vs baseline: 1.1322x; 1.1262x over previous
import contextlib
import numpy as np
import ml_dtypes
import concourse.bass as bass
import concourse.mybir as mybir
from concourse.bass import IndirectOffsetOnAxis
from concourse.bass_utils import run_bass_kernel_spmd
from concourse.library_overlay import lower_extended_insts

F32 = mybir.dt.float32
BF16 = mybir.dt.bfloat16
U32 = mybir.dt.uint32
ALU = mybir.AluOpType
ACT = mybir.ActivationFunctionType
AX = mybir.AxisListType

T = 2048
D = 1024
H = 16
C = 64
FD = 256
NI = 2816
HM = 4096
NE = 8
NT = T // 128
NHALF = NI // 2


class Sem:

    def __init__(self, ctx, nc, name):
        self.h = ctx.enter_context(nc.semaphore(name))
        self.v = 0

    def bump(self, n):
        self.v += n
        return self.v


class Plan:

    ENGS = ("sync", "tensor", "vector", "scalar", "gpsimd")

    def __init__(self, nc, junk=None):
        self.nc = nc
        self._junk = junk
        self.ops = {e: [] for e in self.ENGS}

    def add(self, eng, fn):
        self.ops[eng].append(fn)

    def wait(self, eng, sem, val):
        if val > 0:
            self.ops[eng].append(lambda e, s=sem.h, v=val: e.wait_ge(s, v))

    def sig(self, eng, sem):
        v = sem.bump(1)
        last = self.ops[eng].pop()

        def wrapped(e, f=last, s=sem.h):
            inst = f(e)
            inst.then_inc(s, 1)
            return inst

        self.ops[eng].append(wrapped)
        return v

    def dma(self, eng, sem, out, in_, transpose=False, slow=False):
        if transpose:
            self.ops[eng].append(
                lambda e, o=out, i=in_, s=sem.h: e.dma_start_transpose(o, i).then_inc(s, 16))
        else:
            self.ops[eng].append(
                lambda e, o=out, i=in_, s=sem.h, sl=slow: e.dma_start(
                    out=o, in_=i, allow_slow_non_contiguous=sl).then_inc(s, 16))
        return sem.bump(16)

    def vsep(self, n=2):
        for _ in range(n):
            self.ops["vector"].append(
                lambda e, j=self._junk: e.memset(j[:, 0:1], 0.0))

    def emit(self):
        with self.nc.Block() as B:
            for eng in self.ENGS:
                fns = self.ops[eng]
                if not fns:
                    continue

                def body(e, fns=fns):
                    for f in fns:
                        f(e)

                getattr(B, eng)(body)


def _quant_emit(P, qs, Z, OUT, width, vV, sS, gG, use_g=True, zmark=None):
    nb = width // 16
    bm, s_, sinv, Tt, a, b, b2, sg, v1 = (qs[k] for k in
                                          ("bm", "s", "sinv", "T", "a", "b", "b2", "sg", "v1"))
    Z3 = Z.rearrange("p (a b) -> p a b", b=16)
    P.add("vector", lambda e: e.tensor_reduce(
        bm[:, 0:nb], Z3, AX.X, ALU.max, apply_absolute_value=True))
    P.add("vector", lambda e: e.tensor_scalar(
        s_[:, 0:nb], bm[:, 0:nb], 0.16666667, 1e-12, ALU.mult, ALU.max))
    P.vsep(2)
    P.add("vector", lambda e: e.reciprocal(sinv[:, 0:nb], s_[:, 0:nb]))
    if zmark is not None:
        P.wait("scalar", vV, zmark)
    P.add("scalar", lambda e: e.sign(sg[:, 0:width], Z))
    sv = P.sig("scalar", sS)
    sb = sinv[:, 0:nb].unsqueeze(2).broadcast_to([128, nb, 16])
    T3 = Tt[:, 0:width].rearrange("p (a b) -> p a b", b=16)
    V3s = v1[:, 0:width].rearrange("p (a b) -> p a b", b=16)
    P.wait("vector", sS, sv)
    P.add("vector", lambda e: e.tensor_tensor(v1[:, 0:width], Z, sg[:, 0:width], ALU.mult))
    P.add("vector", lambda e: e.tensor_tensor(T3, V3s, sb, ALU.mult))
    tv = P.sig("vector", vV)
    P.add("vector", lambda e: e.tensor_scalar(a[:, 0:width], Tt[:, 0:width], 0.25, None, ALU.is_gt))
    for thr in (0.75, 1.25, 1.75):
        P.add("vector", lambda e, t=thr: e.scalar_tensor_tensor(
            a[:, 0:width], Tt[:, 0:width], t, a[:, 0:width], ALU.is_gt, ALU.add))
    beng = "gpsimd" if use_g else "vector"
    if use_g:
        P.wait("gpsimd", vV, tv)
    P.add(beng, lambda e: e.tensor_scalar(b[:, 0:width], Tt[:, 0:width], 2.5, None, ALU.is_gt))
    P.add(beng, lambda e: e.tensor_scalar(b2[:, 0:width], Tt[:, 0:width], 3.5, None, ALU.is_gt))
    P.add(beng, lambda e: e.tensor_tensor(b[:, 0:width], b[:, 0:width], b2[:, 0:width], ALU.add))
    P.add(beng, lambda e: e.tensor_scalar(b2[:, 0:width], Tt[:, 0:width], 5.0, 2.0, ALU.is_gt, ALU.mult))
    P.add(beng, lambda e: e.tensor_tensor(b[:, 0:width], b[:, 0:width], b2[:, 0:width], ALU.add))
    if use_g:
        gv = P.sig("gpsimd", gG)
        P.wait("vector", gG, gv)
    P.add("vector", lambda e: e.scalar_tensor_tensor(
        a[:, 0:width], a[:, 0:width], 0.5, b[:, 0:width], ALU.mult, ALU.add))
    P.wait("vector", sS, sv)
    P.add("vector", lambda e: e.tensor_tensor(v1[:, 0:width], a[:, 0:width], sg[:, 0:width], ALU.mult))
    ssb = s_[:, 0:nb].unsqueeze(2).broadcast_to([128, nb, 16])
    O3 = OUT.rearrange("p (a b) -> p a b", b=16)
    V3 = v1[:, 0:width].rearrange("p (a b) -> p a b", b=16)
    P.add("vector", lambda e: e.tensor_tensor(O3, V3, ssb, ALU.mult))


def build(dbg=(), upto=99):
    nc = bass.Bass("TRN2", target_bir_lowering=False)

    def dk(name):
        return "ExternalOutput" if name in dbg else "Internal"

    xin = nc.dram_tensor("x", [T, D], F32, kind="ExternalInput").ap()
    W = {}
    for nm, shp in [
        ("L0_attn_Wq", [D, D]), ("L0_attn_Wk", [D, D]), ("L0_attn_Wv", [D, D]),
        ("L0_attn_Wo", [D, D]), ("L0_attn_Wf", [FD, C]),
        ("L0_ffn_w_gate", [NI, D]), ("L0_ffn_w_up", [NI, D]), ("L0_ffn_w_down", [D, NI]),
        ("L1_attn_Wq", [D, D]), ("L1_attn_bq", [1, D]), ("L1_attn_Wk", [D, D]),
        ("L1_attn_bk", [1, D]), ("L1_attn_Wv", [D, D]), ("L1_attn_bv", [1, D]),
        ("L1_attn_Wo", [D, D]), ("L1_attn_bo", [1, D]),
        ("L1_ffn_alpha", [1, 1]), ("L1_ffn_w_gate", [D, NE]), ("L1_ffn_b_gate", [NE, 1]),
        ("L1_ffn_W1m", [D, HM]), ("L1_ffn_b1m", [1, HM]), ("L1_ffn_W2m", [HM, D]),
        ("L1_ffn_b2m", [1, D]), ("L1_ffn_W1e", [NE * D, HM]), ("L1_ffn_b1e", [NE, HM]),
        ("L1_ffn_W2e", [NE * HM, D]), ("L1_ffn_b2e", [NE, D]),
    ]:
        W[nm] = nc.dram_tensor(nm, shp, F32, kind="ExternalInput").ap()
    y = nc.dram_tensor("y", [T, D], F32, kind="ExternalOutput").ap()
    ident = nc.dram_tensor("ident", [128, 128], BF16, kind="ExternalInput").ap()

    def scr(name, shp, dt=BF16):
        return nc.dram_tensor(name, shp, dt, kind=dk(name)).ap()

    xn0_h = scr("xn0", [T, D])
    xq_h = scr("xq", [T, D])
    wq0_h = scr("wq0", [D, D])
    wk0_h = scr("wk0", [D, D])
    wv0_h = scr("wv0", [D, D])
    wo0_h = scr("wo0", [D, D])
    wf_h = scr("wfb", [FD, C])
    attn0_h = scr("attn0", [T, D])
    wqg_h = scr("wqg", [NI, D])
    wqu_h = scr("wqu", [NI, D])
    wqd_h = scr("wqd", [D, NI])
    hq_h = scr("hq", [T, NI])
    x1_h = scr("x1", [T, D], F32)
    xn1_h = scr("xn1", [T, D])
    wq1_h = scr("wq1", [D, D])
    wk1_h = scr("wk1", [D, D])
    wv1_h = scr("wv1", [D, D])
    wo1_h = scr("wo1", [D, D])
    w1m_h = scr("w1m", [D, HM])
    w2m_h = scr("w2m", [HM, D])
    wgate_h = scr("wgate", [D, NE])
    attn1_h = scr("attn1", [T, D])
    mean_h = scr("meanh", [1, D], F32)
    scal_h = scr("scalh", [1, 16], F32)
    w1g_h = [scr(f"w1g{k}", [D, HM]) for k in range(2)]
    w2g_h = [scr(f"w2g{k}", [HM, D]) for k in range(2)]
    b1g_h = [scr(f"b1g{k}", [1, HM], F32) for k in range(2)]
    b2g_h = [scr(f"b2g{k}", [1, D], F32) for k in range(2)]
    ffn1_h = scr("ffn1", [D, T])
    lnst_h = scr("lnst", [128, 64], F32)
    _dall = set(dbg)
    if "dqfm" in dbg:
        dbg = set(dbg) | {"dkfm", "dvtok", "dkv", "dwfT"}
    if "dQ" in dbg:
        dbg = set(dbg) | {"dK", "dV"}
    if "dSA" in dbg:
        dbg = set(dbg) | {"dSB"}
    dbg_qfm = scr("dqfm", [128, 16384])
    dbg_kfm = scr("dkfm", [128, 16384])
    dbg_vtok = scr("dvtok", [128, 16384])
    dbg_kv = scr("dkv", [128, 2080])
    dbg_wfT = scr("dwfT", [128, 256])
    dbg_h = scr("dh", [T, NI], F32)
    lnst4_h = scr("lnst4", [128, 64], F32)
    dbg_Q = scr("dQ", [T, D], F32)
    dbg_K = scr("dK", [T, D], F32)
    dbg_V = scr("dV", [T, D], F32)
    dbg_SA = scr("dSA", [128, 64, 32], F32)
    dbg_SB = scr("dSB", [128, 32, 32], F32)
    dbg_hfm = scr("dhfm", [128, 16384])
    dbg_exp = scr("dexp", [128, 4096], F32)
    dbg_ps = scr("dps", [128, 4096], F32)
    dbg_g = scr("dg", [T, NI], F32)

    ctx = contextlib.ExitStack()
    with ctx:
        qS = Sem(ctx, nc, "qS")
        qG = Sem(ctx, nc, "qG")
        tM = Sem(ctx, nc, "tM")
        vV = Sem(ctx, nc, "vV")
        sS = Sem(ctx, nc, "sS")
        gG = Sem(ctx, nc, "gG")

        def sb(n, shp, dt=F32, **kw):
            return ctx.enter_context(nc.sbuf_tensor(n, shp, dt, **kw))

        m0 = sb("m0", [128, NT])
        r0 = sb("r0", [128, NT])
        m1 = sb("m1", [128, NT])
        r1 = sb("r1", [128, NT])
        sum_a = sb("sum_a", [128, NT])
        sum_b = sb("sum_b", [128, NT])
        junk = sb("junk", [128, 1024])
        vsept = sb("vsept", [128, 1])
        onesbf = sb("onesbf", [128, 1], BF16)
        screp = sb("screp", [128, 16])
        psA = ctx.enter_context(nc.psum_tensor("psA", [128, 4096], F32))
        xbuf = None

        def bank(i, w=512):
            return psA[:, i * 512: i * 512 + w]

        def bslice(i, off, w):
            return psA[:, i * 512 + off: i * 512 + off + w]

        P = Plan(nc, vsept)
        P.add("vector", lambda e: e.memset(onesbf[:], 1.0))
        for src, dst in [(W["L0_attn_Wq"], wq0_h), (W["L0_attn_Wk"], wk0_h),
                         (W["L0_attn_Wv"], wv0_h), (W["L0_attn_Wo"], wo0_h),
                         (W["L0_attn_Wf"], wf_h),
                         (W["L1_attn_Wq"], wq1_h), (W["L1_attn_Wk"], wk1_h),
                         (W["L1_attn_Wv"], wv1_h), (W["L1_attn_Wo"], wo1_h),
                         (W["L1_ffn_W1m"], w1m_h), (W["L1_ffn_W2m"], w2m_h),
                         (W["L1_ffn_w_gate"], wgate_h)]:
            P.add("gpsimd", lambda e, s=src, d=dst: e.dma_start(out=d, in_=s).then_inc(qG.h, 16))
            qG.bump(16)
        castdone = qG.v
        with nc.sbuf_tensor("xt0", [128, 2, 1024], F32) as xt0, \
             nc.sbuf_tensor("jsq", [128, 2, 1024], F32) as jsq:
            smarks = []
            vmarks0 = []
            for i in range(NT):
                pp = i % 2
                if i >= 2:
                    P.wait("sync", vV, vmarks0[i - 2])
                ldv = P.dma("sync", qS, xt0[:, pp], xin[i * 128:(i + 1) * 128, :])
                P.wait("scalar", qS, ldv)
                if i >= 2:
                    P.wait("scalar", vV, vmarks0[i - 2])
                P.add("scalar", lambda e, pp=pp: e.activation(jsq[:, pp], xt0[:, pp], ACT.Square))
                sm = P.sig("scalar", sS)
                smarks.append(sm)
                P.wait("vector", sS, sm)
                P.add("vector", lambda e, pp=pp, i=i: e.tensor_reduce(
                    sum_a[:, i:i + 1], xt0[:, pp], AX.X, ALU.add))
                P.add("vector", lambda e, pp=pp, i=i: e.tensor_reduce(
                    sum_b[:, i:i + 1], jsq[:, pp], AX.X, ALU.add))
                vmarks0.append(P.sig("vector", vV))
            P.add("vector", lambda e: e.tensor_scalar(m0[:], sum_a[:], 1.0 / D, None, ALU.mult))
            P.add("vector", lambda e: e.tensor_scalar(junk[:, 0:NT], sum_b[:], 1.0 / D, None, ALU.mult))
            P.vsep(3)
            P.add("vector", lambda e: e.tensor_tensor(r0[:], m0[:], m0[:], ALU.mult))
            P.vsep(3)
            P.add("vector", lambda e: e.tensor_tensor(junk[:, 0:NT], junk[:, 0:NT], r0[:], ALU.subtract))
            P.vsep(3)
            P.add("vector", lambda e: e.tensor_scalar(junk[:, 0:NT], junk[:, 0:NT], 1e-5, None, ALU.add))
            vv = P.sig("vector", vV)
            P.wait("scalar", vV, vv)
            P.add("scalar", lambda e: e.activation(junk[:, NT:2 * NT], junk[:, 0:NT], ACT.Sqrt))
            sv2 = P.sig("scalar", sS)
            P.wait("vector", sS, sv2)
            P.add("vector", lambda e: e.reciprocal(r0[:], junk[:, NT:2 * NT]))
            P.emit()

        _es = contextlib.ExitStack()
        with _es:
            xt1 = _es.enter_context(nc.sbuf_tensor("xt1", [128, 2, 1024], F32))
            xnf = _es.enter_context(nc.sbuf_tensor("xnf", [128, 1024], F32))
            xnbf = _es.enter_context(nc.sbuf_tensor("xnbf", [128, 2, 1024], BF16))
            xqbf = _es.enter_context(nc.sbuf_tensor("xqbf", [128, 2, 1024], BF16))
            q_bm = _es.enter_context(nc.sbuf_tensor("q_bm", [128, 64], F32))
            q_s = _es.enter_context(nc.sbuf_tensor("q_s", [128, 64], F32))
            q_sinv = _es.enter_context(nc.sbuf_tensor("q_sinv", [128, 64], F32))
            q_T = _es.enter_context(nc.sbuf_tensor("q_T", [128, 1024], F32))
            q_a = _es.enter_context(nc.sbuf_tensor("q_a", [128, 1024], F32))
            q_b = _es.enter_context(nc.sbuf_tensor("q_b", [128, 1024], F32))
            q_b2 = _es.enter_context(nc.sbuf_tensor("q_b2", [128, 1024], F32))
            q_sg = _es.enter_context(nc.sbuf_tensor("q_sg", [128, 1024], F32))
            q_v1 = _es.enter_context(nc.sbuf_tensor("q_v1", [128, 1024], F32))
            qs = dict(bm=q_bm, s=q_s, sinv=q_sinv, T=q_T, a=q_a, b=q_b, b2=q_b2,
                      sg=q_sg, v1=q_v1)
            P = Plan(nc, vsept)
            vmarks = []
            csm1 = []
            for i in range(NT):
                pp = i % 2
                if i >= 2:
                    P.wait("sync", vV, vmarks[i - 2])
                ldv = P.dma("sync", qS, xt1[:, pp], xin[i * 128:(i + 1) * 128, :])
                P.wait("vector", qS, ldv)
                if i >= 1:
                    P.wait("vector", sS, csm1[i - 1])
                rb = r0[:, i:i + 1].broadcast_to([128, 1024])
                P.add("vector", lambda e, pp=pp, i=i, rb=rb: e.scalar_tensor_tensor(
                    xnf[:], xt1[:, pp], m0[:, i:i + 1], rb, ALU.subtract, ALU.mult))
                xv = P.sig("vector", vV)
                P.wait("scalar", vV, xv)
                P.add("scalar", lambda e, pp=pp: e.copy(xnbf[:, pp], xnf[:]))
                cs = P.sig("scalar", sS)
                csm1.append(cs)
                P.wait("sync", sS, cs)
                P.dma("sync", qS, xn0_h[i * 128:(i + 1) * 128, :], xnbf[:, pp])
                _quant_emit(P, qs, xnf[:], xqbf[:, pp], 1024, vV, sS, gG, use_g=True)
                qv = P.sig("vector", vV)
                vmarks.append(qv)
                P.wait("sync", vV, qv)
                P.dma("sync", qS, xq_h[i * 128:(i + 1) * 128, :], xqbf[:, pp])
            P.dma("sync", qS, lnst_h[:, 0:NT], m0[:])
            P.dma("sync", qS, lnst_h[:, 16:16 + NT], r0[:])
            P.dma("sync", qS, lnst_h[:, 32:48], sum_a[:, 0:16])
            P.emit()

        if upto <= 1:
            return nc

        _es = contextlib.ExitStack()
        with _es:
            xn0T = _es.enter_context(nc.sbuf_tensor("xn0T", [128, 16384], BF16, is_transpose_dest=True))
            qfm = _es.enter_context(nc.sbuf_tensor("qfm", [128, 16384], BF16))
            kfm = _es.enter_context(nc.sbuf_tensor("kfm", [128, 16384], BF16))
            vtok = _es.enter_context(nc.sbuf_tensor("vtok", [128, 16384], BF16))
            a0sb = _es.enter_context(nc.sbuf_tensor("a0sb", [128, 16384], BF16))
            wT = _es.enter_context(nc.sbuf_tensor("wT", [128, 8192], BF16, is_transpose_dest=True))
            wfT = _es.enter_context(nc.sbuf_tensor("wfT", [128, 256], BF16, is_transpose_dest=True))
            kvsb = _es.enter_context(nc.sbuf_tensor("kvsb", [128, 2080], BF16))
            kfb = _es.enter_context(nc.sbuf_tensor("kfb", [128, 2, 256], BF16))
            etmp = _es.enter_context(nc.sbuf_tensor("etmp", [128, 512], F32))
            etmp2 = _es.enter_context(nc.sbuf_tensor("etmp2", [128, 512], F32))
            qff = _es.enter_context(nc.sbuf_tensor("qff", [128, 2, 4096], BF16))
            nrm = _es.enter_context(nc.sbuf_tensor("nrm", [128, 2], F32))
            P = Plan(nc, vsept)
            P.wait("gpsimd", qG, castdone)
            P.wait("sync", qG, castdone)
            for j in range(8):
                P.dma("sync", qS, xn0T[:, j * 2048:(j + 1) * 2048],
                      xn0_h[:, j * 128:(j + 1) * 128], transpose=True)
            idl = P.dma("sync", qS, junk[:, 0:64].bitcast(BF16)[:, 0:128], ident)
            wfl = P.dma("sync", qS, kfb[:, 0, 0:128].rearrange("p (c f) -> p c f", f=64),
                        wf_h.rearrange("(c p) f -> p c f", p=128))
            P.wait("tensor", qS, max(idl, wfl))
            for cc in range(2):
                for bb in range(2):
                    P.add("tensor", lambda e, cc=cc, bb=bb: e.transpose(
                        psA[bb * 64:bb * 64 + 64,
                            3584 + cc * 64:3584 + cc * 64 + 64].bitcast(BF16),
                        kfb[:, 0, cc * 64:(cc + 1) * 64],
                        junk[:, 0:64].bitcast(BF16)[:, 0:128]))
            trm = P.sig("tensor", tM)
            P.wait("scalar", tM, trm)
            for cc in range(2):
                P.add("scalar", lambda e, cc=cc: e.copy(
                    wfT[:, cc * 128:(cc + 1) * 128],
                    psA[:, 3584 + cc * 64:3584 + cc * 64 + 64].bitcast(BF16)))
            wfev = P.sig("scalar", sS)
            P.wait("tensor", sS, wfev)

            def proj_fm(wsrc_h, dest):
                tuse0 = tM.v
                P.wait("sync", tM, tuse0)
                for j in range(8):
                    P.dma("sync", qS, wT[:, j * 1024:(j + 1) * 1024],
                          wsrc_h[:, j * 128:(j + 1) * 128], transpose=True)
                wv_ = qS.v
                P.wait("tensor", qS, wv_)
                for ci in range(8):
                    for j in range(8):
                        st = wT[:, j * 1024 + ci * 128: j * 1024 + ci * 128 + 128]
                        for tc in range(4):
                            rhs = xn0T[:, j * 2048 + tc * 512: j * 2048 + tc * 512 + 512]
                            P.add("tensor", lambda e, st=st, rhs=rhs, tc=tc, j=j: e.matmul(
                                bank(tc), st, rhs, start=(j == 0), stop=(j == 7)))
                    mv = P.sig("tensor", tM)
                    P.wait("scalar", tM, mv)
                    for tc in range(4):
                        dst = dest[:, ci * 2048 + tc * 512: ci * 2048 + tc * 512 + 512]
                        P.add("scalar", lambda e, dst=dst, tc=tc: e.copy(dst, bank(tc)))
                    ev = P.sig("scalar", sS)
                    P.wait("tensor", sS, ev)

            proj_fm(wq0_h, qfm)
            proj_fm(wk0_h, kfm)
            P.wait("sync", tM, tM.v)
            for j in range(8):
                P.dma("sync", qS, wT[:, j * 1024:(j + 1) * 1024],
                      wv0_h[:, j * 128:(j + 1) * 128], transpose=True)
            P.wait("tensor", qS, qS.v)
            ev_marks = []
            for tc in range(NT):
                b0 = (tc % 2) * 2
                if tc >= 2:
                    P.wait("tensor", sS, ev_marks[tc - 2])
                for j in range(8):
                    st = xn0T[:, j * 2048 + tc * 128: j * 2048 + tc * 128 + 128]
                    for hf in range(2):
                        rhs = wT[:, j * 1024 + hf * 512: j * 1024 + hf * 512 + 512]
                        P.add("tensor", lambda e, st=st, rhs=rhs, b=b0 + hf, j=j: e.matmul(
                            bank(b), st, rhs, start=(j == 0), stop=(j == 7)))
                mv = P.sig("tensor", tM)
                P.wait("scalar", tM, mv)
                for hf in range(2):
                    dst = vtok[:, tc * 1024 + hf * 512: tc * 1024 + hf * 512 + 512]
                    P.add("scalar", lambda e, dst=dst, b=b0 + hf: e.copy(dst, bank(b)))
                ev_marks.append(P.sig("scalar", sS))
            P.wait("tensor", sS, ev_marks[-1])

            def kvslot(h, half):
                idx = h * 2 + half
                return idx // 7, (idx % 7) * 65

            kf_ev = {}
            for tc in range(NT):
                for h in range(H):
                    u = tc * H + h
                    pb = 5 + (h % 2)
                    if u >= 2:
                        P.wait("tensor", vV, kf_ev[u - 2])
                    stat = kfm[(h % 2) * 64:(h % 2) * 64 + 64,
                               (h // 2) * 2048 + tc * 128:(h // 2) * 2048 + tc * 128 + 128]
                    hb0 = (h % 2) * 64
                    P.add("tensor", lambda e, st=stat, pb=pb, hb0=hb0: e.matmul(
                        bank(pb, 256), st, wfT[hb0:hb0 + 64, :], start=True, stop=True))
                    mv = P.sig("tensor", tM)
                    P.wait("vector", tM, mv)
                    P.add("vector", lambda e, pb=pb: e.tensor_scalar(
                        etmp[:, 0:256], bank(pb, 256), 0.0, None, ALU.min))
                    ev1 = P.sig("vector", vV)
                    P.wait("scalar", vV, ev1)
                    P.add("scalar", lambda e: e.activation(etmp2[:, 0:256], etmp[:, 0:256], ACT.Exp))
                    sv = P.sig("scalar", sS)
                    P.wait("vector", sS, sv)
                    P.add("vector", lambda e, pb=pb, h=h: e.scalar_tensor_tensor(
                        kfb[:, h % 2], bank(pb, 256), 0.0, etmp2[:, 0:256], ALU.max, ALU.add))
                    kf_ev[u] = P.sig("vector", vV)
                    P.wait("tensor", vV, kf_ev[u])
                    for half in range(2):
                        bk, off = kvslot(h, half)
                        idx = h * 2 + half
                        st0 = (tc == 0) and (idx % 7 == 0)
                        stat2 = kfb[:, h % 2][:, half * 128: half * 128 + 128]
                        rhsv = vtok[:, tc * 1024 + h * 64: tc * 1024 + h * 64 + 64]
                        P.add("tensor", lambda e, st=stat2, r=rhsv, bk=bk, off=off, tc=tc, st0=st0: e.matmul(
                            bslice(bk, off, 64), st, r, start=st0, stop=(tc == NT - 1)))
                        P.add("tensor", lambda e, st=stat2, bk=bk, off=off, tc=tc: e.matmul(
                            bslice(bk, off + 64, 1), st, onesbf[:], start=False, stop=(tc == NT - 1)))
            kv_mm_done = P.sig("tensor", tM)
            P.wait("scalar", tM, kv_mm_done)
            for h in range(H):
                for half in range(2):
                    bk, off = kvslot(h, half)
                    idx = h * 2 + half
                    P.add("scalar", lambda e, bk=bk, off=off, idx=idx: e.copy(
                        kvsb[:, idx * 65: idx * 65 + 65], bslice(bk, off, 65)))
            kv_ev_done = P.sig("scalar", sS)

            prev_a8_last = {}
            for h in range(H):
                hb = h % 2
                if h >= 2:
                    P.wait("tensor", vV, prev_a8_last[h - 2])
                a7m = []
                sv_l = []

                def a7_stt(cj):
                    halfj, t4 = divmod(cj, 4)
                    pbj = 5 + cj % 3
                    src = etmp if cj % 2 == 0 else etmp2
                    dst = qff[:, hb][:, halfj * 2048 + t4 * 512: halfj * 2048 + t4 * 512 + 512]
                    P.wait("vector", sS, sv_l[cj])
                    P.add("vector", lambda e, pbj=pbj, dst=dst, src=src: e.scalar_tensor_tensor(
                        dst, bank(pbj), 0.0, src[:], ALU.max, ALU.add))
                    a7m.append(P.sig("vector", vV))

                for ci in range(8):
                    half, tc4 = divmod(ci, 4)
                    pb = 5 + ci % 3
                    hb0 = (h % 2) * 64
                    stat = wfT[hb0:hb0 + 64, half * 128: half * 128 + 128]
                    rhs = qfm[(h % 2) * 64:(h % 2) * 64 + 64,
                              (h // 2) * 2048 + tc4 * 512:(h // 2) * 2048 + tc4 * 512 + 512]
                    if ci >= 3:
                        P.wait("tensor", vV, a7m[ci - 3])
                    P.add("tensor", lambda e, st=stat, r=rhs, pb=pb: e.matmul(
                        bank(pb), st, r, start=True, stop=True))
                    mv = P.sig("tensor", tM)
                    P.wait("vector", tM, mv)
                    buf = etmp if ci % 2 == 0 else etmp2
                    P.add("vector", lambda e, pb=pb, buf=buf: e.tensor_scalar(
                        buf[:], bank(pb), 0.0, None, ALU.min))
                    ev1 = P.sig("vector", vV)
                    P.wait("scalar", vV, ev1)
                    P.add("scalar", lambda e, buf=buf: e.activation(buf[:], buf[:], ACT.Exp))
                    sv_l.append(P.sig("scalar", sS))
                    if ci >= 1:
                        a7_stt(ci - 1)
                a7_stt(7)
                P.wait("tensor", vV, a7m[-1])
                if h == 0:
                    P.wait("tensor", sS, kv_ev_done)
                a8m = []
                for tc in range(NT):
                    pb = 2 + tc % 2
                    if tc >= 2:
                        P.wait("tensor", vV, a8m[tc - 2])
                    for half in range(2):
                        stat = qff[:, hb][:, half * 2048 + tc * 128: half * 2048 + tc * 128 + 128]
                        rhs = kvsb[:, (h * 2 + half) * 65: (h * 2 + half) * 65 + 65]
                        P.add("tensor", lambda e, st=stat, r=rhs, pb=pb, half=half: e.matmul(
                            bslice(pb, 0, 65), st, r, start=(half == 0), stop=(half == 1)))
                    mv = P.sig("tensor", tM)
                    P.wait("vector", tM, mv)
                    P.add("vector", lambda e, pb=pb: e.tensor_scalar(
                        nrm[:, 0:1], bslice(pb, 64, 1), 1e-6, None, ALU.add))
                    P.vsep(2)
                    P.add("vector", lambda e: e.reciprocal(nrm[:, 1:2], nrm[:, 0:1]))
                    P.vsep(2)
                    dst = a0sb[:, tc * 1024 + h * 64: tc * 1024 + h * 64 + 64]
                    P.add("vector", lambda e, pb=pb, dst=dst: e.tensor_scalar(
                        dst, bslice(pb, 0, 64), nrm[:, 1:2], None, ALU.mult))
                    a8m.append(P.sig("vector", vV))
                prev_a8_last[h] = a8m[-1]
            P.wait("sync", vV, vV.v)
            P.dma("sync", qS, attn0_h.rearrange("(tc p) d -> p tc d", p=128),
                  a0sb[:].rearrange("p (tc d) -> p tc d", d=1024))
            if "dqfm" in dbg or "dwfT" in dbg:
                P.dma("sync", qS, lnst_h[:, 0:64], junk[:, 100:164])
                P.dma("sync", qS, dbg_qfm, qfm[:])
                P.dma("sync", qS, dbg_kfm, kfm[:])
                P.dma("sync", qS, dbg_vtok, vtok[:])
                P.dma("sync", qS, dbg_kv, kvsb[:])
                P.dma("sync", qS, dbg_wfT, wfT[:])
            P.emit()

        if upto <= 2:
            return nc

        _es = contextlib.ExitStack()
        with _es:
            wqf32 = _es.enter_context(nc.sbuf_tensor("wqf32", [128, 2, NI], F32))
            wqbf = _es.enter_context(nc.sbuf_tensor("wqbf", [128, 2, NI], BF16))
            f_bm = _es.enter_context(nc.sbuf_tensor("f_bm", [128, 176], F32))
            f_s = _es.enter_context(nc.sbuf_tensor("f_s", [128, 176], F32))
            f_sinv = _es.enter_context(nc.sbuf_tensor("f_sinv", [128, 176], F32))
            f_T = _es.enter_context(nc.sbuf_tensor("f_T", [128, NI], F32))
            f_a = _es.enter_context(nc.sbuf_tensor("f_a", [128, NI], F32))
            f_b = _es.enter_context(nc.sbuf_tensor("f_b", [128, NI], F32))
            f_b2 = _es.enter_context(nc.sbuf_tensor("f_b2", [128, NI], F32))
            f_sg = _es.enter_context(nc.sbuf_tensor("f_sg", [128, NI], F32))
            f_v1 = _es.enter_context(nc.sbuf_tensor("f_v1", [128, NI], F32))
            fqs = dict(bm=f_bm, s=f_s, sinv=f_sinv, T=f_T, a=f_a, b=f_b, b2=f_b2,
                       sg=f_sg, v1=f_v1)
            P = Plan(nc, vsept)
            vm, sm = [], []
            it = 0
            for (wsrc, wdst, nrows, width) in [
                    (W["L0_ffn_w_gate"], wqg_h, NI, D),
                    (W["L0_ffn_w_up"], wqu_h, NI, D),
                    (W["L0_ffn_w_down"], wqd_h, D, NI)]:
                for cchunk in range(nrows // 128):
                    pp = it % 2
                    if it >= 2:
                        P.wait("sync", vV, vm[it - 2])
                        P.wait("sync", sS, sm[it - 2])
                    ldv = P.dma("sync", qS, wqf32[:, pp][:, 0:width],
                                wsrc[cchunk * 128:(cchunk + 1) * 128, :])
                    P.wait("vector", qS, ldv)
                    P.wait("scalar", qS, ldv)
                    _quant_emit(P, fqs, wqf32[:, pp][:, 0:width], wqbf[:, pp][:, 0:width],
                                width, vV, sS, gG, use_g=True)
                    qv = P.sig("vector", vV)
                    vm.append(qv)
                    sm.append(sS.v)
                    P.wait("sync", vV, qv)
                    P.dma("sync", qS, wdst[cchunk * 128:(cchunk + 1) * 128, :],
                          wqbf[:, pp][:, 0:width])
                    it += 1
            P.emit()

        _es = contextlib.ExitStack()
        with _es:
            xqT = _es.enter_context(nc.sbuf_tensor("xqT", [128, 16384], BF16, is_transpose_dest=True))
            wgT = _es.enter_context(nc.sbuf_tensor("wgT", [128, 8, NHALF], BF16, is_transpose_dest=True))
            wuT = _es.enter_context(nc.sbuf_tensor("wuT", [128, 8, NHALF], BF16, is_transpose_dest=True))
            gsil = _es.enter_context(nc.sbuf_tensor("gsil", [128, NHALF], F32))
            hfull = _es.enter_context(nc.sbuf_tensor("hfull", [128, NHALF], F32))
            hqbf = _es.enter_context(nc.sbuf_tensor("hqbf", [128, NHALF], BF16))
            h_bm = _es.enter_context(nc.sbuf_tensor("h_bm", [128, 88], F32))
            h_s = _es.enter_context(nc.sbuf_tensor("h_s", [128, 88], F32))
            h_sinv = _es.enter_context(nc.sbuf_tensor("h_sinv", [128, 88], F32))
            h_T = _es.enter_context(nc.sbuf_tensor("h_T", [128, NHALF], F32))
            h_a = _es.enter_context(nc.sbuf_tensor("h_a", [128, NHALF], F32))
            h_b = _es.enter_context(nc.sbuf_tensor("h_b", [128, NHALF], F32))
            h_b2 = _es.enter_context(nc.sbuf_tensor("h_b2", [128, NHALF], F32))
            h_sg = _es.enter_context(nc.sbuf_tensor("h_sg", [128, NHALF], F32))
            h_v1 = _es.enter_context(nc.sbuf_tensor("h_v1", [128, NHALF], F32))
            hqs = dict(bm=h_bm, s=h_s, sinv=h_sinv, T=h_T, a=h_a, b=h_b, b2=h_b2,
                       sg=h_sg, v1=h_v1)
            P = Plan(nc, vsept)
            for j in range(8):
                P.dma("sync", qS, xqT[:, j * 2048:(j + 1) * 2048],
                      xq_h[:, j * 128:(j + 1) * 128], transpose=True)
            ncs = [(0, 512), (512, 512), (1024, 384)]
            hfull_marks = []
            silu_marks = []
            for nh in range(2):
                P.wait("sync", tM, tM.v)
                for k in range(8):
                    P.dma("sync", qS, wgT[:, k],
                          wqg_h[nh * NHALF:(nh + 1) * NHALF, k * 128:(k + 1) * 128], transpose=True)
                    P.dma("sync", qS, wuT[:, k],
                          wqu_h[nh * NHALF:(nh + 1) * NHALF, k * 128:(k + 1) * 128], transpose=True)
                wload = qS.v
                P.wait("tensor", qS, wload)
                for ts_ in range(16):
                    if nh > 0 or ts_ > 0:
                        P.wait("tensor", vV, hfull_marks[-1])
                        P.wait("tensor", sS, silu_marks[-1])
                    for k in range(8):
                        st = xqT[:, k * 2048 + ts_ * 128: k * 2048 + ts_ * 128 + 128]
                        for ib, (off, wdt) in enumerate(ncs):
                            P.add("tensor", lambda e, st=st, k=k, ib=ib, off=off, wdt=wdt: e.matmul(
                                bank(ib, wdt), st, wgT[:, k][:, off:off + wdt],
                                start=(k == 0), stop=(k == 7)))
                            P.add("tensor", lambda e, st=st, k=k, ib=ib, off=off, wdt=wdt: e.matmul(
                                bank(3 + ib, wdt), st, wuT[:, k][:, off:off + wdt],
                                start=(k == 0), stop=(k == 7)))
                    mv = P.sig("tensor", tM)
                    P.wait("scalar", tM, mv)
                    P.add("scalar", lambda e: e.activation(gsil[:], psA[:, 0:NHALF], ACT.Silu))
                    sv = P.sig("scalar", sS)
                    silu_marks.append(sv)
                    P.wait("vector", sS, sv)
                    P.wait("vector", tM, mv)
                    P.add("vector", lambda e: e.tensor_tensor(
                        hfull[:], gsil[:], psA[:, 1536:1536 + NHALF], ALU.mult))
                    hm_ = P.sig("vector", vV)
                    hfull_marks.append(hm_)
                    P.vsep(2)
                    if "dh" in dbg:
                        P.wait("sync", vV, hm_)
                        P.dma("sync", qS,
                              dbg_h[ts_ * 128:(ts_ + 1) * 128, nh * NHALF:(nh + 1) * NHALF],
                              hfull[:])
                        P.dma("sync", qS,
                              dbg_g[ts_ * 128:(ts_ + 1) * 128, nh * NHALF:(nh + 1) * NHALF],
                              gsil[:])
                    _quant_emit(P, hqs, hfull[:], hqbf[:], NHALF, vV, sS, gG, use_g=True,
                                zmark=hm_)
                    qv = P.sig("vector", vV)
                    P.wait("sync", vV, qv)
                    P.dma("sync", qS,
                          hq_h[ts_ * 128:(ts_ + 1) * 128, nh * NHALF:(nh + 1) * NHALF], hqbf[:])
            P.emit()

        xbuf = sb("xbuf", [128, 16384])
        _es = contextlib.ExitStack()
        with _es:
            wdT = _es.enter_context(nc.sbuf_tensor("wdT", [128, 22, 1024], BF16, is_transpose_dest=True))
            hqT = _es.enter_context(nc.sbuf_tensor("hqT", [128, 22, 512], BF16, is_transpose_dest=True))
            xorig = _es.enter_context(nc.sbuf_tensor("xorig", [128, 2, 1024], F32))
            a0T = _es.enter_context(nc.sbuf_tensor("a0T", [128, 16384], BF16, is_transpose_dest=True))
            wo0T = _es.enter_context(nc.sbuf_tensor("wo0T", [128, 8, 1024], BF16, is_transpose_dest=True))
            P = Plan(nc, vsept)
            for j in range(8):
                P.dma("sync", qS, a0T[:, j * 2048:(j + 1) * 2048],
                      attn0_h[:, j * 128:(j + 1) * 128], transpose=True)
                P.dma("sync", qS, wo0T[:, j], wo0_h[:, j * 128:(j + 1) * 128], transpose=True)
            for ncc in range(22):
                P.dma("sync", qS, wdT[:, ncc], wqd_h[:, ncc * 128:(ncc + 1) * 128], transpose=True)
            rmarks = []
            for tc in range(4):
                P.wait("sync", tM, tM.v)
                for ncc in range(22):
                    P.dma("sync", qS, hqT[:, ncc],
                          hq_h[tc * 512:(tc + 1) * 512, ncc * 128:(ncc + 1) * 128], transpose=True)
                hload = qS.v
                P.wait("tensor", qS, hload)
                for ts_ in range(4):
                    i_tile = tc * 4 + ts_
                    pp = i_tile % 2
                    pb = (i_tile % 2) * 2
                    if i_tile >= 2:
                        P.wait("tensor", vV, rmarks[i_tile - 2])
                        P.wait("sync", vV, rmarks[i_tile - 2])
                    for ncc in range(22):
                        st = hqT[:, ncc][:, ts_ * 128: ts_ * 128 + 128]
                        for hf in range(2):
                            P.add("tensor", lambda e, st=st, ncc=ncc, hf=hf, pb=pb: e.matmul(
                                bank(pb + hf), st, wdT[:, ncc][:, hf * 512:(hf + 1) * 512],
                                start=(ncc == 0), stop=(ncc == 21)))
                    ob = 4 + (i_tile % 2) * 2
                    for j in range(8):
                        sto = a0T[:, j * 2048 + i_tile * 128: j * 2048 + i_tile * 128 + 128]
                        for hf in range(2):
                            P.add("tensor", lambda e, sto=sto, j=j, hf=hf, ob=ob: e.matmul(
                                bank(ob + hf), sto, wo0T[:, j][:, hf * 512:(hf + 1) * 512],
                                start=(j == 0), stop=(j == 7)))
                    mv = P.sig("tensor", tM)
                    l1 = P.dma("sync", qS, xorig[:, pp], xin[i_tile * 128:(i_tile + 1) * 128, :])
                    P.wait("vector", qS, l1)
                    P.wait("vector", tM, mv)
                    xb = xbuf[:, i_tile * 1024:(i_tile + 1) * 1024]
                    for hf in range(2):
                        P.add("vector", lambda e, xb=xb, hf=hf, pp=pp, ob=ob: e.tensor_tensor(
                            xb[:, hf * 512:(hf + 1) * 512],
                            xorig[:, pp][:, hf * 512:(hf + 1) * 512],
                            bank(ob + hf), ALU.add))
                        P.add("vector", lambda e, xb=xb, hf=hf, pb=pb: e.tensor_tensor(
                            xb[:, hf * 512:(hf + 1) * 512], xb[:, hf * 512:(hf + 1) * 512],
                            bank(pb + hf), ALU.add))
                    rv = P.sig("vector", vV)
                    rmarks.append(rv)
                    P.wait("sync", vV, rv)
                    P.dma("sync", qS, x1_h[i_tile * 128:(i_tile + 1) * 128, :], xb)
            P.emit()

        if upto <= 3:
            return nc

        _es = contextlib.ExitStack()
        with _es:
            xn1f = _es.enter_context(nc.sbuf_tensor("xn1f", [128, 1024], F32))
            xn1b = _es.enter_context(nc.sbuf_tensor("xn1b", [128, 2, 1024], BF16))
            mrow = _es.enter_context(nc.sbuf_tensor("mrow", [1, 1024], F32))
            meanT = _es.enter_context(nc.sbuf_tensor("meanT", [128, 8], BF16))
            meanTf = _es.enter_context(nc.sbuf_tensor("meanTf", [128, 8], F32))
            wg8 = _es.enter_context(nc.sbuf_tensor("wg8", [128, 8], BF16))
            bg8 = _es.enter_context(nc.sbuf_tensor("bg8", [8, 1], F32))
            lg = _es.enter_context(nc.sbuf_tensor("lg", [8, 1], F32))
            lrow = _es.enter_context(nc.sbuf_tensor("lrow", [1, 8], F32))
            srt = _es.enter_context(nc.sbuf_tensor("srt", [1, 8], F32))
            io8 = _es.enter_context(nc.sbuf_tensor("io8", [1, 8], F32))
            tt8 = _es.enter_context(nc.sbuf_tensor("tt8", [1, 8], F32))
            tt8b = _es.enter_context(nc.sbuf_tensor("tt8b", [1, 8], F32))
            sc16 = _es.enter_context(nc.sbuf_tensor("sc16", [1, 16], F32))
            alph = _es.enter_context(nc.sbuf_tensor("alph", [1, 1], F32))
            iotaJ = _es.enter_context(nc.sbuf_tensor("iotaJ", [128, 1], F32))
            idxf = _es.enter_context(nc.sbuf_tensor("idxf", [128, 1], F32))
            idxu = _es.enter_context(nc.sbuf_tensor("idxu", [128, 1], U32))
            gbuf = _es.enter_context(nc.sbuf_tensor("gbuf", [128, HM], BF16))
            bgath = _es.enter_context(nc.sbuf_tensor("bgath", [1, HM], F32))
            bgath2 = _es.enter_context(nc.sbuf_tensor("bgath2", [128, 32], F32))
            jsq = _es.enter_context(nc.sbuf_tensor("jsq4", [128, 2, 1024], F32))
            P = Plan(nc, vsept)
            vst = []
            for i in range(NT):
                pp = i % 2
                xb = xbuf[:, i * 1024:(i + 1) * 1024]
                if i >= 2:
                    P.wait("scalar", vV, vst[i - 2])
                P.add("scalar", lambda e, xb=xb, pp=pp: e.activation(jsq[:, pp], xb, ACT.Square))
                sm = P.sig("scalar", sS)
                P.wait("vector", sS, sm)
                P.add("vector", lambda e, xb=xb, i=i: e.tensor_reduce(
                    sum_a[:, i:i + 1], xb, AX.X, ALU.add))
                P.add("vector", lambda e, pp=pp, i=i: e.tensor_reduce(
                    sum_b[:, i:i + 1], jsq[:, pp], AX.X, ALU.add))
                vst.append(P.sig("vector", vV))
            P.add("vector", lambda e: e.tensor_scalar(m1[:], sum_a[:], 1.0 / D, None, ALU.mult))
            P.add("vector", lambda e: e.tensor_scalar(junk[:, 0:NT], sum_b[:], 1.0 / D, None, ALU.mult))
            P.vsep(3)
            P.add("vector", lambda e: e.tensor_tensor(r1[:], m1[:], m1[:], ALU.mult))
            P.vsep(3)
            P.add("vector", lambda e: e.tensor_tensor(junk[:, 0:NT], junk[:, 0:NT], r1[:], ALU.subtract))
            P.vsep(3)
            P.add("vector", lambda e: e.tensor_scalar(junk[:, 0:NT], junk[:, 0:NT], 1e-5, None, ALU.add))
            vv = P.sig("vector", vV)
            P.wait("scalar", vV, vv)
            P.add("scalar", lambda e: e.activation(junk[:, NT:2 * NT], junk[:, 0:NT], ACT.Sqrt))
            sv2 = P.sig("scalar", sS)
            P.wait("vector", sS, sv2)
            P.add("vector", lambda e: e.reciprocal(r1[:], junk[:, NT:2 * NT]))
            rstat = P.sig("vector", vV)
            if "lnst4" in dbg:
                P.wait("sync", vV, rstat)
                P.dma("sync", qS, lnst4_h[:, 0:16], m1[:])
                P.dma("sync", qS, lnst4_h[:, 16:32], r1[:])
                P.dma("sync", qS, lnst4_h[:, 32:48], sum_a[:])
                P.dma("sync", qS, lnst4_h[:, 48:64], sum_b[:])
            P.wait("scalar", vV, rstat)
            copy_marks = []
            mm_marks = []
            store_marks = []
            for i in range(NT):
                pp = i % 2
                xb = xbuf[:, i * 1024:(i + 1) * 1024]
                rb = r1[:, i:i + 1].broadcast_to([128, 1024])
                if i >= 1:
                    P.wait("vector", sS, copy_marks[i - 1])
                P.add("vector", lambda e, xb=xb, i=i, rb=rb: e.scalar_tensor_tensor(
                    xn1f[:], xb, m1[:, i:i + 1], rb, ALU.subtract, ALU.mult))
                xv = P.sig("vector", vV)
                P.wait("scalar", vV, xv)
                if i >= 2:
                    P.wait("scalar", tM, mm_marks[i - 2])
                    P.wait("scalar", qS, store_marks[i - 2])
                P.add("scalar", lambda e, pp=pp: e.copy(xn1b[:, pp], xn1f[:]))
                cs = P.sig("scalar", sS)
                copy_marks.append(cs)
                P.wait("sync", sS, cs)
                store_marks.append(P.dma("sync", qS, xn1_h[i * 128:(i + 1) * 128, :], xn1b[:, pp]))
                P.wait("tensor", sS, cs)
                for hf in range(2):
                    P.add("tensor", lambda e, pp=pp, hf=hf, i=i: e.matmul(
                        psA[0:1, 3072 + hf * 512: 3072 + hf * 512 + 512], onesbf[:],
                        xn1b[:, pp][:, hf * 512:(hf + 1) * 512],
                        start=(i == 0 and hf == 0), stop=(i == NT - 1)))
                mm_marks.append(P.sig("tensor", tM))
            P.wait("vector", tM, mm_marks[-1])
            P.add("vector", lambda e: e.tensor_scalar(
                mrow[:], psA[0:1, 3072:3072 + 1024], 1.0 / T, None, ALU.mult))
            mv2 = P.sig("vector", vV)
            P.wait("sync", vV, mv2)
            P.dma("sync", qS, mean_h, mrow[:])
            mh2 = P.dma("sync", qS, meanTf[:], mean_h.rearrange("o (c p) -> (o p) c", p=128), slow=True)
            P.wait("scalar", qS, mh2)
            P.add("scalar", lambda e: e.copy(meanT[:], meanTf[:]))
            sv3 = P.sig("scalar", sS)
            P.dma("sync", qS, bg8[:], W["L1_ffn_b_gate"])
            bl = P.dma("sync", qS, alph[:], W["L1_ffn_alpha"])
            P.wait("tensor", sS, sv3)
            P.wait("tensor", vV, mv2)
            lgm = []
            for j in range(8):
                if j > 0:
                    P.wait("sync", tM, lgm[-1])
                wv_ = P.dma("sync", qS, wg8[:], wgate_h[j * 128:(j + 1) * 128, :])
                P.wait("tensor", qS, wv_)
                P.add("tensor", lambda e, j=j: e.matmul(
                    psA[0:8, 3072:3073], wg8[:], meanT[:, j:j + 1], start=(j == 0), stop=(j == 7)))
                lgm.append(P.sig("tensor", tM))
            P.wait("vector", tM, lgm[-1])
            P.wait("vector", qS, bl)
            P.add("vector", lambda e: e.tensor_tensor(lg[:], psA[0:8, 3072:3073], bg8[:], ALU.add))
            lgv = P.sig("vector", vV)
            P.wait("sync", vV, lgv)
            P.dma("sync", qS, scal_h[:, 0:8].rearrange("o e -> (o e)").unsqueeze(1), lg[:])
            lrl = P.dma("sync", qS, lrow[:], scal_h[:, 0:8])
            P.add("gpsimd", lambda e: e.iota(io8[:], [[1, 8]], base=0, channel_multiplier=0,
                                             allow_small_or_imprecise_dtypes=True))
            P.add("gpsimd", lambda e: e.iota(iotaJ[:], [[0, 1]], base=0, channel_multiplier=1,
                                             allow_small_or_imprecise_dtypes=True))
            gio = P.sig("gpsimd", gG)
            P.wait("vector", qS, lrl)
            P.wait("vector", gG, gio)
            P.vsep(3)
            P.add("vector", lambda e: e.max(srt[:], lrow[:]))
            P.vsep(3)
            P.add("vector", lambda e: e.tensor_scalar(tt8[:], lrow[:], srt[:, 0:1], None, ALU.is_equal))
            P.vsep(3)
            P.add("vector", lambda e: e.tensor_tensor(tt8b[:], tt8[:], io8[:], ALU.mult))
            P.vsep(3)
            P.add("vector", lambda e: e.tensor_reduce(sc16[:, 3:4], tt8b[:], AX.X, ALU.add))
            P.vsep(3)
            P.add("vector", lambda e: e.tensor_scalar(tt8[:], tt8[:], -1.0, 1.0, ALU.mult, ALU.add))
            P.vsep(3)
            P.add("vector", lambda e: e.tensor_scalar(tt8b[:], lrow[:], srt[:, 1:2], None, ALU.is_equal))
            P.vsep(3)
            P.add("vector", lambda e: e.tensor_tensor(tt8b[:], tt8b[:], tt8[:], ALU.mult))
            P.vsep(3)
            P.add("vector", lambda e: e.tensor_tensor(tt8b[:], tt8b[:], io8[:], ALU.mult))
            P.vsep(3)
            P.add("vector", lambda e: e.tensor_reduce(sc16[:, 4:5], tt8b[:], AX.X, ALU.add))
            P.vsep(3)
            P.add("vector", lambda e: e.tensor_tensor(sc16[:, 5:6], srt[:, 0:1], srt[:, 1:2], ALU.subtract))
            gv1 = P.sig("vector", vV)
            P.wait("scalar", vV, gv1)
            P.add("scalar", lambda e: e.activation(sc16[:, 6:7], sc16[:, 5:6], ACT.Sigmoid))
            sgv = P.sig("scalar", sS)
            P.wait("vector", sS, sgv)
            P.vsep(3)
            P.add("vector", lambda e: e.tensor_scalar(sc16[:, 0:1], alph[:], 0.1, 1.0, ALU.max, ALU.min))
            P.vsep(3)
            P.add("vector", lambda e: e.tensor_scalar(sc16[:, 7:8], sc16[:, 0:1], -1.0, 1.0, ALU.mult, ALU.add))
            P.vsep(3)
            P.add("vector", lambda e: e.tensor_tensor(sc16[:, 1:2], sc16[:, 6:7], sc16[:, 7:8], ALU.mult))
            P.vsep(3)
            P.add("vector", lambda e: e.tensor_tensor(sc16[:, 2:3], sc16[:, 7:8], sc16[:, 1:2], ALU.subtract))
            scv = P.sig("vector", vV)
            P.wait("sync", vV, scv)
            P.dma("sync", qS, scal_h[:, 0:16], sc16[:])
            screpl = P.dma("sync", qS, screp[:], scal_h.broadcast_to([128, 16]))
            P.wait("vector", qS, screpl)
            for k in range(2):
                scol = screp[:, 3 + k:4 + k]
                for (nch, wid, srcap, dsth, rowsz) in [
                        (8, HM, W["L1_ffn_W1e"], w1g_h[k], D),
                        (32, D, W["L1_ffn_W2e"], w2g_h[k], HM)]:
                    for jc in range(nch):
                        P.add("vector", lambda e, scol=scol, rs=rowsz, jc=jc: e.scalar_tensor_tensor(
                            idxf[:], scol, float(rs), iotaJ[:, 0:1], ALU.mult, ALU.add))
                        P.vsep(3)
                        P.add("vector", lambda e, jc=jc: e.tensor_scalar(
                            idxf[:], idxf[:], float(jc * 128), None, ALU.add))
                        P.vsep(3)
                        P.add("vector", lambda e: e.tensor_copy(idxu[:], idxf[:]))
                        iv = P.sig("vector", vV)
                        P.vsep(3)
                        P.wait("gpsimd", vV, iv)
                        P.add("gpsimd", lambda e, srcap=srcap, wid=wid: e.indirect_dma_start(
                            gbuf[:, 0:wid], None, srcap[0:128, :],
                            IndirectOffsetOnAxis(idxu[:], axis=0)).then_inc(qG.h, 16))
                        gv_ = qG.bump(16)
                        P.wait("sync", qG, gv_)
                        sv_ = P.dma("sync", qS, dsth[jc * 128:(jc + 1) * 128, :], gbuf[:, 0:wid])
                        P.wait("gpsimd", qS, sv_)
                        P.wait("vector", qS, sv_)
                for (bsrc, bw, bdst) in [(W["L1_ffn_b1e"], HM // 128, b1g_h[k]),
                                         (W["L1_ffn_b2e"], D // 128, b2g_h[k])]:
                    P.add("vector", lambda e, scol=scol: e.scalar_tensor_tensor(
                        idxf[:], scol, 128.0, iotaJ[:, 0:1], ALU.mult, ALU.add))
                    P.vsep(3)
                    P.add("vector", lambda e: e.tensor_copy(idxu[:], idxf[:]))
                    iv = P.sig("vector", vV)
                    P.vsep(3)
                    P.wait("gpsimd", vV, iv)
                    srcv = bsrc.rearrange("e (p c) -> (e p) c", p=128)
                    P.add("gpsimd", lambda e, srcv=srcv, bw=bw: e.indirect_dma_start(
                        bgath[0:128, 0:bw].__class__(bgath.tensor, bgath[0:128, 0:bw].offset, bgath[0:128, 0:bw].ap)
                        if False else bgath2[:, 0:bw], None, srcv[0:128, :],
                        IndirectOffsetOnAxis(idxu[:], axis=0)).then_inc(qG.h, 16))
                    gv_ = qG.bump(16)
                    P.wait("sync", qG, gv_)
                    sv_ = P.dma("sync", qS,
                                bdst.rearrange("o (p c) -> (o p) c", c=bw), bgath2[:, 0:bw])
                    P.wait("gpsimd", qS, sv_)
                    P.wait("vector", qS, sv_)
            P.emit()

        if upto <= 4:
            return nc

        _es = contextlib.ExitStack()
        with _es:
            xn1T = _es.enter_context(nc.sbuf_tensor("xn1T", [128, 16384], BF16, is_transpose_dest=True))
            wqkv = _es.enter_context(nc.sbuf_tensor("wqkv", [128, 3, 8, 1024], BF16))
            brep = _es.enter_context(nc.sbuf_tensor("brep", [128, 3, 1024], F32))
            Qt = _es.enter_context(nc.sbuf_tensor("Qt", [128, 1024], F32))
            Kt = _es.enter_context(nc.sbuf_tensor("Kt", [128, 1024], F32))
            Vt = _es.enter_context(nc.sbuf_tensor("Vt", [128, 1024], F32))
            SAs = [_es.enter_context(nc.sbuf_tensor(f"SA{i}", [128, 64, 32], F32)) for i in range(2)]
            SBs = [_es.enter_context(nc.sbuf_tensor(f"SB{i}", [128, 32, 32], F32)) for i in range(2)]
            dnA = _es.enter_context(nc.sbuf_tensor("dnA", [128, 64], F32))
            nmA = _es.enter_context(nc.sbuf_tensor("nmA", [128, 64], F32))
            dnB = _es.enter_context(nc.sbuf_tensor("dnB", [128, 32], F32))
            nmB = _es.enter_context(nc.sbuf_tensor("nmB", [128, 32], F32))
            recd = _es.enter_context(nc.sbuf_tensor("recd", [128, 64], F32))
            mm4 = _es.enter_context(nc.sbuf_tensor("mm4", [128, 4], F32))
            negM = _es.enter_context(nc.sbuf_tensor("negM", [128, 1], F32))
            abuf = _es.enter_context(nc.sbuf_tensor("abuf", [128, 1024], F32))
            abbf = _es.enter_context(nc.sbuf_tensor("abbf", [128, 1024], BF16))
            mneg = _es.enter_context(nc.sbuf_tensor("mneg", [128, 32, 32], F32))
            P = Plan(nc, vsept)
            P.add("gpsimd", lambda e: e.memset(mneg[:], 0.0))
            P.add("gpsimd", lambda e: e.affine_select(
                mneg[:], mneg[:], [[1, 32], [-1, 32]], ALU.is_ge, -1e30,
                channel_multiplier=0))
            mkd = P.sig("gpsimd", gG)
            for j in range(8):
                P.dma("sync", qS, xn1T[:, j * 2048:(j + 1) * 2048],
                      xn1_h[:, j * 128:(j + 1) * 128], transpose=True)
            for wi, wh in enumerate([wq1_h, wk1_h, wv1_h]):
                for j in range(8):
                    P.dma("sync", qS, wqkv[:, wi, j], wh[j * 128:(j + 1) * 128, :])
            for bi_, nm_ in enumerate(["L1_attn_bq", "L1_attn_bk", "L1_attn_bv"]):
                P.dma("sync", qS, brep[:, bi_], W[nm_].broadcast_to([128, 1024]))
            ldall = qS.v
            P.wait("vector", qS, ldall)
            P.add("vector", lambda e: e.tensor_scalar(brep[:, 0], brep[:, 0], 0.125, None, ALU.mult))
            bqs = P.sig("vector", vV)
            P.wait("tensor", qS, ldall)
            scop = []
            adma = []
            for tc in range(NT):
                if tc > 0:
                    P.wait("tensor", vV, vV.v)
                for j in range(8):
                    st = xn1T[:, j * 2048 + tc * 128: j * 2048 + tc * 128 + 128]
                    for wi in range(3):
                        for hf in range(2):
                            P.add("tensor", lambda e, st=st, wi=wi, hf=hf, j=j: e.matmul(
                                bank(wi * 2 + hf), st, wqkv[:, wi, j][:, hf * 512:(hf + 1) * 512],
                                start=(j == 0), stop=(j == 7)))
                mv = P.sig("tensor", tM)
                P.wait("vector", tM, mv)
                if tc == 0:
                    P.wait("vector", vV, bqs)
                if tc >= 2:
                    P.wait("vector", sS, scop[tc - 2])
                for hf in range(2):
                    P.add("vector", lambda e, hf=hf: e.scalar_tensor_tensor(
                        Qt[:, hf * 512:(hf + 1) * 512], bank(hf), 0.125,
                        brep[:, 0][:, hf * 512:(hf + 1) * 512], ALU.mult, ALU.add))
                    P.add("vector", lambda e, hf=hf: e.tensor_tensor(
                        Kt[:, hf * 512:(hf + 1) * 512], bank(2 + hf),
                        brep[:, 1][:, hf * 512:(hf + 1) * 512], ALU.add))
                    P.add("vector", lambda e, hf=hf: e.tensor_tensor(
                        Vt[:, hf * 512:(hf + 1) * 512], bank(4 + hf),
                        brep[:, 2][:, hf * 512:(hf + 1) * 512], ALU.add))
                P.add("vector", lambda e: e.tensor_reduce(mm4[:, 0:1], Qt[:], AX.X, ALU.max,
                                                          apply_absolute_value=True))
                P.add("vector", lambda e: e.tensor_reduce(mm4[:, 1:2], Kt[:], AX.X, ALU.max,
                                                          apply_absolute_value=True))
                P.vsep(3)
                P.add("vector", lambda e: e.tensor_tensor(negM[:], mm4[:, 0:1], mm4[:, 1:2], ALU.mult))
                P.vsep(3)
                P.add("vector", lambda e: e.tensor_scalar(negM[:], negM[:], -1.0, None, ALU.mult))
                qkv = P.sig("vector", vV)
                if "dQ" in dbg:
                    P.wait("sync", vV, qkv)
                    P.dma("sync", qS, dbg_Q[tc * 128:(tc + 1) * 128, :], Qt[:])
                    P.dma("sync", qS, dbg_K[tc * 128:(tc + 1) * 128, :], Kt[:])
                    P.dma("sync", qS, dbg_V[tc * 128:(tc + 1) * 128, :], Vt[:])
                P.wait("scalar", vV, qkv)
                P.wait("gpsimd", vV, qkv)
                svm = [None] * H
                gvm = [None] * H
                evm = [None] * H

                def emit_scores(h):
                    SA, SB = SAs[h % 2], SBs[h % 2]
                    q0 = Qt[:, h * 64: h * 64 + 64]
                    k0 = Kt[:, h * 64: h * 64 + 32]
                    k1 = Kt[:, h * 64 + 32: h * 64 + 64]
                    P.add("vector", lambda e, q0=q0, k0=k0, SA=SA: e.tensor_tensor(
                        SA[:], q0.unsqueeze(2).broadcast_to([128, 64, 32]),
                        k0.unsqueeze(1).broadcast_to([128, 64, 32]), ALU.mult))
                    P.add("vector", lambda e, q0=q0, k1=k1, SB=SB: e.tensor_tensor(
                        SB[:], q0[:, 32:64].unsqueeze(2).broadcast_to([128, 32, 32]),
                        k1.unsqueeze(1).broadcast_to([128, 32, 32]), ALU.mult))
                    svm[h] = P.sig("vector", vV)

                def emit_mask(h):
                    SA, SB = SAs[h % 2], SBs[h % 2]
                    P.wait("gpsimd", vV, svm[h])
                    P.add("gpsimd", lambda e, SA=SA: e.tensor_tensor(
                        SA[:, 0:32, :], SA[:, 0:32, :], mneg[:], ALU.add))
                    P.add("gpsimd", lambda e, SB=SB: e.tensor_tensor(
                        SB[:], SB[:], mneg[:], ALU.add))
                    gvm[h] = P.sig("gpsimd", gG)

                def emit_exp(h):
                    SA, SB = SAs[h % 2], SBs[h % 2]
                    P.wait("scalar", gG, gvm[h])
                    P.add("scalar", lambda e, SA=SA: e.activation(
                        SA[:].rearrange("p a b -> p (a b)"),
                        SA[:].rearrange("p a b -> p (a b)"), ACT.Exp, bias=negM[:]))
                    P.add("scalar", lambda e, SB=SB: e.activation(
                        SB[:].rearrange("p a b -> p (a b)"),
                        SB[:].rearrange("p a b -> p (a b)"), ACT.Exp, bias=negM[:]))
                    evm[h] = P.sig("scalar", sS)

                def emit_tail(h):
                    SA, SB = SAs[h % 2], SBs[h % 2]
                    v0 = Vt[:, h * 64: h * 64 + 32]
                    v1_ = Vt[:, h * 64 + 32: h * 64 + 64]
                    if "dSA" in dbg and tc == 0 and h == 0:
                        P.wait("sync", sS, evm[h])
                        P.dma("sync", qS, dbg_SA, SA[:])
                        P.dma("sync", qS, dbg_SB, SB[:])
                        dwait = qS.v
                        P.wait("vector", qS, dwait)
                    P.wait("vector", sS, evm[h])
                    P.add("vector", lambda e, SA=SA: e.tensor_reduce(dnA[:], SA[:], AX.X, ALU.add))
                    P.add("vector", lambda e, SB=SB: e.tensor_reduce(dnB[:], SB[:], AX.X, ALU.add))
                    P.vsep(2)
                    P.add("vector", lambda e: e.tensor_tensor(
                        dnA[:, 32:64], dnA[:, 32:64], dnB[:], ALU.add))
                    P.add("vector", lambda e, v0=v0, SA=SA: e.tensor_tensor(
                        SA[:], SA[:], v0.unsqueeze(1).broadcast_to([128, 64, 32]), ALU.mult))
                    P.add("vector", lambda e, v1_=v1_, SB=SB: e.tensor_tensor(
                        SB[:], SB[:], v1_.unsqueeze(1).broadcast_to([128, 32, 32]), ALU.mult))
                    P.add("vector", lambda e: e.reciprocal(recd[:], dnA[:]))
                    P.add("vector", lambda e, SA=SA: e.tensor_reduce(nmA[:], SA[:], AX.X, ALU.add))
                    P.add("vector", lambda e, SB=SB: e.tensor_reduce(nmB[:], SB[:], AX.X, ALU.add))
                    P.vsep(2)
                    P.add("vector", lambda e: e.tensor_tensor(
                        nmA[:, 32:64], nmA[:, 32:64], nmB[:], ALU.add))
                    P.vsep(2)
                    P.add("vector", lambda e, h=h: e.tensor_tensor(
                        abuf[:, h * 64:(h + 1) * 64], nmA[:], recd[:], ALU.mult))

                for h in range(H):
                    emit_scores(h)
                    emit_mask(h)
                    emit_exp(h)
                    if h >= 1:
                        emit_tail(h - 1)
                emit_tail(H - 1)
                tiledone = P.sig("vector", vV)
                P.wait("scalar", vV, tiledone)
                if tc > 0:
                    P.wait("scalar", qS, adma[-1])
                P.add("scalar", lambda e: e.copy(abbf[:], abuf[:]))
                av = P.sig("scalar", sS)
                scop.append(av)
                P.wait("sync", sS, av)
                adma.append(P.dma("sync", qS, attn1_h[tc * 128:(tc + 1) * 128, :], abbf[:]))
            P.emit()

        _es = contextlib.ExitStack()
        with _es:
            a1T = _es.enter_context(nc.sbuf_tensor("a1T", [128, 16384], BF16, is_transpose_dest=True))
            woS = _es.enter_context(nc.sbuf_tensor("woS", [128, 8, 1024], BF16))
            borep = _es.enter_context(nc.sbuf_tensor("borep", [128, 1024], F32))
            P = Plan(nc, vsept)
            for j in range(8):
                P.dma("sync", qS, a1T[:, j * 2048:(j + 1) * 2048],
                      attn1_h[:, j * 128:(j + 1) * 128], transpose=True)
                P.dma("sync", qS, woS[:, j], wo1_h[j * 128:(j + 1) * 128, :])
            lda = P.dma("sync", qS, borep[:], W["L1_attn_bo"].broadcast_to([128, 1024]))
            P.wait("tensor", qS, lda)
            P.wait("vector", qS, lda)
            om = []
            for tc in range(NT):
                pb = (tc % 2) * 2
                if tc >= 2:
                    P.wait("tensor", vV, om[tc - 2])
                for j in range(8):
                    st = a1T[:, j * 2048 + tc * 128: j * 2048 + tc * 128 + 128]
                    for hf in range(2):
                        P.add("tensor", lambda e, st=st, j=j, hf=hf, pb=pb: e.matmul(
                            bank(pb + hf), st, woS[:, j][:, hf * 512:(hf + 1) * 512],
                            start=(j == 0), stop=(j == 7)))
                mv = P.sig("tensor", tM)
                P.wait("vector", tM, mv)
                xb = xbuf[:, tc * 1024:(tc + 1) * 1024]
                P.add("vector", lambda e, xb=xb: e.tensor_tensor(xb, xb, borep[:], ALU.add))
                for hf in range(2):
                    P.add("vector", lambda e, xb=xb, hf=hf, pb=pb: e.tensor_tensor(
                        xb[:, hf * 512:(hf + 1) * 512], xb[:, hf * 512:(hf + 1) * 512],
                        bank(pb + hf), ALU.add))
                om.append(P.sig("vector", vV))
            P.emit()

        if upto <= 5:
            return nc

        _es = contextlib.ExitStack()
        with _es:
            xn1T2 = _es.enter_context(nc.sbuf_tensor("xn1T2", [128, 16384], BF16, is_transpose_dest=True))
            w1h = _es.enter_context(nc.sbuf_tensor("w1h", [128, 8, 2048], BF16))
            w2h = _es.enter_context(nc.sbuf_tensor("w2h", [128, 8, 1024], BF16))
            hfm = _es.enter_context(nc.sbuf_tensor("hfm", [128, 32, 512], BF16))
            expacc = _es.enter_context(nc.sbuf_tensor("expacc", [128, 8, 512], F32))
            b1col = _es.enter_context(nc.sbuf_tensor("b1col", [128, 3, 32], F32))
            bcmb = _es.enter_context(nc.sbuf_tensor("bcmb", [128, 8], F32))
            tmp8 = _es.enter_context(nc.sbuf_tensor("tmp8", [128, 8], F32))
            tmp8b = _es.enter_context(nc.sbuf_tensor("tmp8b", [128, 8], F32))
            ffv = _es.enter_context(nc.sbuf_tensor("ffv", [128, 512], F32))
            ffb = _es.enter_context(nc.sbuf_tensor("ffb", [128, 2, 512], BF16))
            ftok = _es.enter_context(nc.sbuf_tensor("ftok", [128, 2, 1024], BF16, is_transpose_dest=True))
            P = Plan(nc, vsept)
            for j in range(8):
                P.dma("sync", qS, xn1T2[:, j * 2048:(j + 1) * 2048],
                      xn1_h[:, j * 128:(j + 1) * 128], transpose=True)
            P.dma("sync", qS, b1col[:, 0], W["L1_ffn_b1m"].rearrange("o (c p) -> (o p) c", p=128), slow=True)
            for k in range(2):
                P.dma("sync", qS, b1col[:, 1 + k], b1g_h[k].rearrange("o (c p) -> (o p) c", p=128), slow=True)
            P.dma("sync", qS, bcmb[:], W["L1_ffn_b2m"].rearrange("o (c p) -> (o p) c", p=128), slow=True)
            P.dma("sync", qS, tmp8[:], b2g_h[0].rearrange("o (c p) -> (o p) c", p=128), slow=True)
            bld = P.dma("sync", qS, tmp8b[:], b2g_h[1].rearrange("o (c p) -> (o p) c", p=128), slow=True)
            P.wait("vector", qS, bld)
            P.add("vector", lambda e: e.tensor_scalar(bcmb[:], bcmb[:], screp[:, 0:1], None, ALU.mult))
            P.vsep(3)
            P.add("vector", lambda e: e.tensor_scalar(tmp8[:], tmp8[:], screp[:, 1:2], None, ALU.mult))
            P.vsep(3)
            P.add("vector", lambda e: e.tensor_tensor(bcmb[:], bcmb[:], tmp8[:], ALU.add))
            P.vsep(3)
            P.add("vector", lambda e: e.tensor_scalar(tmp8b[:], tmp8b[:], screp[:, 2:3], None, ALU.mult))
            P.vsep(3)
            P.add("vector", lambda e: e.tensor_tensor(bcmb[:], bcmb[:], tmp8b[:], ALU.add))
            bcv = P.sig("vector", vV)
            if "dps" in dbg:
                P.wait("sync", vV, bcv)
                P.dma("sync", qS, dbg_ps[:, 24:32], bcmb[:])
                P.dma("sync", qS, dbg_ps[:, 32:40], tmp8[:])
                P.dma("sync", qS, dbg_ps[:, 40:48], tmp8b[:])

            parts = [(w1g_h[0], w2g_h[0], 1, "e1"),
                     (w1g_h[1], w2g_h[1], 2, "e2"),
                     (w1m_h, w2m_h, 0, "main")]
            fmarks = []
            for tc in range(4):
                for (w1src, w2src, bix, kind) in parts:
                    if fmarks:
                        P.wait("tensor", vV, fmarks[-1])
                    gelu_marks = []
                    for nh in range(2):
                        P.wait("sync", tM, tM.v)
                        for jj in range(8):
                            P.dma("sync", qS, w1h[:, jj],
                                  w1src[jj * 128:(jj + 1) * 128, nh * 2048:(nh + 1) * 2048])
                        wl = qS.v
                        P.wait("tensor", qS, wl)
                        for n_ in range(16):
                            n = nh * 16 + n_
                            pb = n_ % 2
                            if n_ >= 2:
                                P.wait("tensor", sS, gelu_marks[-2])
                            for j in range(8):
                                st = w1h[:, j][:, n_ * 128: n_ * 128 + 128]
                                rhs = xn1T2[:, j * 2048 + tc * 512: j * 2048 + tc * 512 + 512]
                                P.add("tensor", lambda e, st=st, rhs=rhs, pb=pb, j=j: e.matmul(
                                    bank(pb), st, rhs, start=(j == 0), stop=(j == 7)))
                            mv = P.sig("tensor", tM)
                            P.wait("scalar", tM, mv)
                            P.add("scalar", lambda e, n=n, pb=pb, bix=bix: e.activation(
                                hfm[:, n], bank(pb), ACT.Gelu_apprx_tanh,
                                bias=b1col[:, bix, n:n + 1]))
                            gelu_marks.append(P.sig("scalar", sS))
                    P.wait("tensor", sS, gelu_marks[-1])
                    if "dhfm" in dbg and tc == 0 and kind == "main":
                        P.wait("sync", sS, gelu_marks[-1])
                        P.dma("sync", qS, dbg_hfm, hfm[:].rearrange("p a b -> p (a b)"))
                    for nq in range(4):
                        P.wait("sync", tM, tM.v)
                        for n_ in range(8):
                            nn = nq * 8 + n_
                            P.dma("sync", qS, w2h[:, n_], w2src[nn * 128:(nn + 1) * 128, :])
                        w2l = qS.v
                        P.wait("tensor", qS, w2l)
                        for n_ in range(8):
                            nn = nq * 8 + n_
                            for i_ in range(8):
                                st = w2h[:, n_][:, i_ * 128: i_ * 128 + 128]
                                P.add("tensor", lambda e, st=st, n_=n_, i_=i_, nn=nn: e.matmul(
                                    bank(i_), st, hfm[:, nn], start=(nn == 0), stop=(nn == 31)))
                        if nq < 3:
                            P.sig("tensor", tM)
                    dmv = P.sig("tensor", tM)
                    if "dps" in dbg and tc == 0 and kind == "e1":
                        P.wait("scalar", tM, dmv)
                        for i_ in range(8):
                            P.add("scalar", lambda e, i_=i_: e.copy(junk[:, 0:512], bank(i_)))
                            dpv = P.sig("scalar", sS)
                            P.wait("sync", sS, dpv)
                            dw = P.dma("sync", qS, dbg_ps[:, i_ * 512:(i_ + 1) * 512], junk[:, 0:512])
                            P.wait("scalar", qS, dw)
                        P.wait("tensor", sS, sS.v)
                        P.wait("vector", sS, sS.v)
                    P.wait("vector", tM, dmv)
                    if kind == "e1":
                        P.wait("vector", vV, bcv)
                        for i_ in range(8):
                            P.add("vector", lambda e, i_=i_: e.tensor_scalar(
                                expacc[:, i_], bank(i_), screp[:, 1:2], bcmb[:, i_:i_ + 1],
                                ALU.mult, ALU.add))
                        fmarks.append(P.sig("vector", vV))
                        if "dexp" in dbg and tc == 0:
                            P.wait("sync", vV, fmarks[-1])
                            P.dma("sync", qS, dbg_exp, expacc[:].rearrange("p a b -> p (a b)"))
                            P.dma("sync", qS, dbg_ps[:, 0:16], screp[:])
                            P.dma("sync", qS, dbg_ps[:, 16:24], bcmb[:])
                            dxw = qS.v
                            P.wait("vector", qS, dxw)
                    elif kind == "e2":
                        for i_ in range(8):
                            P.add("vector", lambda e, i_=i_: e.scalar_tensor_tensor(
                                expacc[:, i_], bank(i_), screp[:, 2:3], expacc[:, i_],
                                ALU.mult, ALU.add))
                        fmarks.append(P.sig("vector", vV))
                        if "dexp" in dbg and tc == 0:
                            P.wait("sync", vV, fmarks[-1])
                            P.dma("sync", qS, dbg_h[0:128, 0:2048],
                                  expacc[:, 0:4].rearrange("p a b -> p (a b)"))
                            dx2 = P.dma("sync", qS, dbg_h[128:256, 0:2048],
                                        expacc[:, 4:8].rearrange("p a b -> p (a b)"))
                            P.wait("vector", qS, dx2)
                    else:
                        fv = None
                        cpm = []
                        dmam = []
                        for i_ in range(8):
                            if i_ >= 1:
                                P.wait("vector", sS, cpm[i_ - 1])
                            P.add("vector", lambda e, i_=i_: e.scalar_tensor_tensor(
                                ffv[:], bank(i_), screp[:, 0:1], expacc[:, i_],
                                ALU.mult, ALU.add))
                            fv = P.sig("vector", vV)
                            P.wait("scalar", vV, fv)
                            if i_ >= 2:
                                P.wait("scalar", qS, dmam[i_ - 2])
                            P.add("scalar", lambda e, i_=i_: e.copy(ffb[:, i_ % 2], ffv[:]))
                            sv_ = P.sig("scalar", sS)
                            cpm.append(sv_)
                            P.wait("sync", sS, sv_)
                            dmam.append(P.dma("sync", qS,
                                  ffn1_h[i_ * 128:(i_ + 1) * 128, tc * 512:(tc + 1) * 512],
                                  ffb[:, i_ % 2]))
                        fmarks.append(fv)
            fsync = qS.v
            P.wait("sync", vV, fmarks[-1])
            addm = []
            for i in range(NT):
                pp = i % 2
                if i >= 2:
                    P.wait("sync", vV, addm[i - 2])
                fl = P.dma("sync", qS, ftok[:, pp], ffn1_h[:, i * 128:(i + 1) * 128], transpose=True)
                P.wait("vector", qS, fl)
                xb = xbuf[:, i * 1024:(i + 1) * 1024]
                P.add("vector", lambda e, xb=xb, pp=pp: e.tensor_tensor(xb, xb, ftok[:, pp], ALU.add))
                addm.append(P.sig("vector", vV))
            P.emit()

        P = Plan(nc, vsept)
        fin = P.dma("sync", qS, y.rearrange("(tc p) d -> p tc d", p=128),
                    xbuf[:].rearrange("p (tc d) -> p tc d", d=1024))
        P.wait("sync", qS, fin)
        P.emit()
        return nc



def kernel(**inputs):
    nc = build()
    lower_extended_insts(nc)
    x = np.ascontiguousarray(inputs["L0_attn_hidden_states"])
    base = {}
    for k, v in inputs.items():
        if k == "L0_attn_hidden_states":
            continue
        a = np.ascontiguousarray(v)
        if k == "L1_ffn_alpha":
            a = a.reshape(1, 1)
        elif k == "L1_ffn_W1e":
            a = a.reshape(NE * D, HM)
        elif k == "L1_ffn_W2e":
            a = a.reshape(NE * HM, D)
        elif k == "L1_ffn_b_gate":
            a = a.reshape(NE, 1)
        elif a.ndim == 1:
            a = a.reshape(1, -1)
        base[k] = a
    base["ident"] = np.eye(128, dtype=ml_dtypes.bfloat16)
    in_maps = [dict(base, x=np.ascontiguousarray(x[c])) for c in range(8)]
    res = run_bass_kernel_spmd(nc, in_maps, core_ids=list(range(8)))
    return np.stack([np.asarray(r["y"]) for r in res.results], axis=0)
